# version 01
# baseline (speedup 1.0000x reference)
import numpy as np
import ml_dtypes
from contextlib import ExitStack

import concourse.bass as bass
import concourse.mybir as mybir
from concourse.bass_utils import run_bass_kernel_spmd
from concourse.bass import IndirectOffsetOnAxis
from concourse.aot_env import allow_ici_all_to_all

F32 = mybir.dt.float32
BF16 = mybir.dt.bfloat16
I32 = mybir.dt.int32
U16 = mybir.dt.uint16
AF = mybir.ActivationFunctionType
ALU = mybir.AluOpType
AXX = mybir.AxisListType.X
BF = ml_dtypes.bfloat16

BUILD_PHASES = 99
DBG_DUMPS = []

NCORES = 8
D = 2048
S = 8192
TOK = S // NCORES
NCH = TOK // 128
NDC = D // 128
NH = 16
HPC = NH // NCORES
FF = 8192
NE = 8
EFF = 512
C2 = 256
NG = 2 * NE
SCALE = 128.0 ** -0.5
SHIFT = 8.0
EPS = 1e-5
NQT = S // 512
KV_ELEMS = 2 * 128 * TOK
QB_ELEMS = 256 * TOK
CT = TOK + 2


class Sem:

    def __init__(self, ctx, nc, name):
        self.h = ctx.enter_context(nc.semaphore(name))
        self.v = 0


class Emitter:

    ENGINES = ("sync", "gpsimd", "tensor", "vector", "scalar")

    def __init__(self):
        self.ops = {e: [] for e in self.ENGINES}

    def add(self, eng, fn, waits=(), inc=None, incn=1):
        w = [(s.h, v) for (s, v) in waits]
        if inc is not None:
            inc.v += incn
            self.ops[eng].append((fn, w, (inc.h, incn)))
        else:
            self.ops[eng].append((fn, w, None))

    def addc(self, eng, fn, sem, waits=()):
        self.add(eng, fn, waits=list(waits) + [(sem, sem.v)], inc=sem)

    def dma(self, eng, out, in_, sem, waits=()):
        self.add(eng, lambda E, out=out, in_=in_: E.dma_start(out=out, in_=in_),
                 waits=waits, inc=sem, incn=16)

    def flush(self, block):
        for eng in self.ENGINES:
            lst = self.ops[eng]
            if not lst:
                continue

            def make(lst):
                def body(E):
                    for fn, waits, inc in lst:
                        for (h, v) in waits:
                            E.wait_ge(h, v)
                        r = fn(E)
                        if inc is not None:
                            if r is None:
                                r = E.engine_nop()
                            r.then_inc(inc[0], inc[1])
                return body
            getattr(block, eng)(make(lst))
        self.ops = {e: [] for e in self.ENGINES}


def build_program():
    nc = bass.Bass("TRN2", target_bir_lowering=False)
    dr = {}

    def DT(name, shape, dt, kind=None):
        dr[name] = (nc.dram_tensor(name, shape, dt) if kind is None
                    else nc.dram_tensor(name, shape, dt, kind=kind))
        return dr[name]

    DT("x", [TOK, D], F32, "ExternalInput")
    DT("wq", [D, D], F32, "ExternalInput")
    DT("wkv", [D, 256], F32, "ExternalInput")
    DT("wo", [D, D], F32, "ExternalInput")
    DT("wfc", [D, FF], F32, "ExternalInput")
    DT("wproj", [FF, D], F32, "ExternalInput")
    DT("routerT", [D, NE], F32, "ExternalInput")
    DT("w1", [NE, D, EFF], F32, "ExternalInput")
    DT("w2", [NE, EFF, D], F32, "ExternalInput")
    DT("w3", [NE, D, EFF], F32, "ExternalInput")
    DT("vecs", [192, 128], F32, "ExternalInput")
    DT("onesr", [1, 128], F32, "ExternalInput")
    DT("identf", [128, 128], F32, "ExternalInput")
    DT("identb", [128, 128], BF16, "ExternalInput")
    DT("onesb", [128, 128], BF16, "ExternalInput")
    DT("Ustrict", [128, 128], F32, "ExternalInput")
    DT("masks", [4, 128, 512], BF16, "ExternalInput")
    DT("iotaC", [128, C2], F32, "ExternalInput")
    DT("pcvals", [128, 2 * NCH], BF16, "ExternalInput")
    DT("out", [TOK, D], F32, "ExternalOutput")
    DT("kv_in", [KV_ELEMS], BF16)
    DT("kv_out", [NCORES * KV_ELEMS], BF16)
    DT("q_in", [NCORES * QB_ELEMS], BF16)
    DT("q_out", [NCORES * QB_ELEMS], BF16)
    DT("at_in", [NCORES * QB_ELEMS], BF16)
    DT("at_out", [NCORES * QB_ELEMS], BF16)
    DT("xn_rows", [TOK, D], BF16)
    DT("moeA", [TOK + 1, D], F32)
    DT("moeB", [TOK + 1, D], F32)

    rg = [list(range(NCORES))]

    with ExitStack() as ctx:
        def SB(name, shape, dt, stack=None):
            t = (stack or ctx).enter_context(nc.sbuf_tensor("sb_" + name, shape, dt))
            TILES[name] = t
            return t
        TILES = {}

        identf = SB("identf", [128, 128], F32)
        identb = SB("identb", [128, 128], BF16)
        onesb = SB("onesb", [128, 128], BF16)
        Us = SB("Ustrict", [128, 128], F32)
        masks = SB("masks", [128, 4 * 512], BF16)
        iotaC = SB("iotaC", [128, C2], F32)
        pcv = SB("pcvals", [128, 2 * NCH], BF16)
        vrows = SB("vrows", [128, 128], F32)
        vrows2 = SB("vrows2", [64, 128], F32)
        onesr = SB("onesr", [1, 128], F32)
        vcols = SB("vcols", [128, 192], F32)
        VG1, VB1, VG2, VB2, VBQ, VBO, VBPJ, VBKV, VBFC = 0, 16, 32, 48, 64, 80, 96, 112, 128
        statc = SB("statc", [128, 64], F32)
        routc = SB("routc", [128, 48], F32)
        mx8 = SB("mx8", [128, 8], F32)
        idx8 = SB("idx8", [128, 8], U16)
        lists = SB("lists", [128, 6 * NG], F32)
        idxgI = SB("idxgI", [128, 2 * NG], I32)
        idxsI = SB("idxsI", [128, 2 * NG], I32)
        gatec = SB("gatec", [128, 2 * NG], F32)
        tpv = SB("tpv", [128, 2 * 3 * NCH], BF16)
        reps = SB("reps", [128, 4 * TOK], F32)
        statrow = SB("statrow", [1, TOK], F32)
        srow_t = SB("srow_t", [32, 128], F32)
        ucol = SB("ucol", [128, 4], F32)
        small = SB("small", [128, 64], F32)
        rec = SB("rec", [128, 512], F32)
        dbg1 = SB("dbg1", [128, 64], F32)
        dbg2 = SB("dbg2", [128, 64], F32)


        sW = Sem(ctx, nc, "sW")
        sX = Sem(ctx, nc, "sX")
        sT = Sem(ctx, nc, "sT")
        sV = Sem(ctx, nc, "sV")
        sC = Sem(ctx, nc, "sC")
        sG = Sem(ctx, nc, "sG")
        sO = Sem(ctx, nc, "sO")
        sCC = Sem(ctx, nc, "sCC")

        block = ctx.enter_context(nc.Block())

        pM0 = ctx.enter_context(nc.psum_tensor("pM0", [128, 512], F32))
        pM1 = ctx.enter_context(nc.psum_tensor("pM1", [128, 512], F32))
        pM2 = ctx.enter_context(nc.psum_tensor("pM2", [128, 512], F32))
        pM3 = ctx.enter_context(nc.psum_tensor("pM3", [128, 512], F32))
        pPh = ctx.enter_context(nc.psum_tensor("pPh", [128, 512], F32))
        pX0 = ctx.enter_context(nc.psum_tensor("pX0", [128, 512], F32))
        pX1 = ctx.enter_context(nc.psum_tensor("pX1", [128, 512], F32))
        pPh2 = ctx.enter_context(nc.psum_tensor("pPh2", [128, 512], F32))

        def pXb(k):
            return (pX0 if (k % 2 == 0) else pX1)[:, 0:64].bitcast(BF16)

        E = Emitter()

        def trunc_finalize():
            for i, name in enumerate(DBG_DUMPS):
                if name.startswith("dram:"):
                    _, dn, nrow = name.split(":")
                    nrow = int(nrow)
                    v = dr[dn].ap()
                    flat = 1
                    for s in v.shape:
                        flat *= s
                    v = v.rearrange(" ".join(f"a{i}" for i in range(len(v.shape))) + " -> (" +
                                    " ".join(f"a{i}" for i in range(len(v.shape))) + ")") if len(v.shape) > 1 else v
                    v = v.rearrange("(r c) -> r c", c=D)
                    E.dma("gpsimd", dr["out"].ap()[i * 128:i * 128 + nrow, :],
                          v[0:nrow, :], sO,
                          waits=[(sT, sT.v), (sV, sV.v), (sC, sC.v), (sX, sX.v),
                                 (sW, sW.v), (sG, sG.v), (sCC, sCC.v), (sO, sO.v)])
                    continue
                coff = 0
                if "@" in name:
                    name, coff = name.split("@")
                    coff = int(coff)
                t = TILES[name]
                sh = t[:].shape
                cols = min((sh[-1] if len(sh) > 1 else 1) - coff, D)
                rows = min(sh[0], 128)
                E.dma("gpsimd", dr["out"].ap()[i * 128:i * 128 + rows, 0:cols],
                      t[0:rows, coff:coff + cols] if len(sh) == 2 else t[0:rows], sO,
                      waits=[(sT, sT.v), (sV, sV.v), (sC, sC.v), (sX, sX.v),
                             (sW, sW.v), (sG, sG.v), (sCC, sCC.v)])
            E.add("gpsimd", lambda g: g.wait_ge(sO.h, sO.v))
            E.flush(block)
            for st in open_stacks:
                st.close()

        open_stacks = []

        def rep_sl(slot, h):
            return reps[:, slot * TOK + h * 512: slot * TOK + (h + 1) * 512]

        def tr_evac_pipe(items, dtype="bf", first_waits=()):
            tb, vb = sT.v, sV.v
            for k, (src, _) in enumerate(items):
                if dtype == "bf":
                    pb = pXb(k)[:, 0:128]
                    idn = identb
                else:
                    pb = (pX0 if (k % 2 == 0) else pX1)[:, 0:128]
                    idn = identf
                w = list(first_waits) if k == 0 else []
                if vb + k - 1 > 0:
                    w.append((sV, vb + k - 1))
                E.add("tensor", lambda t, src=src, pb=pb, idn=idn: t.transpose(pb, src, idn[:]),
                      waits=w, inc=sT)
            for k, (_, ev) in enumerate(items):
                if dtype == "bf":
                    pb = pXb(k)[:, 0:128]
                else:
                    pb = (pX0 if (k % 2 == 0) else pX1)[:, 0:128]
                E.add("vector", lambda v, ev=ev, pb=pb: ev(v, pb),
                      waits=[(sT, tb + k + 1)], inc=sV)

        for nm, t in [("identf", identf), ("identb", identb), ("onesb", onesb),
                      ("Ustrict", Us), ("iotaC", iotaC), ("pcvals", pcv)]:
            E.dma("sync", t[:], dr[nm].ap(), sW)
        E.dma("sync", masks[:].rearrange("p (a b) -> p a b", a=4),
              dr["masks"].ap().rearrange("a p b -> p a b"), sW)
        E.dma("sync", vrows[:], dr["vecs"].ap()[0:128, :], sW)
        E.dma("sync", vrows2[:], dr["vecs"].ap()[128:192, :], sW)
        E.dma("sync", onesr[:], dr["onesr"].ap(), sW)
        cW0 = (sW, sW.v)
        E.add("tensor", lambda t: t.transpose(pX0[:, 0:128], vrows[0:128, :], identf[:]),
              waits=[cW0], inc=sT)
        E.add("tensor", lambda t: t.transpose(pX1[:, 0:64], vrows2[:, :], identf[0:64, 0:64]),
              inc=sT)
        cT0 = (sT, sT.v)
        E.add("vector", lambda v: v.tensor_copy(vcols[:, 0:128], pX0[:, 0:128]), waits=[cT0])
        E.add("vector", lambda v: v.tensor_copy(vcols[:, 128:192], pX1[:, 0:64]), inc=sV)

        if BUILD_PHASES < 1:
            trunc_finalize()
            return nc
        esA = ExitStack()
        open_stacks.append(esA)
        esB = ExitStack()
        open_stacks.append(esB)
        xgT = SB("xgT", [128, NDC * CT], BF16, esB)
        sqdump = SB("sqdump", [128, D], BF16, esA)
        xn_tok = SB("xn_tok", [128, NCH * D], BF16, esA)
        xhat = SB("xhat", [128, NCH * D], BF16, esA)
        xnT_t = SB("xnT_t", [128, TOK], BF16, esA)
        rtw = SB("rtw", [128, NDC * NE], BF16, esA)
        rsc = SB("rsc", [NE, TOK], F32, esA)
        selt = SB("selt", [128, C2], BF16, esA)
        msk8 = SB("msk8", [128, 40], F32, esA)

        E.dma("gpsimd", xhat[:].rearrange("p (c d) -> p c d", c=NCH),
              dr["x"].ap().rearrange("(c p) d -> p c d", p=128), sX)
        E.dma("gpsimd", rtw[:].rearrange("p (c e) -> p c e", c=NDC),
              dr["routerT"].ap().rearrange("(c p) e -> p c e", p=128), sX)
        cX = (sX, sX.v)
        for c in range(NCH):
            E.add("vector", lambda v, c=c: v.reduce_sum(
                statc[:, 48 + c:49 + c], xhat[:, c * D:(c + 1) * D], axis=AXX),
                waits=([cX] if c == 0 else []), inc=(sV if c == NCH - 1 else None))
        for c in range(NCH):
            E.add("scalar", lambda s, c=c: s.activation(
                sqdump[:], xhat[:, c * D:(c + 1) * D], AF.Square,
                accum_out=statc[:, 56 + c:57 + c]),
                waits=([cX] if c == 0 else []), inc=(sC if c == NCH - 1 else None))

        E.addc("vector", lambda v: v.tensor_scalar_mul(statc[:, 0:8], statc[:, 48:56], 1.0 / D),
               sV, waits=[(sC, sC.v)])
        E.addc("vector", lambda v: v.tensor_scalar_mul(statc[:, 48:56], statc[:, 56:64], 1.0 / D), sV)
        E.addc("vector", lambda v: v.tensor_tensor(statc[:, 56:64], statc[:, 0:8], statc[:, 0:8], ALU.mult), sV)
        E.addc("vector", lambda v: v.tensor_tensor(statc[:, 8:16], statc[:, 48:56], statc[:, 56:64], ALU.subtract), sV)
        E.addc("vector", lambda v: v.tensor_scalar_add(statc[:, 8:16], statc[:, 8:16], EPS), sV)
        E.add("scalar", lambda s: s.activation(statc[:, 48:56], statc[:, 8:16], AF.Sqrt),
              waits=[(sV, sV.v)], inc=sC)
        E.addc("vector", lambda v: v.reciprocal(statc[:, 16:24], statc[:, 48:56]),
               sV, waits=[(sC, sC.v)])
        E.addc("vector", lambda v: v.tensor_tensor(statc[:, 48:56], statc[:, 16:24], statc[:, 16:24], ALU.mult), sV)
        E.addc("vector", lambda v: v.tensor_scalar_add(statc[:, 56:64], statc[:, 8:16], -EPS), sV)
        E.addc("vector", lambda v: v.tensor_tensor(statc[:, 48:56], statc[:, 48:56], statc[:, 56:64], ALU.mult), sV)
        E.addc("vector", lambda v: v.tensor_scalar_add(statc[:, 48:56], statc[:, 48:56], EPS), sV)
        E.add("scalar", lambda s: s.activation(statc[:, 48:56], statc[:, 48:56], AF.Sqrt),
              waits=[(sV, sV.v)], inc=sC)
        E.addc("vector", lambda v: v.reciprocal(statc[:, 56:64], statc[:, 48:56]),
               sV, waits=[(sC, sC.v)])
        E.addc("vector", lambda v: v.tensor_tensor(statc[:, 24:32], statc[:, 16:24], statc[:, 56:64], ALU.mult), sV)
        E.addc("vector", lambda v: v.tensor_tensor(statc[:, 32:40], statc[:, 16:24], statc[:, 0:8], ALU.mult), sV)
        E.addc("vector", lambda v: v.tensor_scalar_mul(statc[:, 32:40], statc[:, 32:40], -1.0), sV)
        E.addc("vector", lambda v: v.tensor_tensor(statc[:, 40:48], statc[:, 24:32], statc[:, 0:8], ALU.mult), sV)
        E.addc("vector", lambda v: v.tensor_scalar_mul(statc[:, 40:48], statc[:, 40:48], -1.0), sV)
        for c in range(NCH):
            E.add("vector", lambda v, c=c: v.tensor_scalar(
                xn_tok[:, c * D:(c + 1) * D], xhat[:, c * D:(c + 1) * D],
                statc[:, 0 + c:1 + c], statc[:, 16 + c:17 + c], ALU.subtract, ALU.mult),
                inc=(sV if c == NCH - 1 else None))
        E.dma("sync", dr["xn_rows"].ap().rearrange("(p c) d -> p c d", c=NCH),
              xn_tok[:].rearrange("p (c d) -> p c d", c=NCH), sX, waits=[(sV, sV.v)])

        E.add("tensor", lambda t: t.transpose(pX0[0:32, 0:128], statc[:, 16:48], identf[:]),
              waits=[(sV, sV.v)], inc=sT)
        E.add("vector", lambda v: v.tensor_copy(srow_t[:, :], pX0[0:32, 0:128]),
              waits=[(sT, sT.v)], inc=sV)
        for slot, srcrow in [(0, 1), (1, 3), (2, 0), (3, 2)]:
            E.dma("sync", statrow[0:1, :], srow_t[8 * srcrow:8 * srcrow + 8, :], sX,
                  waits=[(sV, sV.v)] + ([(sT, sT.v)] if slot > 0 else []))
            cXr = (sX, sX.v)
            for h in range(2):
                pb = pX0 if h == 0 else pX1
                E.add("tensor", lambda t, pb=pb, h=h: t.matmul(
                    pb[:, 0:512], onesr[:, :],
                    statrow[0:1, h * 512:(h + 1) * 512], start=True, stop=True),
                    waits=([cXr] if h == 0 else []), inc=sT)
            cTb = (sT, sT.v)
            for h in range(2):
                pb = pX0 if h == 0 else pX1
                E.add("vector", lambda v, pb=pb, slot=slot, h=h: v.tensor_copy(
                    rep_sl(slot, h), pb[:, 0:512]),
                    waits=([cTb] if h == 0 else []), inc=(sV if h == 1 else None))

        for dc in range(NDC):
            items = []
            for c in range(NCH):
                items.append((xhat[:, c * D + dc * 128: c * D + (dc + 1) * 128],
                              lambda v, pb, dc=dc, c=c: v.tensor_copy(
                                  xgT[:, dc * CT + c * 128: dc * CT + (c + 1) * 128], pb)))
            fw = [(sV, sV.v)] if dc == 0 else []
            tr_evac_pipe(items, "bf", first_waits=fw)
            cVe = (sV, sV.v)
            for h in range(2):
                E.add("vector", lambda v, dc=dc, h=h: v.tensor_tensor(
                    xnT_t[:, h * 512:(h + 1) * 512],
                    xgT[:, dc * CT + h * 512: dc * CT + (h + 1) * 512], rep_sl(2, h), ALU.mult),
                    waits=([cVe, (sT, sT.v)] if h == 0 else []))
            for h in range(2):
                E.add("vector", lambda v, h=h: v.tensor_tensor(
                    xnT_t[:, h * 512:(h + 1) * 512], xnT_t[:, h * 512:(h + 1) * 512],
                    rep_sl(3, h), ALU.add), inc=(sV if h == 1 else None))
            cVx = (sV, sV.v)
            for h in range(2):
                pb = pM0 if h == 0 else pM1
                E.add("tensor", lambda t, pb=pb, dc=dc, h=h: t.matmul(
                    pb[0:NE, :], rtw[:, dc * NE:(dc + 1) * NE], xnT_t[:, h * 512:(h + 1) * 512],
                    start=(dc == 0), stop=(dc == NDC - 1)),
                    waits=([cVx] if h == 0 else []), inc=(sT if h == 1 else None))
            E.add("vector", lambda v, dc=dc: v.tensor_scalar_mul(
                xgT[:, dc * CT: dc * CT + TOK], xgT[:, dc * CT: dc * CT + TOK],
                vcols[:, VG1 + dc:VG1 + dc + 1]))
            E.add("vector", lambda v, dc=dc: v.tensor_copy(
                xgT[:, dc * CT + TOK: dc * CT + TOK + 1], vcols[:, VG1 + dc:VG1 + dc + 1]))
            E.add("vector", lambda v, dc=dc: v.tensor_copy(
                xgT[:, dc * CT + TOK + 1: dc * CT + TOK + 2], vcols[:, VB1 + dc:VB1 + dc + 1]),
                inc=sV)

        cRT = (sT, sT.v)
        for h in range(2):
            pb = pM0 if h == 0 else pM1
            E.add("vector", lambda v, pb=pb, h=h: v.tensor_copy(
                rsc[:, h * 512:(h + 1) * 512], pb[0:NE, :]),
                waits=([cRT] if h == 0 else []), inc=(sV if h == 1 else None))
        items = []
        for c in range(NCH):
            items.append((rsc[:, c * 128:(c + 1) * 128],
                          lambda v, pb, c=c: v.tensor_copy(small[:, 8:16], pb[:, 0:8])))
        tbase = sT.v
        for c in range(NCH):
            pb = (pX0 if (c % 2 == 0) else pX1)[:, 0:8]
            E.add("tensor", lambda t, c=c, pb=pb: t.transpose(
                pb, rsc[:, c * 128:(c + 1) * 128], identf[0:8, 0:8]),
                waits=([(sV, sV.v)] if c == 0 else [(sV, sV.v)]), inc=sT)
            E.addc("vector", lambda v, pb=pb: v.tensor_copy(small[:, 8:16], pb),
                   sV, waits=[(sT, sT.v)])
            E.addc("vector", lambda v: v.max(out=mx8[:], in_=small[:, 8:16]), sV)
            E.addc("vector", lambda v: v.max_index(idx8[:], mx8[:], small[:, 8:16]), sV)
            E.addc("vector", lambda v, c=c: v.tensor_copy(routc[:, 32 + c:33 + c], mx8[:, 0:1]), sV)
            E.addc("vector", lambda v, c=c: v.tensor_copy(routc[:, 40 + c:41 + c], mx8[:, 1:2]), sV)
            E.addc("vector", lambda v, c=c: v.tensor_copy(routc[:, 0 + c:1 + c], idx8[:, 0:1]), sV)
            E.addc("vector", lambda v, c=c: v.tensor_copy(routc[:, 8 + c:9 + c], idx8[:, 1:2]), sV)
        E.addc("vector", lambda v: v.tensor_tensor(routc[:, 40:48], routc[:, 40:48], routc[:, 32:40], ALU.subtract), sV)
        E.add("scalar", lambda s: s.activation(routc[:, 32:40], routc[:, 40:48], AF.Exp),
              waits=[(sV, sV.v)], inc=sC)
        E.addc("vector", lambda v: v.tensor_scalar_add(routc[:, 40:48], routc[:, 32:40], 1.0),
               sV, waits=[(sC, sC.v)])
        E.addc("vector", lambda v: v.reciprocal(routc[:, 16:24], routc[:, 40:48]), sV)
        E.addc("vector", lambda v: v.tensor_scalar(routc[:, 24:32], routc[:, 16:24], -1.0, 1.0, ALU.mult, ALU.add), sV)
        for s_ in range(2):
            E.addc("vector", lambda v, s_=s_: v.tensor_copy(
                tpv[:, s_ * 3 * NCH + 0: (s_ + 1) * 3 * NCH: 3], pcv[:, 0:2 * NCH:2]), sV)
            E.addc("vector", lambda v, s_=s_: v.tensor_copy(
                tpv[:, s_ * 3 * NCH + 1: (s_ + 1) * 3 * NCH: 3], pcv[:, 1:2 * NCH:2]), sV)
            E.addc("vector", lambda v, s_=s_: v.tensor_copy(
                tpv[:, s_ * 3 * NCH + 2: (s_ + 1) * 3 * NCH: 3], routc[:, 16 + 8 * s_:24 + 8 * s_]), sV)

        for g_ in range(NG):
            s_, e_ = g_ // NE, g_ % NE
            icol = routc[:, 8 * s_: 8 * s_ + 8]
            E.addc("vector", lambda v, icol=icol, e_=e_: v.tensor_single_scalar(
                msk8[:, 0:8], icol, float(e_), ALU.is_equal),
                sV, waits=[(sT, sT.v)])
            E.addc("vector", lambda v: v.tensor_tensor_scan(
                msk8[:, 8:16], msk8[:, 0:8], msk8[:, 0:8], 0.0, ALU.add, ALU.bypass), sV)
            E.addc("vector", lambda v: v.tensor_tensor(
                msk8[:, 16:24], msk8[:, 8:16], msk8[:, 0:8], ALU.subtract), sV)
            E.add("tensor", lambda t: t.matmul(pPh[:, 0:1], Us[:], msk8[:, 15:16],
                                               start=True, stop=True),
                  waits=[(sV, sV.v)], inc=sT)
            E.addc("vector", lambda v: v.tensor_copy(msk8[:, 24:25], pPh[:, 0:1]),
                   sV, waits=[(sT, sT.v)])
            E.addc("vector", lambda v: v.tensor_scalar_add(msk8[:, 16:24], msk8[:, 16:24], msk8[:, 24:25]), sV)
            E.addc("vector", lambda v: v.tensor_single_scalar(msk8[:, 32:40], msk8[:, 0:8], 0.0, ALU.is_equal), sV)
            E.addc("vector", lambda v: v.scalar_tensor_tensor(
                msk8[:, 16:24], msk8[:, 32:40], 4096.0, msk8[:, 16:24], ALU.mult, ALU.add), sV)
            pg = pM2[:, 8 * (g_ % 2): 8 * (g_ % 2) + 6]
            for c in range(NCH):
                E.addc("vector", lambda v, c=c: v.tensor_single_scalar(
                    selt[:], iotaC[:], msk8[:, 16 + c:17 + c], ALU.is_equal),
                    sV, waits=([] if c == 0 else [(sT, sT.v)]))
                for jb in range(2):
                    E.add("tensor", lambda t, c=c, jb=jb, s_=s_, pg=pg: t.matmul(
                        pg[:, 3 * jb: 3 * (jb + 1)], selt[:, jb * 128:(jb + 1) * 128],
                        tpv[:, s_ * 3 * NCH + 3 * c: s_ * 3 * NCH + 3 * (c + 1)],
                        start=(c == 0 and jb == 0), stop=(c == NCH - 1 and jb == 1),
                        skip_group_check=True),
                        waits=([(sV, sV.v)] if jb == 0 else []),
                        inc=(sT if jb == 1 else None))
            E.addc("vector", lambda v, g_=g_, pg=pg: v.tensor_copy(
                lists[:, 6 * g_: 6 * (g_ + 1)], pg),
                sV, waits=[(sT, sT.v)])
        pp1 = lists[:, 0:6 * NG:3]
        cc = lists[:, 1:6 * NG:3]
        gg = lists[:, 2:6 * NG:3]
        E.addc("vector", lambda v: v.scalar_tensor_tensor(small[:, 0:32], pp1, 8.0, cc, ALU.mult, ALU.add), sV)
        E.addc("vector", lambda v: v.tensor_scalar_add(small[:, 0:32], small[:, 0:32], -8.0), sV)
        E.addc("vector", lambda v: v.tensor_scalar_max(small[:, 32:64], small[:, 0:32], 0.0), sV)
        E.addc("vector", lambda v: v.tensor_copy(idxgI[:], small[:, 32:64]), sV)
        E.addc("vector", lambda v: v.tensor_single_scalar(small[:, 32:64], small[:, 0:32], 0.0, ALU.is_lt), sV)
        E.addc("vector", lambda v: v.scalar_tensor_tensor(small[:, 0:32], small[:, 32:64],
                                                          float(TOK + 8), small[:, 0:32], ALU.mult, ALU.add), sV)
        E.addc("vector", lambda v: v.tensor_copy(idxsI[:], small[:, 0:32]), sV)
        E.addc("vector", lambda v: v.tensor_copy(gatec[:], gg), sV)

        if BUILD_PHASES < 2:
            trunc_finalize()
            return nc
        esA.close()
        wqb = [SB("wqb0", [128, NDC * 128], BF16, esB), SB("wqb1", [128, NDC * 128], BF16, esB)]
        qout = SB("qout", [128, TOK], BF16, esB)
        vtr = [SB("vtr0", [128, 128], BF16, esB), SB("vtr1", [128, 128], BF16, esB)]
        NOB = NDC + 2

        def wq_src(o):
            if o < NDC:
                return dr["wq"].ap().rearrange("(dc p) (ob oj) -> ob p dc oj", p=128, oj=128)[o]
            return dr["wkv"].ap().rearrange("(dc p) (ob oj) -> ob p dc oj", p=128, oj=128)[o - NDC]

        qv_in = dr["q_in"].ap().rearrange("(j r t) -> j r t", j=NCORES, r=256)
        kv_inK = dr["kv_in"].ap().rearrange("(a b c) -> a b c", a=2, b=128)[0]
        kv_inV = dr["kv_in"].ap().rearrange("(a b c) -> a b c", a=2, b=1024)[1]

        E.dma("gpsimd", wqb[0][:].rearrange("p (dc oj) -> p dc oj", dc=NDC), wq_src(0), sW,
              waits=[(sV, sV.v)])
        blk_done_T = []
        epi_done_V = []
        store_O = []
        for o in range(NOB):
            wb = wqb[o % 2]
            if o + 1 < NOB:
                w = [(sT, blk_done_T[o - 1])] if o >= 1 else []
                E.dma("gpsimd", wqb[(o + 1) % 2][:].rearrange("p (dc oj) -> p dc oj", dc=NDC),
                      wq_src(o + 1), sW, waits=w)
            pmain = (pM0, pM1) if (o % 2 == 0) else (pM2, pM3)
            phsl = (pPh if (o % 2 == 0) else pPh2)[:, 0:2]
            for dc in range(NDC):
                wts = []
                if dc == 0:
                    wts.append((sW, sW.v - (16 if o + 1 < NOB else 0)))
                    if o >= 2:
                        wts.append((sV, epi_done_V[o - 2]))
                for h in range(2):
                    E.add("tensor", lambda t, wb=wb, dc=dc, h=h, pmain=pmain: t.matmul(
                        pmain[h][:], wb[:, dc * 128:(dc + 1) * 128],
                        xgT[:, dc * CT + h * 512: dc * CT + (h + 1) * 512],
                        start=(dc == 0), stop=(dc == NDC - 1)),
                        waits=(wts if h == 0 else []))
                E.add("tensor", lambda t, wb=wb, dc=dc, phsl=phsl: t.matmul(
                    phsl, wb[:, dc * 128:(dc + 1) * 128],
                    xgT[:, dc * CT + TOK: dc * CT + TOK + 2],
                    start=(dc == 0), stop=(dc == NDC - 1), skip_group_check=True),
                    inc=(sT if dc == NDC - 1 else None))
            blk_done_T.append(sT.v)
            bias = (vcols[:, VBQ + o: VBQ + o + 1] if o < NDC
                    else vcols[:, VBKV + (o - NDC): VBKV + (o - NDC) + 1])
            wq_epi_w = [(sT, sT.v)]
            if o >= 1:
                wq_epi_w.append((sO, store_O[o - 1]))
            E.addc("vector", lambda v, phsl=phsl: v.tensor_copy(ucol[:, 0:2], phsl),
                   sV, waits=wq_epi_w)
            E.addc("vector", lambda v, bias=bias: v.tensor_tensor(ucol[:, 2:3], ucol[:, 1:2], bias, ALU.add), sV)
            for h in range(2):
                sl = slice(h * 512, (h + 1) * 512)
                E.add("vector", lambda v, sl=sl, h=h, pmain=pmain: v.tensor_tensor(
                    qout[:, sl], pmain[h][:], rep_sl(0, h), ALU.mult))
                E.add("vector", lambda v, sl=sl, h=h: v.scalar_tensor_tensor(
                    qout[:, sl], rep_sl(1, h), ucol[:, 0:1], qout[:, sl], ALU.mult, ALU.add))
                E.add("vector", lambda v, sl=sl: v.tensor_scalar_add(qout[:, sl], qout[:, sl], ucol[:, 2:3]),
                      inc=(sV if h == 1 else None))
            epi_done_V.append(sV.v)
            if o < NDC:
                E.dma("sync", qv_in[o // 2, (o % 2) * 128:(o % 2) * 128 + 128, :], qout[:], sO,
                      waits=[(sV, sV.v)])
                store_O.append(sO.v)
            elif o == NDC:
                E.dma("sync", kv_inK, qout[:], sO, waits=[(sV, sV.v)])
                store_O.append(sO.v)
            else:
                tb0, vb0, ob0 = sT.v, sV.v, sO.v
                for c in range(NCH):
                    vb = vtr[c % 2]
                    w = [(sV, sV.v)] if c == 0 else []
                    pb = pXb(c)[:, 0:128]
                    if c >= 2:
                        w.append((sV, vb0 + c - 1))
                    E.add("tensor", lambda t, c=c, pb=pb: t.transpose(
                        pb, qout[:, c * 128:(c + 1) * 128], identb[:]), waits=w, inc=sT)
                    w2_ = [(sT, sT.v)]
                    if c >= 2:
                        w2_.append((sO, ob0 + 16 * (c - 1)))
                    E.add("vector", lambda v, vb=vb, pb=pb: v.tensor_copy(vb[:], pb),
                          waits=w2_, inc=sV)
                    E.dma("sync", kv_inV[c * 128:(c + 1) * 128, :], vb[:], sO, waits=[(sV, sV.v)])
                store_O.append(sO.v)

        E.add("gpsimd", lambda g: g.collective_compute(
            "AllGather", ALU.bypass, replica_groups=rg,
            ins=[dr["kv_in"].ap().opt()], outs=[dr["kv_out"].ap().opt()]),
            waits=[(sO, sO.v)], inc=sCC)

        def a2a_q(g):
            with allow_ici_all_to_all():
                return g.collective_compute("AllToAll", ALU.bypass, replica_groups=rg,
                                            ins=[dr["q_in"].ap().opt()], outs=[dr["q_out"].ap().opt()])
        E.add("gpsimd", a2a_q, inc=sCC)
        cc_after_kvq = sCC.v

        if BUILD_PHASES < 3:
            trunc_finalize()
            return nc
        esB.close()
        esC = ExitStack()
        open_stacks.append(esC)
        gx = [SB("gx0", [128, D], BF16, esC), SB("gx1", [128, D], BF16, esC)]
        xeT = SB("xeT", [128, NDC * 512], BF16, esC)
        w13 = SB("w13", [128, NDC * 512], BF16, esC)
        w2b = SB("w2b", [128, 4 * D], BF16, esC)
        y1t = SB("y1t", [128, 4 * 512], BF16, esC)
        y3t = SB("y3t", [128, 4 * 512], BF16, esC)
        wout = [SB("wout0", [128, D], F32, esC), SB("wout1", [128, D], F32, esC)]
        xnr = dr["xn_rows"].ap()

        gath_T = sT.v
        for e_ in range(NE):
            cols = [2 * e_, 2 * e_ + 1, 2 * (NE + e_), 2 * (NE + e_) + 1]
            for ci, k in enumerate(cols):
                gbuf = gx[ci % 2]
                w = [(sX, sX.v), (sV, sV.v), (sO, sO.v)] if (e_ == 0 and ci == 0) else []
                if e_ > 0 or ci >= 2:
                    w.append((sT, gath_T))
                E.add("gpsimd", lambda g, gbuf=gbuf, k=k: g.indirect_dma_start(
                    gbuf[:], None, xnr, IndirectOffsetOnAxis(idxgI[:, k:k + 1], 0)),
                    waits=w, inc=sG, incn=16)
                tb0, vb0 = sT.v, sV.v
                for dc in range(NDC):
                    pb = pXb(dc)[:, 0:128]
                    w = [(sG, sG.v)] if dc == 0 else []
                    if vb0 + dc - 1 > 0:
                        w.append((sV, vb0 + dc - 1))
                    E.add("tensor", lambda t, gbuf=gbuf, dc=dc, pb=pb: t.transpose(
                        pb, gbuf[:, dc * 128:(dc + 1) * 128], identb[:]), waits=w, inc=sT)
                    E.add("vector", lambda v, dc=dc, ci=ci, pb=pb: v.tensor_copy(
                        xeT[:, dc * 512 + ci * 128: dc * 512 + (ci + 1) * 128], pb),
                        waits=[(sT, tb0 + dc + 1)], inc=sV)
                gath_T = sT.v
            for wi, wname, ydst in [(0, "w1", y1t), (1, "w3", y3t)]:
                E.dma("gpsimd", w13[:].rearrange("p (dc f) -> p dc f", dc=NDC),
                      dr[wname].ap()[e_].rearrange("(dc p) f -> p dc f", p=128), sW,
                      waits=[(sT, sT.v)])
                for ffo in range(4):
                    wts = []
                    if ffo == 0:
                        wts = [(sW, sW.v), (sC, sC.v)] + ([(sV, sV.v)] if wi == 0 else [])
                    if ffo >= 2:
                        wts.append((sC, sC.v - 1))
                    pb = pM0 if (ffo % 2 == 0) else pM1
                    for dc in range(NDC):
                        E.add("tensor", lambda t, dc=dc, ffo=ffo, pb=pb: t.matmul(
                            pb[:], w13[:, dc * 512 + ffo * 128: dc * 512 + (ffo + 1) * 128],
                            xeT[:, dc * 512:(dc + 1) * 512],
                            start=(dc == 0), stop=(dc == NDC - 1)),
                            waits=(wts if dc == 0 else []),
                            inc=(sT if dc == NDC - 1 else None))
                    fn = AF.Silu if wi == 0 else AF.Copy
                    E.add("scalar", lambda s, ydst=ydst, ffo=ffo, pb=pb, fn=fn: s.activation(
                        ydst[:, ffo * 512:(ffo + 1) * 512], pb[:], fn),
                        waits=[(sT, sT.v)], inc=sC)
            E.add("vector", lambda v: v.tensor_tensor(y1t[:], y1t[:], y3t[:], ALU.mult),
                  waits=[(sC, sC.v)], inc=sV)
            E.dma("gpsimd", w2b[:].rearrange("p (fc d) -> p fc d", fc=4),
                  dr["w2"].ap()[e_].rearrange("(fc p) d -> p fc d", p=128), sW,
                  waits=[(sT, sT.v)])
            scat_G = []
            for tc in range(4):
                wo_ = wout[tc % 2]
                k = cols[tc]
                tb0, vb0 = sT.v, sV.v
                for do in range(4):
                    pb = pM2 if (do % 2 == 0) else pM3
                    wts = []
                    if do == 0 and tc == 0:
                        wts = [(sW, sW.v), (sV, sV.v)]
                    if vb0 + do - 1 > 0:
                        wts.append((sV, vb0 + do - 1))
                    for fc in range(4):
                        E.add("tensor", lambda t, fc=fc, do=do, tc=tc, pb=pb: t.matmul(
                            pb[:], y1t[:, fc * 512 + tc * 128: fc * 512 + (tc + 1) * 128],
                            w2b[:, fc * D + do * 512: fc * D + (do + 1) * 512],
                            start=(fc == 0), stop=(fc == 3)),
                            waits=(wts if fc == 0 else []),
                            inc=(sT if fc == 3 else None))
                for do in range(4):
                    pb = pM2 if (do % 2 == 0) else pM3
                    w = [(sT, tb0 + do + 1)]
                    if do == 0 and tc >= 2:
                        w.append((sG, scat_G[tc - 2]))
                    E.add("vector", lambda v, do=do, pb=pb, wo_=wo_, k=k: v.tensor_scalar_mul(
                        wo_[:, do * 512:(do + 1) * 512], pb[:], gatec[:, k:k + 1]),
                        waits=w, inc=sV)
                dst = dr["moeA"] if tc < 2 else dr["moeB"]
                E.add("gpsimd", lambda g, dst=dst, k=k, wo_=wo_: g.indirect_dma_start(
                    dst.ap(), IndirectOffsetOnAxis(idxsI[:, k:k + 1], 0), wo_[:], None),
                    waits=[(sV, sV.v)], inc=sG, incn=16)
                scat_G.append(sG.v)
        moe_G = sG.v

        if BUILD_PHASES < 4:
            trunc_finalize()
            return nc
        esC.close()
        es4 = ExitStack()
        open_stacks.append(es4)
        kT = SB("kT", [128, S], BF16, es4)
        vT = SB("vT", [128, 64 * 128], BF16, es4)
        qtl = [SB("qt0", [128, 512], BF16, es4), SB("qt1", [128, 512], BF16, es4)]
        prb = [SB("prb0", [128, 512], BF16, es4), SB("prb1", [128, 512], BF16, es4)]
        atile = SB("atile", [128, 512], BF16, es4)

        E.dma("sync", kT[:].rearrange("p (r t) -> p r t", r=NCORES),
              dr["kv_out"].ap().rearrange("(r a p t) -> p r a t", r=NCORES, a=2, p=128)[:, :, 0, :],
              sX, waits=[(sCC, cc_after_kvq), (sT, sT.v), (sV, sV.v), (sC, sC.v),
                         (sG, sG.v), (sO, sO.v)])
        kvoV = dr["kv_out"].ap().rearrange("(r a c p j) -> r a c p j", r=NCORES, a=2,
                                           c=NCH, p=128)[:, 1].rearrange("r c p j -> p r c j")
        for r_ in range(NCORES):
            E.dma("sync", vT[:].rearrange("p (r c j) -> p r c j", r=NCORES, c=NCH)[:, r_],
                  kvoV[:, r_], sX)
        qov = dr["q_out"].ap().rearrange("(j r t) -> j r t", j=NCORES, r=256)
        at_inv = dr["at_in"].ap().rearrange("(j r t) -> j r t", j=NCORES, r=256)

        kv_ready = (sX, sX.v)
        prev_qload = {0: None, 1: None}
        prev_store_O = None
        for hh in range(HPC):
            for qt in range(NQT):
                NK = 4 * qt + 4
                qb = qtl[qt % 2]
                w = [kv_ready] if (hh == 0 and qt == 0) else []
                if prev_qload[qt % 2] is not None:
                    w.append((sT, prev_qload[qt % 2]))
                E.dma("sync", qb[:], qov[qt // 2, hh * 128:(hh + 1) * 128,
                                         (qt % 2) * 512:((qt % 2) + 1) * 512], sX, waits=w)
                qload = (sX, sX.v)
                tb, cb, vb = sT.v, sC.v, sV.v
                sval = {}
                oval = {}
                cnt = tb
                order = []
                for k in range(NK + 1):
                    if k < NK:
                        cnt += 1; sval[k] = cnt; order.append(("S", k))
                    if k >= 1:
                        cnt += 1; oval[k - 1] = cnt; order.append(("O", k - 1))
                for tag, k in order:
                    if tag == "S":
                        pS = pM0 if (k % 2 == 0) else pM1
                        w = []
                        if k == 0:
                            w.append(qload)
                            if prev_store_O is not None:
                                w.append((sO, prev_store_O))
                        if k >= 2:
                            w.append((sC, cb + k - 1))
                        E.add("tensor", lambda t, k=k, pS=pS, qb=qb: t.matmul(
                            pS[:], kT[:, k * 128:(k + 1) * 128], qb[:], start=True, stop=True),
                            waits=w, inc=sT)
                    else:
                        pr = prb[k % 2]
                        w = []
                        if k >= 4 * qt:
                            w.append((sV, vb + (k - 4 * qt) + 1))
                        else:
                            w.append((sC, cb + k + 1))
                        E.add("tensor", lambda t, k=k, pr=pr, NK=NK: t.matmul(
                            pM2[:], onesb[:], pr[:], start=(k == 0), stop=(k == NK - 1)),
                            waits=w)
                        E.add("tensor", lambda t, k=k, pr=pr, NK=NK: t.matmul(
                            pM3[:], vT[:, k * 128:(k + 1) * 128], pr[:],
                            start=(k == 0), stop=(k == NK - 1)), inc=sT)
                for k in range(NK):
                    pS = pM0 if (k % 2 == 0) else pM1
                    pr = prb[k % 2]
                    w = [(sT, sval[k])]
                    if k >= 2:
                        w.append((sT, oval[k - 2]))
                    else:
                        w.append((sT, tb))
                    E.add("scalar", lambda s, pS=pS, pr=pr: s.activation(
                        pr[:], pS[:], AF.Exp, bias=vcols[:, 114:115], scale=SCALE), waits=w, inc=sC)
                for r_ in range(4):
                    k = 4 * qt + r_
                    pr = prb[k % 2]
                    E.add("vector", lambda v, pr=pr, r_=r_: v.tensor_tensor(
                        pr[:], pr[:], masks[:, r_ * 512:(r_ + 1) * 512], ALU.mult),
                        waits=[(sC, cb + k + 1)] + ([(sT, oval[k - 2])] if k >= 2 else []),
                        inc=sV)
                E.add("vector", lambda v: v.reciprocal(rec[:], pM2[:]),
                      waits=[(sT, sT.v)])
                E.add("vector", lambda v: v.tensor_tensor(atile[:], pM3[:], rec[:], ALU.mult),
                      inc=sV)
                E.dma("sync", at_inv[qt // 2, hh * 128:(hh + 1) * 128,
                                     (qt % 2) * 512:((qt % 2) + 1) * 512], atile[:], sO,
                      waits=[(sV, sV.v)])
                prev_qload[qt % 2] = sT.v
                prev_store_O = sO.v

        def a2a_at(g):
            with allow_ici_all_to_all():
                return g.collective_compute("AllToAll", ALU.bypass, replica_groups=rg,
                                            ins=[dr["at_in"].ap().opt()], outs=[dr["at_out"].ap().opt()])
        E.add("gpsimd", a2a_at, waits=[(sO, sO.v)], inc=sCC)
        cc_after_at = sCC.v

        if BUILD_PHASES < 5:
            trunc_finalize()
            return nc
        es4.close()
        es2 = ExitStack()
        open_stacks.append(es2)
        A = SB("A", [128, NCH * D], F32)
        CT2 = TOK + 2
        x2b = SB("x2b", [128, NDC * CT2], BF16)
        xn_o = [SB("xn_o0", [128, NCH * 128], BF16, es2), SB("xn_o1", [128, NCH * 128], BF16, es2)]
        atn = SB("atn", [128, NH * TOK], BF16, es2)
        xnTt = SB("xnTt", [128, TOK], BF16, es2)
        x2f = SB("x2f", [128, TOK], F32, es2)
        wob = [SB("wob0", [128, NDC * 128], BF16, es2), SB("wob1", [128, NDC * 128], BF16, es2)]

        E.dma("sync", atn[:].rearrange("p (h t) -> p h t", h=NH),
              dr["at_out"].ap().rearrange("(j k p t) -> p (j k) t", j=NCORES, k=HPC, p=128),
              sX, waits=[(sCC, cc_after_at), (sT, sT.v), (sV, sV.v), (sO, sO.v)])
        E.dma("sync", A[:].rearrange("p (c d) -> p c d", c=NCH),
              dr["x"].ap().rearrange("(c p) d -> p c d", p=128), sX)
        atn_ready = (sX, sX.v)
        xnp_T = []

        wo_src = dr["wo"].ap().rearrange("(dc p) (ob oj) -> ob p dc oj", p=128, oj=128)
        E.dma("gpsimd", wob[0][:].rearrange("p (dc oj) -> p dc oj", dc=NDC), wo_src[0], sW)
        blk_T = []
        epi_V = []
        for o in range(NDC):
            wb = wob[o % 2]
            if o + 1 < NDC:
                w = [(sT, blk_T[o - 1])] if o >= 1 else []
                E.dma("gpsimd", wob[(o + 1) % 2][:].rearrange("p (dc oj) -> p dc oj", dc=NDC),
                      wo_src[o + 1], sW, waits=w)
            pmo = (pM0, pM1) if (o % 2 == 0) else (pM2, pM3)
            for hc in range(NH):
                wts = []
                if hc == 0:
                    wts = [(sW, sW.v - (16 if o + 1 < NDC else 0)), atn_ready]
                    if o >= 1:
                        wts.append((sV, epi_V[o - 1]))
                for h in range(2):
                    E.add("tensor", lambda t, wb=wb, hc=hc, h=h, pmo=pmo: t.matmul(
                        pmo[h][:], wb[:, hc * 128:(hc + 1) * 128],
                        atn[:, hc * TOK + h * 512: hc * TOK + (h + 1) * 512],
                        start=(hc == 0), stop=(hc == NH - 1)),
                        waits=(wts if h == 0 else []),
                        inc=(sT if (hc == NH - 1 and h == 1) else None))
            blk_T.append(sT.v)
            xno = xn_o[o % 2]
            wxn = [(sT, xnp_T[o - 2])] if o >= 2 else []
            E.dma("sync", xno[:].rearrange("p (c d) -> p c d", c=NCH),
                  dr["xn_rows"].ap().rearrange("(p c) d -> p c d", c=NCH)[:, :, o * 128:(o + 1) * 128],
                  sX, waits=wxn)
            items = [(xno[:, c * 128:(c + 1) * 128],
                      lambda v, pb, c=c: v.tensor_copy(xnTt[:, c * 128:(c + 1) * 128], pb))
                     for c in range(NCH)]
            tr_evac_pipe(items, "bf", first_waits=[(sV, sV.v), (sX, sX.v)])
            xnp_T.append(sT.v)
            mm_done = (sT, blk_T[-1])
            for h in range(2):
                sl = slice(h * 512, (h + 1) * 512)
                pm = pmo[h]
                E.add("vector", lambda v, sl=sl, pm=pm: v.tensor_tensor(
                    x2f[:, sl], pm[:], xnTt[:, sl], ALU.add),
                    waits=([mm_done, (sV, sV.v)] if h == 0 else []))
                E.add("vector", lambda v, sl=sl, o=o: v.tensor_scalar_add(
                    x2f[:, sl], x2f[:, sl], vcols[:, VBO + o:VBO + o + 1]))
            E.add("vector", lambda v, o=o: v.tensor_copy(x2b[:, o * CT2: o * CT2 + TOK], x2f[:]))
            E.add("vector", lambda v, o=o: v.tensor_copy(
                x2b[:, o * CT2 + TOK: o * CT2 + TOK + 1], vcols[:, VG2 + o:VG2 + o + 1]))
            E.add("vector", lambda v, o=o: v.tensor_copy(
                x2b[:, o * CT2 + TOK + 1: o * CT2 + TOK + 2], vcols[:, VB2 + o:VB2 + o + 1]),
                inc=sV)
            tbA, vbA = sT.v, sV.v
            for c in range(NCH):
                pb = (pX0 if (c % 2 == 0) else pX1)[:, 0:128]
                w = [(sV, sV.v)] if c == 0 else []
                if c >= 2:
                    w.append((sV, vbA + c - 1))
                E.add("tensor", lambda t, c=c, o=o, pb=pb: t.transpose(
                    pb, x2f[:, c * 128:(c + 1) * 128], identf[:]), waits=w, inc=sT)
                asl = A[:, c * D + o * 128: c * D + (o + 1) * 128]
                E.add("vector", lambda v, asl=asl, pb=pb: v.tensor_tensor(asl, asl, pb, ALU.add),
                      waits=[(sT, tbA + c + 1)], inc=sV)
            epi_V.append(sV.v)

        sq2 = SB("sq2", [128, TOK], BF16, es2)
        for o in range(NDC):
            for h in range(2):
                E.add("tensor", lambda t, o=o, h=h: t.matmul(
                    (pM2 if h == 0 else pM3)[:], onesb[:],
                    x2b[:, o * CT2 + h * 512: o * CT2 + (h + 1) * 512],
                    start=(o == 0), stop=(o == NDC - 1)),
                    waits=([(sV, sV.v)] if (o == 0 and h == 0) else []),
                    inc=(sT if (o == NDC - 1 and h == 1) else None))
        for o in range(NDC):
            E.add("vector", lambda v, o=o: v.tensor_tensor(
                sq2[:], x2b[:, o * CT2: o * CT2 + TOK], x2b[:, o * CT2: o * CT2 + TOK], ALU.mult),
                waits=([(sT, sT.v)] if o > 0 else []), inc=sV)
            for h in range(2):
                E.add("tensor", lambda t, o=o, h=h: t.matmul(
                    (pX0 if h == 0 else pX1)[:], onesb[:], sq2[:, h * 512:(h + 1) * 512],
                    start=(o == 0), stop=(o == NDC - 1)),
                    waits=([(sV, sV.v)] if h == 0 else []),
                    inc=(sT if h == 1 else None))
        E.add("vector", lambda v: v.tensor_scalar_mul(rep_sl(2, 0), pM2[:], 1.0 / D),
              waits=[(sT, sT.v)])
        E.add("vector", lambda v: v.tensor_scalar_mul(rep_sl(2, 1), pM3[:], 1.0 / D))
        E.add("vector", lambda v: v.tensor_scalar_mul(rep_sl(3, 0), pX0[:], 1.0 / D))
        E.add("vector", lambda v: v.tensor_scalar_mul(rep_sl(3, 1), pX1[:], 1.0 / D))
        for h in range(2):
            E.add("vector", lambda v, h=h: v.tensor_tensor(rec[:], rep_sl(2, h), rep_sl(2, h), ALU.mult))
            E.add("vector", lambda v, h=h: v.tensor_tensor(rep_sl(3, h), rep_sl(3, h), rec[:], ALU.subtract))
            E.add("vector", lambda v, h=h: v.tensor_scalar_add(rep_sl(3, h), rep_sl(3, h), EPS),
                  inc=(sV if h == 1 else None))
        E.add("scalar", lambda s: s.activation(reps[:, 3 * TOK:4 * TOK], reps[:, 3 * TOK:4 * TOK], AF.Sqrt),
              waits=[(sV, sV.v)], inc=sC)
        E.add("vector", lambda v: v.reciprocal(reps[:, 0:TOK], reps[:, 3 * TOK:4 * TOK]),
              waits=[(sC, sC.v)])
        E.add("vector", lambda v: v.tensor_tensor(reps[:, TOK:2 * TOK], reps[:, 0:TOK],
                                                  reps[:, 2 * TOK:3 * TOK], ALU.mult))
        E.add("vector", lambda v: v.tensor_scalar_mul(reps[:, TOK:2 * TOK], reps[:, TOK:2 * TOK], -1.0),
              inc=sV)

        if BUILD_PHASES < 6:
            trunc_finalize()
            return nc
        es2.close()
        es6 = ExitStack()
        open_stacks.append(es6)
        NSW = 4
        NOF = (FF // NSW) // 128
        gbuf = SB("gbuf", [128, NOF * TOK], BF16, es6)
        wfb = [SB("wfb0", [128, NDC * 128], BF16, es6), SB("wfb1", [128, NDC * 128], BF16, es6)]
        wpb = [SB("wpb0", [128, NOF * 128], BF16, es6), SB("wpb1", [128, NOF * 128], BF16, es6)]
        fcf = SB("fcf", [128, TOK], F32, es6)
        spf = SB("spf", [128, 512], F32, es6)

        wfc_src = dr["wfc"].ap().rearrange("(dc p) (ob oj) -> ob p dc oj", p=128, oj=128)
        wpj_src = dr["wproj"].ap().rearrange("(sw fc p) (ob oj) -> sw ob p fc oj",
                                             p=128, oj=128, sw=NSW)
        for sw in range(NSW):
            E.dma("gpsimd", wfb[0][:].rearrange("p (dc oj) -> p dc oj", dc=NDC),
                  wfc_src[sw * NOF], sW, waits=[(sT, sT.v)])
            blkf_T = []
            gel_C = []
            for of in range(NOF):
                og = sw * NOF + of
                wb = wfb[of % 2]
                if of + 1 < NOF:
                    w = [(sT, blkf_T[of - 1])] if of >= 1 else []
                    E.dma("gpsimd", wfb[(of + 1) % 2][:].rearrange("p (dc oj) -> p dc oj", dc=NDC),
                          wfc_src[og + 1], sW, waits=w)
                phsl = (pPh if (of % 2 == 0) else pPh2)[:, 0:2]
                for dc in range(NDC):
                    wts = []
                    if dc == 0:
                        wts = [(sW, sW.v - (16 if of + 1 < NOF else 0))]
                        if of <= 1:
                            wts.append((sV, sV.v))
                            wts.append((sC, sC.v))
                        if of >= 2:
                            wts.append((sC, gel_C[of - 2]))
                    for h in range(2):
                        E.add("tensor", lambda t, wb=wb, dc=dc, h=h, of=of: t.matmul(
                            (pM0 if of % 2 == 0 else pM2, pM1 if of % 2 == 0 else pM3)[h][:],
                            wb[:, dc * 128:(dc + 1) * 128],
                            x2b[:, dc * CT2 + h * 512: dc * CT2 + (h + 1) * 512],
                            start=(dc == 0), stop=(dc == NDC - 1)),
                            waits=(wts if h == 0 else []))
                    E.add("tensor", lambda t, wb=wb, dc=dc, phsl=phsl: t.matmul(
                        phsl, wb[:, dc * 128:(dc + 1) * 128],
                        x2b[:, dc * CT2 + TOK: dc * CT2 + TOK + 2],
                        start=(dc == 0), stop=(dc == NDC - 1), skip_group_check=True),
                        inc=(sT if dc == NDC - 1 else None))
                blkf_T.append(sT.v)
                E.addc("vector", lambda v, phsl=phsl: v.tensor_copy(ucol[:, 0:2], phsl),
                       sV, waits=[(sT, sT.v), (sC, sC.v)])
                E.addc("vector", lambda v, og=og: v.tensor_tensor(
                    ucol[:, 2:3], ucol[:, 1:2], vcols[:, VBFC + og:VBFC + og + 1], ALU.add), sV)
                for h in range(2):
                    sl = slice(h * 512, (h + 1) * 512)
                    pm = (pM0 if of % 2 == 0 else pM2, pM1 if of % 2 == 0 else pM3)[h]
                    E.add("vector", lambda v, sl=sl, h=h, pm=pm: v.tensor_tensor(
                        fcf[:, sl], pm[:], rep_sl(0, h), ALU.mult),
                        waits=([(sC, gel_C[of - 1])] if (h == 0 and of >= 1) else []))
                    E.add("vector", lambda v, sl=sl, h=h: v.scalar_tensor_tensor(
                        fcf[:, sl], rep_sl(1, h), ucol[:, 0:1], fcf[:, sl], ALU.mult, ALU.add),
                        inc=(sV if h == 1 else None))
                E.add("scalar", lambda s, of=of: s.activation(
                    gbuf[:, of * TOK:(of + 1) * TOK], fcf[:], AF.Gelu_apprx_tanh, bias=ucol[:, 2:3]),
                    waits=[(sV, sV.v)], inc=sC)
                gel_C.append(sC.v)
            E.dma("gpsimd", wpb[0][:].rearrange("p (fc oj) -> p fc oj", fc=NOF),
                  wpj_src[sw, 0], sW, waits=[(sT, sT.v), (sC, sC.v)])
            blkp_T = []
            for o in range(NDC):
                wb = wpb[o % 2]
                if o + 1 < NDC:
                    w = [(sT, blkp_T[o - 1])] if o >= 1 else []
                    E.dma("gpsimd", wpb[(o + 1) % 2][:].rearrange("p (fc oj) -> p fc oj", fc=NOF),
                          wpj_src[sw, o + 1], sW, waits=w)
                for h in range(2):
                    pm = pM2 if h == 0 else pM3
                    for fc in range(NOF):
                        wts = []
                        if fc == 0:
                            wts = [(sW, sW.v - (16 if o + 1 < NDC else 0)), (sV, sV.v)]
                            if h == 0 and o == 0:
                                wts.append((sC, sC.v))
                        E.add("tensor", lambda t, wb=wb, fc=fc, pm=pm, h=h: t.matmul(
                            pm[:], wb[:, fc * 128:(fc + 1) * 128],
                            gbuf[:, fc * TOK + h * 512: fc * TOK + (h + 1) * 512],
                            start=(fc == 0), stop=(fc == NOF - 1)),
                            waits=(wts if fc == 0 else []),
                            inc=(sT if fc == NOF - 1 else None))
                    if sw == 0:
                        E.add("vector", lambda v, pm=pm, o=o: v.tensor_scalar_add(
                            spf[:], pm[:], vcols[:, VBPJ + o:VBPJ + o + 1]),
                            waits=[(sT, sT.v)])
                    else:
                        E.add("vector", lambda v, pm=pm: v.tensor_copy(spf[:], pm[:]),
                              waits=[(sT, sT.v)])
                    E.add("vector", lambda v: None, inc=sV)
                    tbB, vbB = sT.v, sV.v
                    for cc_ in range(4):
                        c = h * 4 + cc_
                        pb = (pX0 if (cc_ % 2 == 0) else pX1)[:, 0:128]
                        w = [(sV, sV.v)] if cc_ == 0 else []
                        if cc_ >= 2:
                            w.append((sV, vbB + cc_ - 1))
                        E.add("tensor", lambda t, cc_=cc_, pb=pb: t.transpose(
                            pb, spf[:, cc_ * 128:(cc_ + 1) * 128], identf[:]), waits=w, inc=sT)
                        asl = A[:, c * D + o * 128: c * D + (o + 1) * 128]
                        E.add("vector", lambda v, asl=asl, pb=pb: v.tensor_tensor(
                            asl, asl, pb, ALU.add), waits=[(sT, tbB + cc_ + 1)], inc=sV)
                blkp_T.append(sT.v)

        if BUILD_PHASES < 7:
            trunc_finalize()
            return nc
        es6.close()
        fm = [SB("fm0", [128, D], F32), SB("fm1", [128, D], F32)]
        for c in range(NCH):
            buf = fm[c % 2]
            w = [(sG, moe_G), (sV, sV.v)] if c == 0 else [(sV, sV.v)]
            E.dma("sync", buf[:], dr["moeA"].ap()[c:TOK:NCH, :], sX, waits=w)
            E.add("vector", lambda v, c=c, buf=buf: v.tensor_tensor(
                A[:, c * D:(c + 1) * D], A[:, c * D:(c + 1) * D], buf[:], ALU.add),
                waits=[(sX, sX.v)], inc=sV)
            E.dma("sync", buf[:], dr["moeB"].ap()[c:TOK:NCH, :], sX, waits=[(sV, sV.v)])
            E.add("vector", lambda v, c=c, buf=buf: v.tensor_tensor(
                A[:, c * D:(c + 1) * D], A[:, c * D:(c + 1) * D], buf[:], ALU.add),
                waits=[(sX, sX.v)], inc=sV)
            E.dma("sync", dr["out"].ap()[c * 128:(c + 1) * 128, :], A[:, c * D:(c + 1) * D],
                  sO, waits=[(sV, sV.v)])
        E.add("gpsimd", lambda g: g.wait_ge(sO.h, sO.v))

        E.flush(block)
    return nc


def _host_inputs(inputs):
    x = np.asarray(inputs["L0_attn_hidden_states"])[0]
    vecs = np.zeros((192, 128), np.float32)

    def putv(row0, vec):
        v = np.asarray(vec, np.float32).reshape(-1, 128)
        vecs[row0:row0 + v.shape[0]] = v

    putv(0, inputs["L0_attn_ln1_g"]); putv(16, inputs["L0_attn_ln1_b"])
    putv(32, inputs["L0_attn_ln2_g"]); putv(48, inputs["L0_attn_ln2_b"])
    putv(64, inputs["L0_attn_bq"]); putv(80, inputs["L0_attn_bo"])
    putv(96, inputs["L0_attn_bproj"]); putv(112, inputs["L0_attn_bkv"])
    putv(128, inputs["L0_attn_bfc"])
    vecs[114] = -SHIFT
    masks = np.zeros((4, 128, 512), np.float32)
    for r in range(4):
        masks[r] = (np.arange(512)[None, :] >= (np.arange(128)[:, None] + 128 * r))
    pcvals = np.zeros((128, 2 * NCH), np.float32)
    for c in range(NCH):
        pcvals[:, 2 * c] = np.arange(128) + 1.0
        pcvals[:, 2 * c + 1] = c
    common = {
        "wq": np.asarray(inputs["L0_attn_wq"], np.float32),
        "wkv": np.asarray(inputs["L0_attn_wkv"], np.float32),
        "wo": np.asarray(inputs["L0_attn_wo"], np.float32),
        "wfc": np.asarray(inputs["L0_attn_wfc"], np.float32),
        "wproj": np.asarray(inputs["L0_attn_wproj"], np.float32),
        "routerT": np.ascontiguousarray(np.asarray(inputs["L0_ffn_router_w"], np.float32).T),
        "w1": np.asarray(inputs["L0_ffn_w1"], np.float32),
        "w2": np.asarray(inputs["L0_ffn_w2"], np.float32),
        "w3": np.asarray(inputs["L0_ffn_w3"], np.float32),
        "vecs": vecs,
        "identf": np.eye(128, dtype=np.float32),
        "identb": np.eye(128, dtype=np.float32).astype(BF),
        "onesb": np.ones((128, 128), np.float32).astype(BF),
        "Ustrict": (np.arange(128)[:, None] < np.arange(128)[None, :]).astype(np.float32),
        "masks": masks.astype(BF),
        "iotaC": np.tile(np.arange(C2, dtype=np.float32), (128, 1)),
        "pcvals": pcvals.astype(BF),
        "onesr": np.ones((1, 128), np.float32),
    }
    in_maps = []
    for c in range(NCORES):
        m = dict(common)
        m["x"] = np.ascontiguousarray(x[c * TOK:(c + 1) * TOK])
        in_maps.append(m)
    return in_maps


def kernel(**inputs):
    in_maps = _host_inputs(inputs)
    nc = build_program()
    res = run_bass_kernel_spmd(nc, in_maps, core_ids=list(range(NCORES)))
    out = np.concatenate([np.asarray(res.results[c]["out"]) for c in range(NCORES)], axis=0)
    return out[None].astype(np.float32)


# version 07
# speedup vs baseline: 1.0647x; 1.0647x over previous
import numpy as np
import ml_dtypes
from contextlib import ExitStack

import concourse.bass as bass
import concourse.mybir as mybir
from concourse.bass_utils import run_bass_kernel_spmd
from concourse.bass import IndirectOffsetOnAxis
from concourse.aot_env import allow_ici_all_to_all

F32 = mybir.dt.float32
BF16 = mybir.dt.bfloat16
I32 = mybir.dt.int32
U16 = mybir.dt.uint16
AF = mybir.ActivationFunctionType
ALU = mybir.AluOpType
AXX = mybir.AxisListType.X
BF = ml_dtypes.bfloat16

BUILD_PHASES = 99
DBG_DUMPS = []

NCORES = 8
D = 2048
S = 8192
TOK = S // NCORES
NCH = TOK // 128
NDC = D // 128
NH = 16
HPC = NH // NCORES
FF = 8192
NE = 8
EFF = 512
C2 = 256
NG = 2 * NE
SCALE = 128.0 ** -0.5
SHIFT = 8.0
EPS = 1e-5
NQT = S // 512
KV_ELEMS = 2 * 128 * TOK
QB_ELEMS = 256 * TOK
CT = TOK + 2


class Sem:

    def __init__(self, ctx, nc, name):
        self.h = ctx.enter_context(nc.semaphore(name))
        self.v = 0


class Emitter:

    ENGINES = ("sync", "gpsimd", "tensor", "vector", "scalar")

    def __init__(self):
        self.ops = {e: [] for e in self.ENGINES}

    def add(self, eng, fn, waits=(), inc=None, incn=1):
        w = [(s.h, v) for (s, v) in waits]
        if inc is not None:
            inc.v += incn
            self.ops[eng].append((fn, w, (inc.h, incn)))
        else:
            self.ops[eng].append((fn, w, None))

    def addc(self, eng, fn, sem, waits=()):
        self.add(eng, fn, waits=list(waits) + [(sem, sem.v)], inc=sem)

    def dma(self, eng, out, in_, sem, waits=()):
        self.add(eng, lambda E, out=out, in_=in_: E.dma_start(out=out, in_=in_),
                 waits=waits, inc=sem, incn=16)

    def flush(self, block):
        for eng in self.ENGINES:
            lst = self.ops[eng]
            if not lst:
                continue

            def make(lst):
                def body(E):
                    for fn, waits, inc in lst:
                        for (h, v) in waits:
                            E.wait_ge(h, v)
                        r = fn(E)
                        if inc is not None:
                            if r is None:
                                r = E.engine_nop()
                            r.then_inc(inc[0], inc[1])
                return body
            getattr(block, eng)(make(lst))
        self.ops = {e: [] for e in self.ENGINES}


def build_program():
    nc = bass.Bass("TRN2", target_bir_lowering=False)
    dr = {}

    def DT(name, shape, dt, kind=None):
        dr[name] = (nc.dram_tensor(name, shape, dt) if kind is None
                    else nc.dram_tensor(name, shape, dt, kind=kind))
        return dr[name]

    DT("x", [TOK, D], F32, "ExternalInput")
    DT("wq", [D, D], F32, "ExternalInput")
    DT("wkv", [D, 256], F32, "ExternalInput")
    DT("wo", [D, D], F32, "ExternalInput")
    DT("wfc", [D, FF], F32, "ExternalInput")
    DT("wproj", [FF, D], F32, "ExternalInput")
    DT("routerT", [D, NE], F32, "ExternalInput")
    DT("w1", [NE, D, EFF], F32, "ExternalInput")
    DT("w2", [NE, EFF, D], F32, "ExternalInput")
    DT("w3", [NE, D, EFF], F32, "ExternalInput")
    DT("vecs", [192, 128], F32, "ExternalInput")
    DT("onesr", [1, 128], F32, "ExternalInput")
    DT("identf", [128, 128], F32, "ExternalInput")
    DT("identb", [128, 128], BF16, "ExternalInput")
    DT("onesb", [128, 128], BF16, "ExternalInput")
    DT("Ustrict", [128, 128], F32, "ExternalInput")
    DT("masks", [4, 128, 512], BF16, "ExternalInput")
    DT("iotaC", [128, C2], F32, "ExternalInput")
    DT("pcvals", [128, 2 * NCH], BF16, "ExternalInput")
    DT("out", [TOK, D], F32, "ExternalOutput")
    DT("kv_in", [KV_ELEMS], BF16)
    DT("kv_out", [NCORES * KV_ELEMS], BF16)
    DT("q_in", [NCORES * QB_ELEMS], BF16)
    DT("q_out", [NCORES * QB_ELEMS], BF16)
    DT("at_in", [NCORES * QB_ELEMS], BF16)
    DT("at_out", [NCORES * QB_ELEMS], BF16)
    DT("xn_rows", [TOK, D], BF16)
    DT("w1s", [NE, D, EFF], BF16)
    DT("w2s", [NE, EFF, D], BF16)
    DT("w3s", [NE, D, EFF], BF16)
    DT("moeA", [TOK + 1, D], F32)
    DT("moeB", [TOK + 1, D], F32)

    rg = [list(range(NCORES))]

    with ExitStack() as ctx:
        def SB(name, shape, dt, stack=None):
            t = (stack or ctx).enter_context(nc.sbuf_tensor("sb_" + name, shape, dt))
            TILES[name] = t
            return t
        TILES = {}

        identf = SB("identf", [128, 128], F32)
        identb = SB("identb", [128, 128], BF16)
        onesb = SB("onesb", [128, 128], BF16)
        Us = SB("Ustrict", [128, 128], F32)
        masks = SB("masks", [128, 4 * 512], BF16)
        iotaC = SB("iotaC", [128, C2], F32)
        pcv = SB("pcvals", [128, 2 * NCH], BF16)
        vrows = SB("vrows", [128, 128], F32)
        vrows2 = SB("vrows2", [64, 128], F32)
        onesr = SB("onesr", [1, 128], F32)
        vcols = SB("vcols", [128, 192], F32)
        VG1, VB1, VG2, VB2, VBQ, VBO, VBPJ, VBKV, VBFC = 0, 16, 32, 48, 64, 80, 96, 112, 128
        statc = SB("statc", [128, 64], F32)
        routc = SB("routc", [128, 48], F32)
        mx8 = SB("mx8", [128, 8], F32)
        idx8 = SB("idx8", [128, 8], U16)
        lists = SB("lists", [128, 6 * NG], F32)
        idxgI = SB("idxgI", [128, 2 * NG], I32)
        idxsI = SB("idxsI", [128, 2 * NG], I32)
        gatec = SB("gatec", [128, 2 * NG], F32)
        tpv = SB("tpv", [128, 2 * 3 * NCH], BF16)
        reps = SB("reps", [128, 4 * TOK], F32)
        statrow = SB("statrow", [1, TOK], F32)
        srow_t = SB("srow_t", [32, 128], F32)
        ucol = SB("ucol", [128, 4], F32)
        small = SB("small", [128, 64], F32)
        rec = SB("rec", [128, 512], F32)
        dbg1 = SB("dbg1", [128, 64], F32)
        dbg2 = SB("dbg2", [128, 64], F32)


        sW = Sem(ctx, nc, "sW")
        sX = Sem(ctx, nc, "sX")
        sT = Sem(ctx, nc, "sT")
        sV = Sem(ctx, nc, "sV")
        sC = Sem(ctx, nc, "sC")
        sG = Sem(ctx, nc, "sG")
        sO = Sem(ctx, nc, "sO")
        sCC = Sem(ctx, nc, "sCC")

        block = ctx.enter_context(nc.Block())

        pM0 = ctx.enter_context(nc.psum_tensor("pM0", [128, 512], F32))
        pM1 = ctx.enter_context(nc.psum_tensor("pM1", [128, 512], F32))
        pM2 = ctx.enter_context(nc.psum_tensor("pM2", [128, 512], F32))
        pM3 = ctx.enter_context(nc.psum_tensor("pM3", [128, 512], F32))
        pPh = ctx.enter_context(nc.psum_tensor("pPh", [128, 512], F32))
        pX0 = ctx.enter_context(nc.psum_tensor("pX0", [128, 512], F32))
        pX1 = ctx.enter_context(nc.psum_tensor("pX1", [128, 512], F32))
        pPh2 = ctx.enter_context(nc.psum_tensor("pPh2", [128, 512], F32))

        def pXb(k):
            return (pX0 if (k % 2 == 0) else pX1)[:, 0:64].bitcast(BF16)

        E = Emitter()

        def trunc_finalize():
            for i, name in enumerate(DBG_DUMPS):
                if name.startswith("dram:"):
                    _, dn, nrow = name.split(":")
                    nrow = int(nrow)
                    v = dr[dn].ap()
                    flat = 1
                    for s in v.shape:
                        flat *= s
                    v = v.rearrange(" ".join(f"a{i}" for i in range(len(v.shape))) + " -> (" +
                                    " ".join(f"a{i}" for i in range(len(v.shape))) + ")") if len(v.shape) > 1 else v
                    v = v.rearrange("(r c) -> r c", c=D)
                    E.dma("gpsimd", dr["out"].ap()[i * 128:i * 128 + nrow, :],
                          v[0:nrow, :], sO,
                          waits=[(sT, sT.v), (sV, sV.v), (sC, sC.v), (sX, sX.v),
                                 (sW, sW.v), (sG, sG.v), (sCC, sCC.v), (sO, sO.v)])
                    continue
                coff = 0
                if "@" in name:
                    name, coff = name.split("@")
                    coff = int(coff)
                t = TILES[name]
                sh = t[:].shape
                cols = min((sh[-1] if len(sh) > 1 else 1) - coff, D)
                rows = min(sh[0], 128)
                E.dma("gpsimd", dr["out"].ap()[i * 128:i * 128 + rows, 0:cols],
                      t[0:rows, coff:coff + cols] if len(sh) == 2 else t[0:rows], sO,
                      waits=[(sT, sT.v), (sV, sV.v), (sC, sC.v), (sX, sX.v),
                             (sW, sW.v), (sG, sG.v), (sCC, sCC.v)])
            E.add("gpsimd", lambda g: g.wait_ge(sO.h, sO.v))
            E.flush(block)
            for st in open_stacks:
                st.close()

        open_stacks = []

        def rep_sl(slot, h):
            return reps[:, slot * TOK + h * 512: slot * TOK + (h + 1) * 512]

        def tr_evac_pipe(items, dtype="bf", first_waits=()):
            tb, vb = sT.v, sV.v
            for k, (src, _) in enumerate(items):
                if dtype == "bf":
                    pb = pXb(k)[:, 0:128]
                    idn = identb
                else:
                    pb = (pX0 if (k % 2 == 0) else pX1)[:, 0:128]
                    idn = identf
                w = list(first_waits) if k == 0 else []
                if vb + k - 1 > 0:
                    w.append((sV, vb + k - 1))
                E.add("tensor", lambda t, src=src, pb=pb, idn=idn: t.transpose(pb, src, idn[:]),
                      waits=w, inc=sT)
            for k, (_, ev) in enumerate(items):
                if dtype == "bf":
                    pb = pXb(k)[:, 0:128]
                else:
                    pb = (pX0 if (k % 2 == 0) else pX1)[:, 0:128]
                E.add("vector", lambda v, ev=ev, pb=pb: ev(v, pb),
                      waits=[(sT, tb + k + 1)], inc=sV)

        for nm, t in [("identf", identf), ("identb", identb), ("onesb", onesb),
                      ("Ustrict", Us), ("iotaC", iotaC), ("pcvals", pcv)]:
            E.dma("sync", t[:], dr[nm].ap(), sW)
        E.dma("sync", masks[:].rearrange("p (a b) -> p a b", a=4),
              dr["masks"].ap().rearrange("a p b -> p a b"), sW)
        E.dma("sync", vrows[:], dr["vecs"].ap()[0:128, :], sW)
        E.dma("sync", vrows2[:], dr["vecs"].ap()[128:192, :], sW)
        E.dma("sync", onesr[:], dr["onesr"].ap(), sW)
        cW0 = (sW, sW.v)
        E.add("tensor", lambda t: t.transpose(pX0[:, 0:128], vrows[0:128, :], identf[:]),
              waits=[cW0], inc=sT)
        E.add("tensor", lambda t: t.transpose(pX1[:, 0:64], vrows2[:, :], identf[0:64, 0:64]),
              inc=sT)
        cT0 = (sT, sT.v)
        E.add("vector", lambda v: v.tensor_copy(vcols[:, 0:128], pX0[:, 0:128]), waits=[cT0])
        E.add("vector", lambda v: v.tensor_copy(vcols[:, 128:192], pX1[:, 0:64]), inc=sV)

        if BUILD_PHASES < 1:
            trunc_finalize()
            return nc
        esA = ExitStack()
        open_stacks.append(esA)
        esB = ExitStack()
        open_stacks.append(esB)
        xgT = SB("xgT", [128, NDC * CT], BF16, esB)
        sqdump = SB("sqdump", [128, D], BF16, esA)
        xn_tok = SB("xn_tok", [128, NCH * D], BF16, esA)
        xhat = SB("xhat", [128, NCH * D], BF16, esA)
        xnT_t = SB("xnT_t", [128, TOK], BF16, esA)
        rtw = SB("rtw", [128, NDC * NE], BF16, esA)
        rsc = SB("rsc", [NE, TOK], F32, esA)
        selt = SB("selt", [128, C2], BF16, esA)
        msk8 = SB("msk8", [128, 40], F32, esA)

        E.dma("gpsimd", xhat[:].rearrange("p (c d) -> p c d", c=NCH),
              dr["x"].ap().rearrange("(c p) d -> p c d", p=128), sX)
        E.dma("gpsimd", rtw[:].rearrange("p (c e) -> p c e", c=NDC),
              dr["routerT"].ap().rearrange("(c p) e -> p c e", p=128), sX)
        cX = (sX, sX.v)
        precast_W = []
        for e_ in range(NE):
            E.dma("gpsimd", dr["w1s"].ap()[e_], dr["w1"].ap()[e_], sW)
            E.dma("gpsimd", dr["w3s"].ap()[e_], dr["w3"].ap()[e_], sW)
            E.dma("gpsimd", dr["w2s"].ap()[e_], dr["w2"].ap()[e_], sW)
            precast_W.append(sW.v)
        for c in range(NCH):
            E.add("vector", lambda v, c=c: v.reduce_sum(
                statc[:, 48 + c:49 + c], xhat[:, c * D:(c + 1) * D], axis=AXX),
                waits=([cX] if c == 0 else []), inc=(sV if c == NCH - 1 else None))
        for c in range(NCH):
            E.add("scalar", lambda s, c=c: s.activation(
                sqdump[:], xhat[:, c * D:(c + 1) * D], AF.Square,
                accum_out=statc[:, 56 + c:57 + c]),
                waits=([cX] if c == 0 else []), inc=(sC if c == NCH - 1 else None))

        E.addc("vector", lambda v: v.tensor_scalar_mul(statc[:, 0:8], statc[:, 48:56], 1.0 / D),
               sV, waits=[(sC, sC.v)])
        E.addc("vector", lambda v: v.tensor_scalar_mul(statc[:, 48:56], statc[:, 56:64], 1.0 / D), sV)
        E.addc("vector", lambda v: v.tensor_tensor(statc[:, 56:64], statc[:, 0:8], statc[:, 0:8], ALU.mult), sV)
        E.addc("vector", lambda v: v.tensor_tensor(statc[:, 8:16], statc[:, 48:56], statc[:, 56:64], ALU.subtract), sV)
        E.addc("vector", lambda v: v.tensor_scalar_add(statc[:, 8:16], statc[:, 8:16], EPS), sV)
        E.add("scalar", lambda s: s.activation(statc[:, 48:56], statc[:, 8:16], AF.Sqrt),
              waits=[(sV, sV.v)], inc=sC)
        E.addc("vector", lambda v: v.reciprocal(statc[:, 16:24], statc[:, 48:56]),
               sV, waits=[(sC, sC.v)])
        E.addc("vector", lambda v: v.tensor_tensor(statc[:, 48:56], statc[:, 16:24], statc[:, 16:24], ALU.mult), sV)
        E.addc("vector", lambda v: v.tensor_scalar_add(statc[:, 56:64], statc[:, 8:16], -EPS), sV)
        E.addc("vector", lambda v: v.tensor_tensor(statc[:, 48:56], statc[:, 48:56], statc[:, 56:64], ALU.mult), sV)
        E.addc("vector", lambda v: v.tensor_scalar_add(statc[:, 48:56], statc[:, 48:56], EPS), sV)
        E.add("scalar", lambda s: s.activation(statc[:, 48:56], statc[:, 48:56], AF.Sqrt),
              waits=[(sV, sV.v)], inc=sC)
        E.addc("vector", lambda v: v.reciprocal(statc[:, 56:64], statc[:, 48:56]),
               sV, waits=[(sC, sC.v)])
        E.addc("vector", lambda v: v.tensor_tensor(statc[:, 24:32], statc[:, 16:24], statc[:, 56:64], ALU.mult), sV)
        E.addc("vector", lambda v: v.tensor_tensor(statc[:, 32:40], statc[:, 16:24], statc[:, 0:8], ALU.mult), sV)
        E.addc("vector", lambda v: v.tensor_scalar_mul(statc[:, 32:40], statc[:, 32:40], -1.0), sV)
        E.addc("vector", lambda v: v.tensor_tensor(statc[:, 40:48], statc[:, 24:32], statc[:, 0:8], ALU.mult), sV)
        E.addc("vector", lambda v: v.tensor_scalar_mul(statc[:, 40:48], statc[:, 40:48], -1.0), sV)
        for c in range(NCH):
            E.add("vector", lambda v, c=c: v.tensor_scalar(
                xn_tok[:, c * D:(c + 1) * D], xhat[:, c * D:(c + 1) * D],
                statc[:, 0 + c:1 + c], statc[:, 16 + c:17 + c], ALU.subtract, ALU.mult),
                inc=(sV if c == NCH - 1 else None))
        E.dma("sync", dr["xn_rows"].ap().rearrange("(p c) d -> p c d", c=NCH),
              xn_tok[:].rearrange("p (c d) -> p c d", c=NCH), sX, waits=[(sV, sV.v)])

        E.add("tensor", lambda t: t.transpose(pX0[0:32, 0:128], statc[:, 16:48], identf[:]),
              waits=[(sV, sV.v)], inc=sT)
        E.add("vector", lambda v: v.tensor_copy(srow_t[:, :], pX0[0:32, 0:128]),
              waits=[(sT, sT.v)], inc=sV)
        for slot, srcrow in [(0, 1), (1, 3), (2, 0), (3, 2)]:
            E.dma("sync", statrow[0:1, :], srow_t[8 * srcrow:8 * srcrow + 8, :], sX,
                  waits=[(sV, sV.v)] + ([(sT, sT.v)] if slot > 0 else []))
            cXr = (sX, sX.v)
            for h in range(2):
                pb = pX0 if h == 0 else pX1
                E.add("tensor", lambda t, pb=pb, h=h: t.matmul(
                    pb[:, 0:512], onesr[:, :],
                    statrow[0:1, h * 512:(h + 1) * 512], start=True, stop=True),
                    waits=([cXr] if h == 0 else []), inc=sT)
            cTb = (sT, sT.v)
            for h in range(2):
                pb = pX0 if h == 0 else pX1
                E.add("vector", lambda v, pb=pb, slot=slot, h=h: v.tensor_copy(
                    rep_sl(slot, h), pb[:, 0:512]),
                    waits=([cTb] if h == 0 else []), inc=(sV if h == 1 else None))

        for dc in range(NDC):
            items = []
            for c in range(NCH):
                items.append((xhat[:, c * D + dc * 128: c * D + (dc + 1) * 128],
                              lambda v, pb, dc=dc, c=c: v.tensor_copy(
                                  xgT[:, dc * CT + c * 128: dc * CT + (c + 1) * 128], pb)))
            fw = [(sV, sV.v)] if dc == 0 else []
            tr_evac_pipe(items, "bf", first_waits=fw)
            cVe = (sV, sV.v)
            for h in range(2):
                E.add("vector", lambda v, dc=dc, h=h: v.tensor_tensor(
                    xnT_t[:, h * 512:(h + 1) * 512],
                    xgT[:, dc * CT + h * 512: dc * CT + (h + 1) * 512], rep_sl(2, h), ALU.mult),
                    waits=([cVe, (sT, sT.v)] if h == 0 else []))
            for h in range(2):
                E.add("vector", lambda v, h=h: v.tensor_tensor(
                    xnT_t[:, h * 512:(h + 1) * 512], xnT_t[:, h * 512:(h + 1) * 512],
                    rep_sl(3, h), ALU.add), inc=(sV if h == 1 else None))
            cVx = (sV, sV.v)
            for h in range(2):
                pb = pM0 if h == 0 else pM1
                E.add("tensor", lambda t, pb=pb, dc=dc, h=h: t.matmul(
                    pb[0:NE, :], rtw[:, dc * NE:(dc + 1) * NE], xnT_t[:, h * 512:(h + 1) * 512],
                    start=(dc == 0), stop=(dc == NDC - 1)),
                    waits=([cVx] if h == 0 else []), inc=(sT if h == 1 else None))
            E.add("vector", lambda v, dc=dc: v.tensor_scalar_mul(
                xgT[:, dc * CT: dc * CT + TOK], xgT[:, dc * CT: dc * CT + TOK],
                vcols[:, VG1 + dc:VG1 + dc + 1]))
            E.add("vector", lambda v, dc=dc: v.tensor_copy(
                xgT[:, dc * CT + TOK: dc * CT + TOK + 1], vcols[:, VG1 + dc:VG1 + dc + 1]))
            E.add("vector", lambda v, dc=dc: v.tensor_copy(
                xgT[:, dc * CT + TOK + 1: dc * CT + TOK + 2], vcols[:, VB1 + dc:VB1 + dc + 1]),
                inc=sV)

        cRT = (sT, sT.v)
        for h in range(2):
            pb = pM0 if h == 0 else pM1
            E.add("vector", lambda v, pb=pb, h=h: v.tensor_copy(
                rsc[:, h * 512:(h + 1) * 512], pb[0:NE, :]),
                waits=([cRT] if h == 0 else []), inc=(sV if h == 1 else None))
        items = []
        for c in range(NCH):
            items.append((rsc[:, c * 128:(c + 1) * 128],
                          lambda v, pb, c=c: v.tensor_copy(small[:, 8:16], pb[:, 0:8])))
        tbase = sT.v
        for c in range(NCH):
            pb = (pX0 if (c % 2 == 0) else pX1)[:, 0:8]
            E.add("tensor", lambda t, c=c, pb=pb: t.transpose(
                pb, rsc[:, c * 128:(c + 1) * 128], identf[0:8, 0:8]),
                waits=([(sV, sV.v)] if c == 0 else [(sV, sV.v)]), inc=sT)
            E.addc("vector", lambda v, pb=pb: v.tensor_copy(small[:, 8:16], pb),
                   sV, waits=[(sT, sT.v)])
            E.addc("vector", lambda v: v.max(out=mx8[:], in_=small[:, 8:16]), sV)
            E.addc("vector", lambda v: v.max_index(idx8[:], mx8[:], small[:, 8:16]), sV)
            E.addc("vector", lambda v, c=c: v.tensor_copy(routc[:, 32 + c:33 + c], mx8[:, 0:1]), sV)
            E.addc("vector", lambda v, c=c: v.tensor_copy(routc[:, 40 + c:41 + c], mx8[:, 1:2]), sV)
            E.addc("vector", lambda v, c=c: v.tensor_copy(routc[:, 0 + c:1 + c], idx8[:, 0:1]), sV)
            E.addc("vector", lambda v, c=c: v.tensor_copy(routc[:, 8 + c:9 + c], idx8[:, 1:2]), sV)
        E.addc("vector", lambda v: v.tensor_tensor(routc[:, 40:48], routc[:, 40:48], routc[:, 32:40], ALU.subtract), sV)
        E.add("scalar", lambda s: s.activation(routc[:, 32:40], routc[:, 40:48], AF.Exp),
              waits=[(sV, sV.v)], inc=sC)
        E.addc("vector", lambda v: v.tensor_scalar_add(routc[:, 40:48], routc[:, 32:40], 1.0),
               sV, waits=[(sC, sC.v)])
        E.addc("vector", lambda v: v.reciprocal(routc[:, 16:24], routc[:, 40:48]), sV)
        E.addc("vector", lambda v: v.tensor_scalar(routc[:, 24:32], routc[:, 16:24], -1.0, 1.0, ALU.mult, ALU.add), sV)
        for s_ in range(2):
            E.addc("vector", lambda v, s_=s_: v.tensor_copy(
                tpv[:, s_ * 3 * NCH + 0: (s_ + 1) * 3 * NCH: 3], pcv[:, 0:2 * NCH:2]), sV)
            E.addc("vector", lambda v, s_=s_: v.tensor_copy(
                tpv[:, s_ * 3 * NCH + 1: (s_ + 1) * 3 * NCH: 3], pcv[:, 1:2 * NCH:2]), sV)
            E.addc("vector", lambda v, s_=s_: v.tensor_copy(
                tpv[:, s_ * 3 * NCH + 2: (s_ + 1) * 3 * NCH: 3], routc[:, 16 + 8 * s_:24 + 8 * s_]), sV)

        for g_ in range(NG):
            s_, e_ = g_ // NE, g_ % NE
            icol = routc[:, 8 * s_: 8 * s_ + 8]
            E.addc("vector", lambda v, icol=icol, e_=e_: v.tensor_single_scalar(
                msk8[:, 0:8], icol, float(e_), ALU.is_equal),
                sV, waits=[(sT, sT.v)])
            E.addc("vector", lambda v: v.tensor_tensor_scan(
                msk8[:, 8:16], msk8[:, 0:8], msk8[:, 0:8], 0.0, ALU.add, ALU.bypass), sV)
            E.addc("vector", lambda v: v.tensor_tensor(
                msk8[:, 16:24], msk8[:, 8:16], msk8[:, 0:8], ALU.subtract), sV)
            E.add("tensor", lambda t: t.matmul(pPh[:, 0:1], Us[:], msk8[:, 15:16],
                                               start=True, stop=True),
                  waits=[(sV, sV.v)], inc=sT)
            E.addc("vector", lambda v: v.tensor_copy(msk8[:, 24:25], pPh[:, 0:1]),
                   sV, waits=[(sT, sT.v)])
            E.addc("vector", lambda v: v.tensor_scalar_add(msk8[:, 16:24], msk8[:, 16:24], msk8[:, 24:25]), sV)
            E.addc("vector", lambda v: v.tensor_single_scalar(msk8[:, 32:40], msk8[:, 0:8], 0.0, ALU.is_equal), sV)
            E.addc("vector", lambda v: v.scalar_tensor_tensor(
                msk8[:, 16:24], msk8[:, 32:40], 4096.0, msk8[:, 16:24], ALU.mult, ALU.add), sV)
            pg = pM2[:, 8 * (g_ % 2): 8 * (g_ % 2) + 6]
            for c in range(NCH):
                E.addc("vector", lambda v, c=c: v.tensor_single_scalar(
                    selt[:], iotaC[:], msk8[:, 16 + c:17 + c], ALU.is_equal),
                    sV, waits=([] if c == 0 else [(sT, sT.v)]))
                for jb in range(2):
                    E.add("tensor", lambda t, c=c, jb=jb, s_=s_, pg=pg: t.matmul(
                        pg[:, 3 * jb: 3 * (jb + 1)], selt[:, jb * 128:(jb + 1) * 128],
                        tpv[:, s_ * 3 * NCH + 3 * c: s_ * 3 * NCH + 3 * (c + 1)],
                        start=(c == 0 and jb == 0), stop=(c == NCH - 1 and jb == 1),
                        skip_group_check=True),
                        waits=([(sV, sV.v)] if jb == 0 else []),
                        inc=(sT if jb == 1 else None))
            E.addc("vector", lambda v, g_=g_, pg=pg: v.tensor_copy(
                lists[:, 6 * g_: 6 * (g_ + 1)], pg),
                sV, waits=[(sT, sT.v)])
        pp1 = lists[:, 0:6 * NG:3]
        cc = lists[:, 1:6 * NG:3]
        gg = lists[:, 2:6 * NG:3]
        E.addc("vector", lambda v: v.scalar_tensor_tensor(small[:, 0:32], pp1, 8.0, cc, ALU.mult, ALU.add), sV)
        E.addc("vector", lambda v: v.tensor_scalar_add(small[:, 0:32], small[:, 0:32], -8.0), sV)
        E.addc("vector", lambda v: v.tensor_scalar_max(small[:, 32:64], small[:, 0:32], 0.0), sV)
        E.addc("vector", lambda v: v.tensor_copy(idxgI[:], small[:, 32:64]), sV)
        E.addc("vector", lambda v: v.tensor_single_scalar(small[:, 32:64], small[:, 0:32], 0.0, ALU.is_lt), sV)
        E.addc("vector", lambda v: v.scalar_tensor_tensor(small[:, 0:32], small[:, 32:64],
                                                          float(TOK + 8), small[:, 0:32], ALU.mult, ALU.add), sV)
        E.addc("vector", lambda v: v.tensor_copy(idxsI[:], small[:, 0:32]), sV)
        E.addc("vector", lambda v: v.tensor_copy(gatec[:], gg), sV)

        if BUILD_PHASES < 2:
            trunc_finalize()
            return nc
        esA.close()
        wqb = [SB("wqb0", [128, NDC * 128], BF16, esB), SB("wqb1", [128, NDC * 128], BF16, esB)]
        qout = SB("qout", [128, TOK], BF16, esB)
        vtr = [SB("vtr0", [128, 128], BF16, esB), SB("vtr1", [128, 128], BF16, esB)]
        NOB = NDC + 2

        def wq_src(o):
            if o < NDC:
                return dr["wq"].ap().rearrange("(dc p) (ob oj) -> ob p dc oj", p=128, oj=128)[o]
            return dr["wkv"].ap().rearrange("(dc p) (ob oj) -> ob p dc oj", p=128, oj=128)[o - NDC]

        qv_in = dr["q_in"].ap().rearrange("(j r t) -> j r t", j=NCORES, r=256)
        kv_inK = dr["kv_in"].ap().rearrange("(a b c) -> a b c", a=2, b=128)[0]
        kv_inV = dr["kv_in"].ap().rearrange("(a b c) -> a b c", a=2, b=1024)[1]

        E.dma("gpsimd", wqb[0][:].rearrange("p (dc oj) -> p dc oj", dc=NDC), wq_src(0), sW,
              waits=[(sV, sV.v)])
        blk_done_T = []
        epi_done_V = []
        store_O = []
        for o in range(NOB):
            wb = wqb[o % 2]
            if o + 1 < NOB:
                w = [(sT, blk_done_T[o - 1])] if o >= 1 else []
                E.dma("gpsimd", wqb[(o + 1) % 2][:].rearrange("p (dc oj) -> p dc oj", dc=NDC),
                      wq_src(o + 1), sW, waits=w)
            pmain = (pM0, pM1) if (o % 2 == 0) else (pM2, pM3)
            phsl = (pPh if (o % 2 == 0) else pPh2)[:, 0:2]
            for dc in range(NDC):
                wts = []
                if dc == 0:
                    wts.append((sW, sW.v - (16 if o + 1 < NOB else 0)))
                    if o >= 2:
                        wts.append((sV, epi_done_V[o - 2]))
                for h in range(2):
                    E.add("tensor", lambda t, wb=wb, dc=dc, h=h, pmain=pmain: t.matmul(
                        pmain[h][:], wb[:, dc * 128:(dc + 1) * 128],
                        xgT[:, dc * CT + h * 512: dc * CT + (h + 1) * 512],
                        start=(dc == 0), stop=(dc == NDC - 1)),
                        waits=(wts if h == 0 else []))
                E.add("tensor", lambda t, wb=wb, dc=dc, phsl=phsl: t.matmul(
                    phsl, wb[:, dc * 128:(dc + 1) * 128],
                    xgT[:, dc * CT + TOK: dc * CT + TOK + 2],
                    start=(dc == 0), stop=(dc == NDC - 1), skip_group_check=True),
                    inc=(sT if dc == NDC - 1 else None))
            blk_done_T.append(sT.v)
            bias = (vcols[:, VBQ + o: VBQ + o + 1] if o < NDC
                    else vcols[:, VBKV + (o - NDC): VBKV + (o - NDC) + 1])
            wq_epi_w = [(sT, sT.v)]
            if o >= 1:
                wq_epi_w.append((sO, store_O[o - 1]))
            E.addc("vector", lambda v, phsl=phsl: v.tensor_copy(ucol[:, 0:2], phsl),
                   sV, waits=wq_epi_w)
            E.addc("vector", lambda v, bias=bias: v.tensor_tensor(ucol[:, 2:3], ucol[:, 1:2], bias, ALU.add), sV)
            for h in range(2):
                sl = slice(h * 512, (h + 1) * 512)
                E.add("vector", lambda v, sl=sl, h=h, pmain=pmain: v.tensor_tensor(
                    qout[:, sl], pmain[h][:], rep_sl(0, h), ALU.mult))
                E.add("vector", lambda v, sl=sl, h=h: v.scalar_tensor_tensor(
                    qout[:, sl], rep_sl(1, h), ucol[:, 0:1], qout[:, sl], ALU.mult, ALU.add))
                E.add("vector", lambda v, sl=sl: v.tensor_scalar_add(qout[:, sl], qout[:, sl], ucol[:, 2:3]),
                      inc=(sV if h == 1 else None))
            epi_done_V.append(sV.v)
            if o < NDC:
                E.dma("sync", qv_in[o // 2, (o % 2) * 128:(o % 2) * 128 + 128, :], qout[:], sO,
                      waits=[(sV, sV.v)])
                store_O.append(sO.v)
            elif o == NDC:
                E.dma("sync", kv_inK, qout[:], sO, waits=[(sV, sV.v)])
                store_O.append(sO.v)
            else:
                tb0, vb0, ob0 = sT.v, sV.v, sO.v
                for c in range(NCH):
                    vb = vtr[c % 2]
                    w = [(sV, sV.v)] if c == 0 else []
                    pb = pXb(c)[:, 0:128]
                    if c >= 2:
                        w.append((sV, vb0 + c - 1))
                    E.add("tensor", lambda t, c=c, pb=pb: t.transpose(
                        pb, qout[:, c * 128:(c + 1) * 128], identb[:]), waits=w, inc=sT)
                    w2_ = [(sT, sT.v)]
                    if c >= 2:
                        w2_.append((sO, ob0 + 16 * (c - 1)))
                    E.add("vector", lambda v, vb=vb, pb=pb: v.tensor_copy(vb[:], pb),
                          waits=w2_, inc=sV)
                    E.dma("sync", kv_inV[c * 128:(c + 1) * 128, :], vb[:], sO, waits=[(sV, sV.v)])
                store_O.append(sO.v)

        E.add("gpsimd", lambda g: g.collective_compute(
            "AllGather", ALU.bypass, replica_groups=rg,
            ins=[dr["kv_in"].ap().opt()], outs=[dr["kv_out"].ap().opt()]),
            waits=[(sO, sO.v)], inc=sCC)

        def a2a_q(g):
            with allow_ici_all_to_all():
                return g.collective_compute("AllToAll", ALU.bypass, replica_groups=rg,
                                            ins=[dr["q_in"].ap().opt()], outs=[dr["q_out"].ap().opt()])
        E.add("gpsimd", a2a_q, inc=sCC)
        cc_after_kvq = sCC.v

        if BUILD_PHASES < 3:
            trunc_finalize()
            return nc
        esB.close()
        esC = ExitStack()
        open_stacks.append(esC)
        gx = [SB("gx0", [128, D], BF16, esC), SB("gx1", [128, D], BF16, esC)]
        xeT = SB("xeT", [128, NDC * 512], BF16, esC)
        w13 = [SB("w13a", [128, NDC * 512], BF16, esC), SB("w13b", [128, NDC * 512], BF16, esC)]
        w2b = [SB("w2ba", [128, 4 * D], BF16, esC), SB("w2bb", [128, 4 * D], BF16, esC)]
        y1t = SB("y1t", [128, 4 * 512], BF16, esC)
        y3t = SB("y3t", [128, 4 * 512], BF16, esC)
        wout = [SB("wout0", [128, D], F32, esC), SB("wout1", [128, D], F32, esC)]
        xnr = dr["xn_rows"].ap()

        gath_T = sT.v
        E.dma("sync", w13[0][:].rearrange("p (dc f) -> p dc f", dc=NDC),
              dr["w1s"].ap()[0].rearrange("(dc p) f -> p dc f", p=128), sW,
              waits=[(sW, precast_W[0]), (sV, sV.v), (sO, sO.v), (sT, sT.v)])
        E.dma("sync", w13[1][:].rearrange("p (dc f) -> p dc f", dc=NDC),
              dr["w3s"].ap()[0].rearrange("(dc p) f -> p dc f", p=128), sW)
        E.dma("sync", w2b[0][:].rearrange("p (fc d) -> p fc d", fc=4),
              dr["w2s"].ap()[0].rearrange("(fc p) d -> p fc d", p=128), sW)
        w13_ld = [sW.v - 32, sW.v]
        w2_ld = [sW.v, 0]
        w13_mm_T = [0, 0]
        w2_mm_T = [0, 0]
        wout_free = [0, 0]
        for e_ in range(NE):
            cols = [2 * e_, 2 * e_ + 1, 2 * (NE + e_), 2 * (NE + e_) + 1]
            for ci, k in enumerate(cols):
                gbuf = gx[ci % 2]
                w = ([(sX, sX.v), (sV, sV.v), (sO, sO.v), (sW, precast_W[-1])]
                     if (e_ == 0 and ci == 0) else [])
                w.append((sW, sW.v))
                if e_ > 0 or ci >= 2:
                    w.append((sT, gath_T))
                E.add("gpsimd", lambda g, gbuf=gbuf, k=k: g.indirect_dma_start(
                    gbuf[:], None, xnr, IndirectOffsetOnAxis(idxgI[:, k:k + 1], 0)),
                    waits=w, inc=sG, incn=16)
                tb0, vb0 = sT.v, sV.v
                for dc in range(NDC):
                    pb = pXb(dc)[:, 0:128]
                    w = [(sG, sG.v)] if dc == 0 else []
                    if vb0 + dc - 1 > 0:
                        w.append((sV, vb0 + dc - 1))
                    E.add("tensor", lambda t, gbuf=gbuf, dc=dc, pb=pb: t.transpose(
                        pb, gbuf[:, dc * 128:(dc + 1) * 128], identb[:]), waits=w, inc=sT)
                    E.add("vector", lambda v, dc=dc, ci=ci, pb=pb: v.tensor_copy(
                        xeT[:, dc * 512 + ci * 128: dc * 512 + (ci + 1) * 128], pb),
                        waits=[(sT, tb0 + dc + 1)], inc=sV)
                gath_T = sT.v
            for wi, ydst in [(0, y1t), (1, y3t)]:
                for ffo in range(4):
                    wts = []
                    if ffo == 0:
                        wts = [(sW, w13_ld[wi]), (sC, sC.v)] + ([(sV, sV.v)] if wi == 0 else [])
                    if ffo >= 2:
                        wts.append((sC, sC.v - 1))
                    pb = pM0 if (ffo % 2 == 0) else pM1
                    for dc in range(NDC):
                        E.add("tensor", lambda t, dc=dc, ffo=ffo, pb=pb, wbuf=w13[wi]: t.matmul(
                            pb[:], wbuf[:, dc * 512 + ffo * 128: dc * 512 + (ffo + 1) * 128],
                            xeT[:, dc * 512:(dc + 1) * 512],
                            start=(dc == 0), stop=(dc == NDC - 1)),
                            waits=(wts if dc == 0 else []),
                            inc=(sT if dc == NDC - 1 else None))
                    fn = AF.Silu if wi == 0 else AF.Copy
                    E.add("scalar", lambda s, ydst=ydst, ffo=ffo, pb=pb, fn=fn: s.activation(
                        ydst[:, ffo * 512:(ffo + 1) * 512], pb[:], fn),
                        waits=[(sT, sT.v)], inc=sC)
                w13_mm_T[wi] = sT.v
            E.add("vector", lambda v: v.tensor_tensor(y1t[:], y1t[:], y3t[:], ALU.mult),
                  waits=[(sC, sC.v)], inc=sV)
            if e_ + 1 < NE:
                E.dma("sync", w13[0][:].rearrange("p (dc f) -> p dc f", dc=NDC),
                      dr["w1s"].ap()[e_ + 1].rearrange("(dc p) f -> p dc f", p=128), sW,
                      waits=[(sW, precast_W[e_ + 1]), (sT, w13_mm_T[0])])
                E.dma("sync", w13[1][:].rearrange("p (dc f) -> p dc f", dc=NDC),
                      dr["w3s"].ap()[e_ + 1].rearrange("(dc p) f -> p dc f", p=128), sW,
                      waits=[(sT, w13_mm_T[1])])
                w13_ld = [sW.v - 16, sW.v]
                E.dma("sync", w2b[(e_ + 1) % 2][:].rearrange("p (fc d) -> p fc d", fc=4),
                      dr["w2s"].ap()[e_ + 1].rearrange("(fc p) d -> p fc d", p=128), sW,
                      waits=[(sT, w2_mm_T[(e_ + 1) % 2])])
                w2_ld[(e_ + 1) % 2] = sW.v
            scat_G = []
            w2buf = w2b[e_ % 2]
            for tc in range(4):
                wo_ = wout[tc % 2]
                k = cols[tc]
                tb0, vb0 = sT.v, sV.v
                for do in range(4):
                    pb = pM2 if (do % 2 == 0) else pM3
                    wts = []
                    if do == 0 and tc == 0:
                        wts = [(sW, w2_ld[e_ % 2]), (sV, sV.v)]
                    if vb0 + do - 1 > 0:
                        wts.append((sV, vb0 + do - 1))
                    for fc in range(4):
                        E.add("tensor", lambda t, fc=fc, do=do, tc=tc, pb=pb, w2buf=w2buf: t.matmul(
                            pb[:], y1t[:, fc * 512 + tc * 128: fc * 512 + (tc + 1) * 128],
                            w2buf[:, fc * D + do * 512: fc * D + (do + 1) * 512],
                            start=(fc == 0), stop=(fc == 3)),
                            waits=(wts if fc == 0 else []),
                            inc=(sT if fc == 3 else None))
                for do in range(4):
                    pb = pM2 if (do % 2 == 0) else pM3
                    w = [(sT, tb0 + do + 1)]
                    if do == 0 and wout_free[tc % 2] > 0:
                        w.append((sG, wout_free[tc % 2]))
                    E.add("vector", lambda v, do=do, pb=pb, wo_=wo_, k=k: v.tensor_scalar_mul(
                        wo_[:, do * 512:(do + 1) * 512], pb[:], gatec[:, k:k + 1]),
                        waits=w, inc=sV)
                dst = dr["moeA"] if tc < 2 else dr["moeB"]
                E.add("gpsimd", lambda g, dst=dst, k=k, wo_=wo_: g.indirect_dma_start(
                    dst.ap(), IndirectOffsetOnAxis(idxsI[:, k:k + 1], 0), wo_[:], None),
                    waits=[(sV, sV.v), (sW, sW.v), (sG, sG.v)], inc=sG, incn=16)
                scat_G.append(sG.v)
                wout_free[tc % 2] = sG.v
            w2_mm_T[e_ % 2] = sT.v
        moe_G = sG.v

        if BUILD_PHASES < 4:
            trunc_finalize()
            return nc
        esC.close()
        es4 = ExitStack()
        open_stacks.append(es4)
        kT = SB("kT", [128, S], BF16, es4)
        vT = SB("vT", [128, 64 * 128], BF16, es4)
        qtl = [SB("qt0", [128, 512], BF16, es4), SB("qt1", [128, 512], BF16, es4)]
        prb = [SB("prb0", [128, 512], BF16, es4), SB("prb1", [128, 512], BF16, es4)]
        atile = SB("atile", [128, 512], BF16, es4)

        E.dma("sync", kT[:].rearrange("p (r t) -> p r t", r=NCORES),
              dr["kv_out"].ap().rearrange("(r a p t) -> p r a t", r=NCORES, a=2, p=128)[:, :, 0, :],
              sX, waits=[(sCC, cc_after_kvq), (sT, sT.v), (sV, sV.v), (sC, sC.v),
                         (sG, sG.v), (sO, sO.v)])
        kvoV = dr["kv_out"].ap().rearrange("(r a c p j) -> r a c p j", r=NCORES, a=2,
                                           c=NCH, p=128)[:, 1].rearrange("r c p j -> p r c j")
        for r_ in range(NCORES):
            E.dma("sync", vT[:].rearrange("p (r c j) -> p r c j", r=NCORES, c=NCH)[:, r_],
                  kvoV[:, r_], sX)
        qov = dr["q_out"].ap().rearrange("(j r t) -> j r t", j=NCORES, r=256)
        at_inv = dr["at_in"].ap().rearrange("(j r t) -> j r t", j=NCORES, r=256)

        kv_ready = (sX, sX.v)
        prev_qload = {0: None, 1: None}
        prev_store_O = None
        for hh in range(HPC):
            for qt in range(NQT):
                NK = 4 * qt + 4
                qb = qtl[qt % 2]
                w = [kv_ready] if (hh == 0 and qt == 0) else []
                if prev_qload[qt % 2] is not None:
                    w.append((sT, prev_qload[qt % 2]))
                E.dma("sync", qb[:], qov[qt // 2, hh * 128:(hh + 1) * 128,
                                         (qt % 2) * 512:((qt % 2) + 1) * 512], sX, waits=w)
                qload = (sX, sX.v)
                tb, cb, vb = sT.v, sC.v, sV.v
                sval = {}
                oval = {}
                cnt = tb
                order = []
                for k in range(NK + 1):
                    if k < NK:
                        cnt += 1; sval[k] = cnt; order.append(("S", k))
                    if k >= 1:
                        cnt += 1; oval[k - 1] = cnt; order.append(("O", k - 1))
                for tag, k in order:
                    if tag == "S":
                        pS = pM0 if (k % 2 == 0) else pM1
                        w = []
                        if k == 0:
                            w.append(qload)
                            if prev_store_O is not None:
                                w.append((sO, prev_store_O))
                        if k >= 2:
                            w.append((sC, cb + k - 1))
                        E.add("tensor", lambda t, k=k, pS=pS, qb=qb: t.matmul(
                            pS[:], kT[:, k * 128:(k + 1) * 128], qb[:], start=True, stop=True),
                            waits=w, inc=sT)
                    else:
                        pr = prb[k % 2]
                        w = []
                        if k >= 4 * qt:
                            w.append((sV, vb + (k - 4 * qt) + 1))
                        else:
                            w.append((sC, cb + k + 1))
                        E.add("tensor", lambda t, k=k, pr=pr, NK=NK: t.matmul(
                            pM2[:], onesb[:], pr[:], start=(k == 0), stop=(k == NK - 1)),
                            waits=w)
                        E.add("tensor", lambda t, k=k, pr=pr, NK=NK: t.matmul(
                            pM3[:], vT[:, k * 128:(k + 1) * 128], pr[:],
                            start=(k == 0), stop=(k == NK - 1)), inc=sT)
                for k in range(NK):
                    pS = pM0 if (k % 2 == 0) else pM1
                    pr = prb[k % 2]
                    w = [(sT, sval[k])]
                    if k >= 2:
                        w.append((sT, oval[k - 2]))
                    else:
                        w.append((sT, tb))
                    E.add("scalar", lambda s, pS=pS, pr=pr: s.activation(
                        pr[:], pS[:], AF.Exp, bias=vcols[:, 114:115], scale=SCALE), waits=w, inc=sC)
                for r_ in range(4):
                    k = 4 * qt + r_
                    pr = prb[k % 2]
                    E.add("vector", lambda v, pr=pr, r_=r_: v.tensor_tensor(
                        pr[:], pr[:], masks[:, r_ * 512:(r_ + 1) * 512], ALU.mult),
                        waits=[(sC, cb + k + 1)] + ([(sT, oval[k - 2])] if k >= 2 else []),
                        inc=sV)
                E.add("vector", lambda v: v.reciprocal(rec[:], pM2[:]),
                      waits=[(sT, sT.v)])
                E.add("vector", lambda v: v.tensor_tensor(atile[:], pM3[:], rec[:], ALU.mult),
                      inc=sV)
                E.dma("sync", at_inv[qt // 2, hh * 128:(hh + 1) * 128,
                                     (qt % 2) * 512:((qt % 2) + 1) * 512], atile[:], sO,
                      waits=[(sV, sV.v)])
                prev_qload[qt % 2] = sT.v
                prev_store_O = sO.v

        def a2a_at(g):
            with allow_ici_all_to_all():
                return g.collective_compute("AllToAll", ALU.bypass, replica_groups=rg,
                                            ins=[dr["at_in"].ap().opt()], outs=[dr["at_out"].ap().opt()])
        E.add("gpsimd", a2a_at, waits=[(sO, sO.v)], inc=sCC)
        cc_after_at = sCC.v

        if BUILD_PHASES < 5:
            trunc_finalize()
            return nc
        es4.close()
        es2 = ExitStack()
        open_stacks.append(es2)
        A = SB("A", [128, NCH * D], F32)
        CT2 = TOK + 2
        x2b = SB("x2b", [128, NDC * CT2], BF16)
        xn_o = [SB("xn_o0", [128, NCH * 128], BF16, es2), SB("xn_o1", [128, NCH * 128], BF16, es2)]
        atn = SB("atn", [128, NH * TOK], BF16, es2)
        xnTt = SB("xnTt", [128, TOK], BF16, es2)
        x2f = SB("x2f", [128, TOK], F32, es2)
        wob = [SB("wob0", [128, NDC * 128], BF16, es2), SB("wob1", [128, NDC * 128], BF16, es2)]

        E.dma("sync", atn[:].rearrange("p (h t) -> p h t", h=NH),
              dr["at_out"].ap().rearrange("(j k p t) -> p (j k) t", j=NCORES, k=HPC, p=128),
              sX, waits=[(sCC, cc_after_at), (sT, sT.v), (sV, sV.v), (sO, sO.v)])
        E.dma("sync", A[:].rearrange("p (c d) -> p c d", c=NCH),
              dr["x"].ap().rearrange("(c p) d -> p c d", p=128), sX)
        atn_ready = (sX, sX.v)
        xnp_T = []

        wo_src = dr["wo"].ap().rearrange("(dc p) (ob oj) -> ob p dc oj", p=128, oj=128)
        E.dma("gpsimd", wob[0][:].rearrange("p (dc oj) -> p dc oj", dc=NDC), wo_src[0], sW)
        blk_T = []
        epi_V = []
        for o in range(NDC):
            wb = wob[o % 2]
            if o + 1 < NDC:
                w = [(sT, blk_T[o - 1])] if o >= 1 else []
                E.dma("gpsimd", wob[(o + 1) % 2][:].rearrange("p (dc oj) -> p dc oj", dc=NDC),
                      wo_src[o + 1], sW, waits=w)
            pmo = (pM0, pM1) if (o % 2 == 0) else (pM2, pM3)
            for hc in range(NH):
                wts = []
                if hc == 0:
                    wts = [(sW, sW.v - (16 if o + 1 < NDC else 0)), atn_ready]
                    if o >= 1:
                        wts.append((sV, epi_V[o - 1]))
                for h in range(2):
                    E.add("tensor", lambda t, wb=wb, hc=hc, h=h, pmo=pmo: t.matmul(
                        pmo[h][:], wb[:, hc * 128:(hc + 1) * 128],
                        atn[:, hc * TOK + h * 512: hc * TOK + (h + 1) * 512],
                        start=(hc == 0), stop=(hc == NH - 1)),
                        waits=(wts if h == 0 else []),
                        inc=(sT if (hc == NH - 1 and h == 1) else None))
            blk_T.append(sT.v)
            xno = xn_o[o % 2]
            wxn = [(sT, xnp_T[o - 2])] if o >= 2 else []
            E.dma("sync", xno[:].rearrange("p (c d) -> p c d", c=NCH),
                  dr["xn_rows"].ap().rearrange("(p c) d -> p c d", c=NCH)[:, :, o * 128:(o + 1) * 128],
                  sX, waits=wxn)
            items = [(xno[:, c * 128:(c + 1) * 128],
                      lambda v, pb, c=c: v.tensor_copy(xnTt[:, c * 128:(c + 1) * 128], pb))
                     for c in range(NCH)]
            tr_evac_pipe(items, "bf", first_waits=[(sV, sV.v), (sX, sX.v)])
            xnp_T.append(sT.v)
            mm_done = (sT, blk_T[-1])
            for h in range(2):
                sl = slice(h * 512, (h + 1) * 512)
                pm = pmo[h]
                E.add("vector", lambda v, sl=sl, pm=pm: v.tensor_tensor(
                    x2f[:, sl], pm[:], xnTt[:, sl], ALU.add),
                    waits=([mm_done, (sV, sV.v)] if h == 0 else []))
                E.add("vector", lambda v, sl=sl, o=o: v.tensor_scalar_add(
                    x2f[:, sl], x2f[:, sl], vcols[:, VBO + o:VBO + o + 1]))
            E.add("vector", lambda v, o=o: v.tensor_copy(x2b[:, o * CT2: o * CT2 + TOK], x2f[:]))
            E.add("vector", lambda v, o=o: v.tensor_copy(
                x2b[:, o * CT2 + TOK: o * CT2 + TOK + 1], vcols[:, VG2 + o:VG2 + o + 1]))
            E.add("vector", lambda v, o=o: v.tensor_copy(
                x2b[:, o * CT2 + TOK + 1: o * CT2 + TOK + 2], vcols[:, VB2 + o:VB2 + o + 1]),
                inc=sV)
            tbA, vbA = sT.v, sV.v
            for c in range(NCH):
                pb = (pX0 if (c % 2 == 0) else pX1)[:, 0:128]
                w = [(sV, sV.v)] if c == 0 else []
                if c >= 2:
                    w.append((sV, vbA + c - 1))
                E.add("tensor", lambda t, c=c, o=o, pb=pb: t.transpose(
                    pb, x2f[:, c * 128:(c + 1) * 128], identf[:]), waits=w, inc=sT)
                asl = A[:, c * D + o * 128: c * D + (o + 1) * 128]
                E.add("vector", lambda v, asl=asl, pb=pb: v.tensor_tensor(asl, asl, pb, ALU.add),
                      waits=[(sT, tbA + c + 1)], inc=sV)
            epi_V.append(sV.v)

        sq2 = SB("sq2", [128, TOK], BF16, es2)
        for o in range(NDC):
            for h in range(2):
                E.add("tensor", lambda t, o=o, h=h: t.matmul(
                    (pM2 if h == 0 else pM3)[:], onesb[:],
                    x2b[:, o * CT2 + h * 512: o * CT2 + (h + 1) * 512],
                    start=(o == 0), stop=(o == NDC - 1)),
                    waits=([(sV, sV.v)] if (o == 0 and h == 0) else []),
                    inc=(sT if (o == NDC - 1 and h == 1) else None))
        for o in range(NDC):
            E.add("vector", lambda v, o=o: v.tensor_tensor(
                sq2[:], x2b[:, o * CT2: o * CT2 + TOK], x2b[:, o * CT2: o * CT2 + TOK], ALU.mult),
                waits=([(sT, sT.v)] if o > 0 else []), inc=sV)
            for h in range(2):
                E.add("tensor", lambda t, o=o, h=h: t.matmul(
                    (pX0 if h == 0 else pX1)[:], onesb[:], sq2[:, h * 512:(h + 1) * 512],
                    start=(o == 0), stop=(o == NDC - 1)),
                    waits=([(sV, sV.v)] if h == 0 else []),
                    inc=(sT if h == 1 else None))
        E.add("vector", lambda v: v.tensor_scalar_mul(rep_sl(2, 0), pM2[:], 1.0 / D),
              waits=[(sT, sT.v)])
        E.add("vector", lambda v: v.tensor_scalar_mul(rep_sl(2, 1), pM3[:], 1.0 / D))
        E.add("vector", lambda v: v.tensor_scalar_mul(rep_sl(3, 0), pX0[:], 1.0 / D))
        E.add("vector", lambda v: v.tensor_scalar_mul(rep_sl(3, 1), pX1[:], 1.0 / D))
        for h in range(2):
            E.add("vector", lambda v, h=h: v.tensor_tensor(rec[:], rep_sl(2, h), rep_sl(2, h), ALU.mult))
            E.add("vector", lambda v, h=h: v.tensor_tensor(rep_sl(3, h), rep_sl(3, h), rec[:], ALU.subtract))
            E.add("vector", lambda v, h=h: v.tensor_scalar_add(rep_sl(3, h), rep_sl(3, h), EPS),
                  inc=(sV if h == 1 else None))
        E.add("scalar", lambda s: s.activation(reps[:, 3 * TOK:4 * TOK], reps[:, 3 * TOK:4 * TOK], AF.Sqrt),
              waits=[(sV, sV.v)], inc=sC)
        E.add("vector", lambda v: v.reciprocal(reps[:, 0:TOK], reps[:, 3 * TOK:4 * TOK]),
              waits=[(sC, sC.v)])
        E.add("vector", lambda v: v.tensor_tensor(reps[:, TOK:2 * TOK], reps[:, 0:TOK],
                                                  reps[:, 2 * TOK:3 * TOK], ALU.mult))
        E.add("vector", lambda v: v.tensor_scalar_mul(reps[:, TOK:2 * TOK], reps[:, TOK:2 * TOK], -1.0),
              inc=sV)

        if BUILD_PHASES < 6:
            trunc_finalize()
            return nc
        es2.close()
        es6 = ExitStack()
        open_stacks.append(es6)
        NSW = 4
        NOF = (FF // NSW) // 128
        gbuf = SB("gbuf", [128, NOF * TOK], BF16, es6)
        wfb = [SB("wfb0", [128, NDC * 128], BF16, es6), SB("wfb1", [128, NDC * 128], BF16, es6)]
        wpb = [SB("wpb0", [128, NOF * 128], BF16, es6), SB("wpb1", [128, NOF * 128], BF16, es6)]
        fcf = SB("fcf", [128, TOK], F32, es6)
        spf = [SB("spf0", [128, 512], F32, es6), SB("spf1", [128, 512], F32, es6)]

        wfc_src = dr["wfc"].ap().rearrange("(dc p) (ob oj) -> ob p dc oj", p=128, oj=128)
        wpj_src = dr["wproj"].ap().rearrange("(sw fc p) (ob oj) -> sw ob p fc oj",
                                             p=128, oj=128, sw=NSW)
        for sw in range(NSW):
            E.dma("gpsimd", wfb[0][:].rearrange("p (dc oj) -> p dc oj", dc=NDC),
                  wfc_src[sw * NOF], sW, waits=[(sT, sT.v)])
            blkf_T = []
            gel_C = []
            for of in range(NOF):
                og = sw * NOF + of
                wb = wfb[of % 2]
                if of + 1 < NOF:
                    w = [(sT, blkf_T[of - 1])] if of >= 1 else []
                    E.dma("gpsimd", wfb[(of + 1) % 2][:].rearrange("p (dc oj) -> p dc oj", dc=NDC),
                          wfc_src[og + 1], sW, waits=w)
                phsl = (pPh if (of % 2 == 0) else pPh2)[:, 0:2]
                for dc in range(NDC):
                    wts = []
                    if dc == 0:
                        wts = [(sW, sW.v - (16 if of + 1 < NOF else 0))]
                        if of <= 1:
                            wts.append((sV, sV.v))
                            wts.append((sC, sC.v))
                        if of >= 2:
                            wts.append((sC, gel_C[of - 2]))
                    for h in range(2):
                        E.add("tensor", lambda t, wb=wb, dc=dc, h=h, of=of: t.matmul(
                            (pM0 if of % 2 == 0 else pM2, pM1 if of % 2 == 0 else pM3)[h][:],
                            wb[:, dc * 128:(dc + 1) * 128],
                            x2b[:, dc * CT2 + h * 512: dc * CT2 + (h + 1) * 512],
                            start=(dc == 0), stop=(dc == NDC - 1)),
                            waits=(wts if h == 0 else []))
                    E.add("tensor", lambda t, wb=wb, dc=dc, phsl=phsl: t.matmul(
                        phsl, wb[:, dc * 128:(dc + 1) * 128],
                        x2b[:, dc * CT2 + TOK: dc * CT2 + TOK + 2],
                        start=(dc == 0), stop=(dc == NDC - 1), skip_group_check=True),
                        inc=(sT if dc == NDC - 1 else None))
                blkf_T.append(sT.v)
                E.addc("vector", lambda v, phsl=phsl: v.tensor_copy(ucol[:, 0:2], phsl),
                       sV, waits=[(sT, sT.v), (sC, sC.v)])
                E.addc("vector", lambda v, og=og: v.tensor_tensor(
                    ucol[:, 2:3], ucol[:, 1:2], vcols[:, VBFC + og:VBFC + og + 1], ALU.add), sV)
                for h in range(2):
                    sl = slice(h * 512, (h + 1) * 512)
                    pm = (pM0 if of % 2 == 0 else pM2, pM1 if of % 2 == 0 else pM3)[h]
                    E.add("vector", lambda v, sl=sl, h=h, pm=pm: v.tensor_tensor(
                        fcf[:, sl], pm[:], rep_sl(0, h), ALU.mult),
                        waits=([(sC, gel_C[of - 1])] if (h == 0 and of >= 1) else []))
                    E.add("vector", lambda v, sl=sl, h=h: v.scalar_tensor_tensor(
                        fcf[:, sl], rep_sl(1, h), ucol[:, 0:1], fcf[:, sl], ALU.mult, ALU.add),
                        inc=(sV if h == 1 else None))
                E.add("scalar", lambda s, of=of: s.activation(
                    gbuf[:, of * TOK:(of + 1) * TOK], fcf[:], AF.Gelu_apprx_tanh, bias=ucol[:, 2:3]),
                    waits=[(sV, sV.v)], inc=sC)
                gel_C.append(sC.v)
            E.dma("gpsimd", wpb[0][:].rearrange("p (fc oj) -> p fc oj", fc=NOF),
                  wpj_src[sw, 0], sW, waits=[(sT, sT.v), (sC, sC.v)])
            blkp_o = []
            pbank_evac = {0: 0, 1: 0, 2: 0, 3: 0}
            for o in range(NDC):
                wb = wpb[o % 2]
                if o + 1 < NDC:
                    w = [(sT, blkp_o[o - 1])] if o >= 1 else []
                    E.dma("gpsimd", wpb[(o + 1) % 2][:].rearrange("p (fc oj) -> p fc oj", fc=NOF),
                          wpj_src[sw, o + 1], sW, waits=w)
                for h in range(2):
                    bi = 2 * (o % 2) + h
                    pm = (pM0, pM1, pM2, pM3)[bi]
                    sp = spf[(2 * o + h) % 2]
                    for fc in range(NOF):
                        wts = []
                        if fc == 0:
                            wts = [(sW, sW.v - (16 if o + 1 < NDC else 0))]
                            if pbank_evac[bi] > 0:
                                wts.append((sV, pbank_evac[bi]))
                            if o <= 1:
                                wts.append((sV, sV.v))
                                wts.append((sC, sC.v))
                        E.add("tensor", lambda t, wb=wb, fc=fc, pm=pm, h=h: t.matmul(
                            pm[:], wb[:, fc * 128:(fc + 1) * 128],
                            gbuf[:, fc * TOK + h * 512: fc * TOK + (h + 1) * 512],
                            start=(fc == 0), stop=(fc == NOF - 1)),
                            waits=(wts if fc == 0 else []),
                            inc=(sT if fc == NOF - 1 else None))
                    if h == 1:
                        blkp_o.append(sT.v)
                    if sw == 0:
                        E.add("vector", lambda v, pm=pm, o=o, sp=sp: v.tensor_scalar_add(
                            sp[:], pm[:], vcols[:, VBPJ + o:VBPJ + o + 1]),
                            waits=[(sT, sT.v)], inc=sV)
                    else:
                        E.add("vector", lambda v, pm=pm, sp=sp: v.tensor_copy(sp[:], pm[:]),
                              waits=[(sT, sT.v)], inc=sV)
                    pbank_evac[bi] = sV.v
                    tbB, vbB = sT.v, sV.v
                    for cc_ in range(4):
                        c = h * 4 + cc_
                        pb = (pX0 if (cc_ % 2 == 0) else pX1)[:, 0:128]
                        w = [(sV, vbB)] if cc_ == 0 else []
                        if cc_ >= 2:
                            w.append((sV, vbB + cc_ - 1))
                        E.add("tensor", lambda t, cc_=cc_, pb=pb, sp=sp: t.transpose(
                            pb, sp[:, cc_ * 128:(cc_ + 1) * 128], identf[:]), waits=w, inc=sT)
                        asl = A[:, c * D + o * 128: c * D + (o + 1) * 128]
                        E.add("vector", lambda v, asl=asl, pb=pb: v.tensor_tensor(
                            asl, asl, pb, ALU.add), waits=[(sT, tbB + cc_ + 1)], inc=sV)

        if BUILD_PHASES < 7:
            trunc_finalize()
            return nc
        es6.close()
        fm = [SB("fm0", [128, D], F32), SB("fm1", [128, D], F32)]
        for c in range(NCH):
            buf = fm[c % 2]
            w = [(sG, moe_G), (sV, sV.v)] if c == 0 else [(sV, sV.v)]
            E.dma("sync", buf[:], dr["moeA"].ap()[c:TOK:NCH, :], sX, waits=w)
            E.add("vector", lambda v, c=c, buf=buf: v.tensor_tensor(
                A[:, c * D:(c + 1) * D], A[:, c * D:(c + 1) * D], buf[:], ALU.add),
                waits=[(sX, sX.v)], inc=sV)
            E.dma("sync", buf[:], dr["moeB"].ap()[c:TOK:NCH, :], sX, waits=[(sV, sV.v)])
            E.add("vector", lambda v, c=c, buf=buf: v.tensor_tensor(
                A[:, c * D:(c + 1) * D], A[:, c * D:(c + 1) * D], buf[:], ALU.add),
                waits=[(sX, sX.v)], inc=sV)
            E.dma("sync", dr["out"].ap()[c * 128:(c + 1) * 128, :], A[:, c * D:(c + 1) * D],
                  sO, waits=[(sV, sV.v)])
        E.add("gpsimd", lambda g: g.wait_ge(sO.h, sO.v))

        E.flush(block)
    return nc


def _host_inputs(inputs):
    x = np.asarray(inputs["L0_attn_hidden_states"])[0]
    vecs = np.zeros((192, 128), np.float32)

    def putv(row0, vec):
        v = np.asarray(vec, np.float32).reshape(-1, 128)
        vecs[row0:row0 + v.shape[0]] = v

    putv(0, inputs["L0_attn_ln1_g"]); putv(16, inputs["L0_attn_ln1_b"])
    putv(32, inputs["L0_attn_ln2_g"]); putv(48, inputs["L0_attn_ln2_b"])
    putv(64, inputs["L0_attn_bq"]); putv(80, inputs["L0_attn_bo"])
    putv(96, inputs["L0_attn_bproj"]); putv(112, inputs["L0_attn_bkv"])
    putv(128, inputs["L0_attn_bfc"])
    vecs[114] = -SHIFT
    masks = np.zeros((4, 128, 512), np.float32)
    for r in range(4):
        masks[r] = (np.arange(512)[None, :] >= (np.arange(128)[:, None] + 128 * r))
    pcvals = np.zeros((128, 2 * NCH), np.float32)
    for c in range(NCH):
        pcvals[:, 2 * c] = np.arange(128) + 1.0
        pcvals[:, 2 * c + 1] = c
    common = {
        "wq": np.asarray(inputs["L0_attn_wq"], np.float32),
        "wkv": np.asarray(inputs["L0_attn_wkv"], np.float32),
        "wo": np.asarray(inputs["L0_attn_wo"], np.float32),
        "wfc": np.asarray(inputs["L0_attn_wfc"], np.float32),
        "wproj": np.asarray(inputs["L0_attn_wproj"], np.float32),
        "routerT": np.ascontiguousarray(np.asarray(inputs["L0_ffn_router_w"], np.float32).T),
        "w1": np.asarray(inputs["L0_ffn_w1"], np.float32),
        "w2": np.asarray(inputs["L0_ffn_w2"], np.float32),
        "w3": np.asarray(inputs["L0_ffn_w3"], np.float32),
        "vecs": vecs,
        "identf": np.eye(128, dtype=np.float32),
        "identb": np.eye(128, dtype=np.float32).astype(BF),
        "onesb": np.ones((128, 128), np.float32).astype(BF),
        "Ustrict": (np.arange(128)[:, None] < np.arange(128)[None, :]).astype(np.float32),
        "masks": masks.astype(BF),
        "iotaC": np.tile(np.arange(C2, dtype=np.float32), (128, 1)),
        "pcvals": pcvals.astype(BF),
        "onesr": np.ones((1, 128), np.float32),
    }
    in_maps = []
    for c in range(NCORES):
        m = dict(common)
        m["x"] = np.ascontiguousarray(x[c * TOK:(c + 1) * TOK])
        in_maps.append(m)
    return in_maps


def kernel(**inputs):
    in_maps = _host_inputs(inputs)
    nc = build_program()
    res = run_bass_kernel_spmd(nc, in_maps, core_ids=list(range(NCORES)))
    out = np.concatenate([np.asarray(res.results[c]["out"]) for c in range(NCORES)], axis=0)
    return out[None].astype(np.float32)


# version 08 (final)
# speedup vs baseline: 1.1026x; 1.0355x over previous
import numpy as np
import ml_dtypes
from contextlib import ExitStack

import concourse.bass as bass
import concourse.mybir as mybir
from concourse.bass_utils import run_bass_kernel_spmd
from concourse.bass import IndirectOffsetOnAxis
from concourse.aot_env import allow_ici_all_to_all

F32 = mybir.dt.float32
BF16 = mybir.dt.bfloat16
I32 = mybir.dt.int32
U16 = mybir.dt.uint16
AF = mybir.ActivationFunctionType
ALU = mybir.AluOpType
AXX = mybir.AxisListType.X
BF = ml_dtypes.bfloat16

BUILD_PHASES = 99
DBG_DUMPS = []

NCORES = 8
D = 2048
S = 8192
TOK = S // NCORES
NCH = TOK // 128
NDC = D // 128
NH = 16
HPC = NH // NCORES
FF = 8192
NE = 8
EFF = 512
C2 = 256
NG = 2 * NE
SCALE = 128.0 ** -0.5
SHIFT = 8.0
EPS = 1e-5
NQT = S // 512
KV_ELEMS = 2 * 128 * TOK
QB_ELEMS = 256 * TOK
CT = TOK + 2


class Sem:

    def __init__(self, ctx, nc, name):
        self.h = ctx.enter_context(nc.semaphore(name))
        self.v = 0


class Emitter:

    ENGINES = ("sync", "gpsimd", "tensor", "vector", "scalar")

    def __init__(self):
        self.ops = {e: [] for e in self.ENGINES}

    def add(self, eng, fn, waits=(), inc=None, incn=1):
        w = [(s.h, v) for (s, v) in waits]
        if inc is not None:
            inc.v += incn
            self.ops[eng].append((fn, w, (inc.h, incn)))
        else:
            self.ops[eng].append((fn, w, None))

    def addc(self, eng, fn, sem, waits=()):
        self.add(eng, fn, waits=list(waits) + [(sem, sem.v)], inc=sem)

    def dma(self, eng, out, in_, sem, waits=()):
        self.add(eng, lambda E, out=out, in_=in_: E.dma_start(out=out, in_=in_),
                 waits=waits, inc=sem, incn=16)

    def flush(self, block):
        for eng in self.ENGINES:
            lst = self.ops[eng]
            if not lst:
                continue

            def make(lst):
                def body(E):
                    for fn, waits, inc in lst:
                        for (h, v) in waits:
                            E.wait_ge(h, v)
                        r = fn(E)
                        if inc is not None:
                            if r is None:
                                r = E.engine_nop()
                            r.then_inc(inc[0], inc[1])
                return body
            getattr(block, eng)(make(lst))
        self.ops = {e: [] for e in self.ENGINES}


def build_program():
    nc = bass.Bass("TRN2", target_bir_lowering=False)
    dr = {}

    def DT(name, shape, dt, kind=None):
        dr[name] = (nc.dram_tensor(name, shape, dt) if kind is None
                    else nc.dram_tensor(name, shape, dt, kind=kind))
        return dr[name]

    DT("x", [TOK, D], F32, "ExternalInput")
    DT("wq", [D, D], F32, "ExternalInput")
    DT("wkv", [D, 256], F32, "ExternalInput")
    DT("wo", [D, D], F32, "ExternalInput")
    DT("wfc", [D, FF], F32, "ExternalInput")
    DT("wproj", [FF, D], F32, "ExternalInput")
    DT("routerT", [D, NE], F32, "ExternalInput")
    DT("w1", [NE, D, EFF], F32, "ExternalInput")
    DT("w2", [NE, EFF, D], F32, "ExternalInput")
    DT("w3", [NE, D, EFF], F32, "ExternalInput")
    DT("vecs", [192, 128], F32, "ExternalInput")
    DT("onesr", [1, 128], F32, "ExternalInput")
    DT("identf", [128, 128], F32, "ExternalInput")
    DT("identb", [128, 128], BF16, "ExternalInput")
    DT("onesb", [128, 128], BF16, "ExternalInput")
    DT("Ustrict", [128, 128], F32, "ExternalInput")
    DT("masks", [4, 128, 512], BF16, "ExternalInput")
    DT("iotaC", [128, C2], F32, "ExternalInput")
    DT("pcvals", [128, 2 * NCH], BF16, "ExternalInput")
    DT("out", [TOK, D], F32, "ExternalOutput")
    DT("kv_in", [KV_ELEMS], BF16)
    DT("kv_out", [NCORES * KV_ELEMS], BF16)
    DT("q_in", [NCORES * QB_ELEMS], BF16)
    DT("q_out", [NCORES * QB_ELEMS], BF16)
    DT("at_in", [NCORES * QB_ELEMS], BF16)
    DT("at_out", [NCORES * QB_ELEMS], BF16)
    DT("xn_rows", [TOK, D], BF16)
    DT("w1s", [NE, D, EFF], BF16)
    DT("w2s", [NE, EFF, D], BF16)
    DT("w3s", [NE, D, EFF], BF16)
    DT("moeA", [TOK + 1, D], F32)
    DT("moeB", [TOK + 1, D], F32)

    rg = [list(range(NCORES))]

    with ExitStack() as ctx:
        def SB(name, shape, dt, stack=None):
            t = (stack or ctx).enter_context(nc.sbuf_tensor("sb_" + name, shape, dt))
            TILES[name] = t
            return t
        TILES = {}

        identf = SB("identf", [128, 128], F32)
        identb = SB("identb", [128, 128], BF16)
        onesb = SB("onesb", [128, 128], BF16)
        Us = SB("Ustrict", [128, 128], F32)
        masks = SB("masks", [128, 4 * 512], BF16)
        iotaC = SB("iotaC", [128, C2], F32)
        pcv = SB("pcvals", [128, 2 * NCH], BF16)
        vrows = SB("vrows", [128, 128], F32)
        vrows2 = SB("vrows2", [64, 128], F32)
        onesr = SB("onesr", [1, 128], F32)
        vcols = SB("vcols", [128, 192], F32)
        VG1, VB1, VG2, VB2, VBQ, VBO, VBPJ, VBKV, VBFC = 0, 16, 32, 48, 64, 80, 96, 112, 128
        statc = SB("statc", [128, 64], F32)
        routc = SB("routc", [128, 48], F32)
        mx8 = SB("mx8", [128, 8], F32)
        idx8 = SB("idx8", [128, 8], U16)
        lists = SB("lists", [128, 6 * NG], F32)
        idxgI = SB("idxgI", [128, 2 * NG], I32)
        idxsI = SB("idxsI", [128, 2 * NG], I32)
        gatec = SB("gatec", [128, 2 * NG], F32)
        tpv = SB("tpv", [128, 2 * 3 * NCH], BF16)
        reps = SB("reps", [128, 4 * TOK], F32)
        statrow = SB("statrow", [1, TOK], F32)
        srow_t = SB("srow_t", [32, 128], F32)
        ucol = SB("ucol", [128, 4], F32)
        small = SB("small", [128, 64], F32)
        rec = SB("rec", [128, 512], F32)
        dbg1 = SB("dbg1", [128, 64], F32)
        dbg2 = SB("dbg2", [128, 64], F32)


        sW = Sem(ctx, nc, "sW")
        sX = Sem(ctx, nc, "sX")
        sT = Sem(ctx, nc, "sT")
        sV = Sem(ctx, nc, "sV")
        sC = Sem(ctx, nc, "sC")
        sG = Sem(ctx, nc, "sG")
        sO = Sem(ctx, nc, "sO")
        sCC = Sem(ctx, nc, "sCC")

        block = ctx.enter_context(nc.Block())

        pM0 = ctx.enter_context(nc.psum_tensor("pM0", [128, 512], F32))
        pM1 = ctx.enter_context(nc.psum_tensor("pM1", [128, 512], F32))
        pM2 = ctx.enter_context(nc.psum_tensor("pM2", [128, 512], F32))
        pM3 = ctx.enter_context(nc.psum_tensor("pM3", [128, 512], F32))
        pPh = ctx.enter_context(nc.psum_tensor("pPh", [128, 512], F32))
        pX0 = ctx.enter_context(nc.psum_tensor("pX0", [128, 512], F32))
        pX1 = ctx.enter_context(nc.psum_tensor("pX1", [128, 512], F32))
        pPh2 = ctx.enter_context(nc.psum_tensor("pPh2", [128, 512], F32))

        def pXb(k):
            return (pX0 if (k % 2 == 0) else pX1)[:, 0:64].bitcast(BF16)

        E = Emitter()

        def trunc_finalize():
            for i, name in enumerate(DBG_DUMPS):
                if name.startswith("dram:"):
                    _, dn, nrow = name.split(":")
                    nrow = int(nrow)
                    v = dr[dn].ap()
                    flat = 1
                    for s in v.shape:
                        flat *= s
                    v = v.rearrange(" ".join(f"a{i}" for i in range(len(v.shape))) + " -> (" +
                                    " ".join(f"a{i}" for i in range(len(v.shape))) + ")") if len(v.shape) > 1 else v
                    v = v.rearrange("(r c) -> r c", c=D)
                    E.dma("gpsimd", dr["out"].ap()[i * 128:i * 128 + nrow, :],
                          v[0:nrow, :], sO,
                          waits=[(sT, sT.v), (sV, sV.v), (sC, sC.v), (sX, sX.v),
                                 (sW, sW.v), (sG, sG.v), (sCC, sCC.v), (sO, sO.v)])
                    continue
                coff = 0
                if "@" in name:
                    name, coff = name.split("@")
                    coff = int(coff)
                t = TILES[name]
                sh = t[:].shape
                cols = min((sh[-1] if len(sh) > 1 else 1) - coff, D)
                rows = min(sh[0], 128)
                E.dma("gpsimd", dr["out"].ap()[i * 128:i * 128 + rows, 0:cols],
                      t[0:rows, coff:coff + cols] if len(sh) == 2 else t[0:rows], sO,
                      waits=[(sT, sT.v), (sV, sV.v), (sC, sC.v), (sX, sX.v),
                             (sW, sW.v), (sG, sG.v), (sCC, sCC.v)])
            E.add("gpsimd", lambda g: g.wait_ge(sO.h, sO.v))
            E.flush(block)
            for st in open_stacks:
                st.close()

        open_stacks = []

        def rep_sl(slot, h):
            return reps[:, slot * TOK + h * 512: slot * TOK + (h + 1) * 512]

        def tr_evac_pipe(items, dtype="bf", first_waits=()):
            tb, vb = sT.v, sV.v
            for k, (src, _) in enumerate(items):
                if dtype == "bf":
                    pb = pXb(k)[:, 0:128]
                    idn = identb
                else:
                    pb = (pX0 if (k % 2 == 0) else pX1)[:, 0:128]
                    idn = identf
                w = list(first_waits) if k == 0 else []
                if vb + k - 1 > 0:
                    w.append((sV, vb + k - 1))
                E.add("tensor", lambda t, src=src, pb=pb, idn=idn: t.transpose(pb, src, idn[:]),
                      waits=w, inc=sT)
            for k, (_, ev) in enumerate(items):
                if dtype == "bf":
                    pb = pXb(k)[:, 0:128]
                else:
                    pb = (pX0 if (k % 2 == 0) else pX1)[:, 0:128]
                E.add("vector", lambda v, ev=ev, pb=pb: ev(v, pb),
                      waits=[(sT, tb + k + 1)], inc=sV)

        for nm, t in [("identf", identf), ("identb", identb), ("onesb", onesb),
                      ("Ustrict", Us), ("iotaC", iotaC), ("pcvals", pcv)]:
            E.dma("sync", t[:], dr[nm].ap(), sW)
        E.dma("sync", masks[:].rearrange("p (a b) -> p a b", a=4),
              dr["masks"].ap().rearrange("a p b -> p a b"), sW)
        E.dma("sync", vrows[:], dr["vecs"].ap()[0:128, :], sW)
        E.dma("sync", vrows2[:], dr["vecs"].ap()[128:192, :], sW)
        E.dma("sync", onesr[:], dr["onesr"].ap(), sW)
        cW0 = (sW, sW.v)
        E.add("tensor", lambda t: t.transpose(pX0[:, 0:128], vrows[0:128, :], identf[:]),
              waits=[cW0], inc=sT)
        E.add("tensor", lambda t: t.transpose(pX1[:, 0:64], vrows2[:, :], identf[0:64, 0:64]),
              inc=sT)
        cT0 = (sT, sT.v)
        E.add("vector", lambda v: v.tensor_copy(vcols[:, 0:128], pX0[:, 0:128]), waits=[cT0])
        E.add("vector", lambda v: v.tensor_copy(vcols[:, 128:192], pX1[:, 0:64]), inc=sV)

        if BUILD_PHASES < 1:
            trunc_finalize()
            return nc
        esA = ExitStack()
        open_stacks.append(esA)
        esB = ExitStack()
        open_stacks.append(esB)
        xgT = SB("xgT", [128, NDC * CT], BF16, esB)
        sqdump = SB("sqdump", [128, D], BF16, esA)
        xn_tok = SB("xn_tok", [128, NCH * D], BF16, esA)
        xhat = SB("xhat", [128, NCH * D], BF16, esA)
        xnT_t = SB("xnT_t", [128, TOK], BF16, esA)
        rtw = SB("rtw", [128, NDC * NE], BF16, esA)
        rsc = SB("rsc", [NE, TOK], F32, esA)
        selt = SB("selt", [128, C2], BF16, esA)
        msk8 = SB("msk8", [128, 40], F32, esA)

        E.dma("gpsimd", xhat[:].rearrange("p (c d) -> p c d", c=NCH),
              dr["x"].ap().rearrange("(c p) d -> p c d", p=128), sX)
        E.dma("gpsimd", rtw[:].rearrange("p (c e) -> p c e", c=NDC),
              dr["routerT"].ap().rearrange("(c p) e -> p c e", p=128), sX)
        cX = (sX, sX.v)
        precast_W = []
        for e_ in range(NE):
            E.dma("gpsimd", dr["w1s"].ap()[e_], dr["w1"].ap()[e_], sW)
            E.dma("gpsimd", dr["w3s"].ap()[e_], dr["w3"].ap()[e_], sW)
            E.dma("gpsimd", dr["w2s"].ap()[e_], dr["w2"].ap()[e_], sW)
            precast_W.append(sW.v)
        for c in range(NCH):
            E.add("vector", lambda v, c=c: v.reduce_sum(
                statc[:, 48 + c:49 + c], xhat[:, c * D:(c + 1) * D], axis=AXX),
                waits=([cX] if c == 0 else []), inc=(sV if c == NCH - 1 else None))
        for c in range(NCH):
            E.add("scalar", lambda s, c=c: s.activation(
                sqdump[:], xhat[:, c * D:(c + 1) * D], AF.Square,
                accum_out=statc[:, 56 + c:57 + c]),
                waits=([cX] if c == 0 else []), inc=(sC if c == NCH - 1 else None))

        E.addc("vector", lambda v: v.tensor_scalar_mul(statc[:, 0:8], statc[:, 48:56], 1.0 / D),
               sV, waits=[(sC, sC.v)])
        E.addc("vector", lambda v: v.tensor_scalar_mul(statc[:, 48:56], statc[:, 56:64], 1.0 / D), sV)
        E.addc("vector", lambda v: v.tensor_tensor(statc[:, 56:64], statc[:, 0:8], statc[:, 0:8], ALU.mult), sV)
        E.addc("vector", lambda v: v.tensor_tensor(statc[:, 8:16], statc[:, 48:56], statc[:, 56:64], ALU.subtract), sV)
        E.addc("vector", lambda v: v.tensor_scalar_add(statc[:, 8:16], statc[:, 8:16], EPS), sV)
        E.add("scalar", lambda s: s.activation(statc[:, 48:56], statc[:, 8:16], AF.Sqrt),
              waits=[(sV, sV.v)], inc=sC)
        E.addc("vector", lambda v: v.reciprocal(statc[:, 16:24], statc[:, 48:56]),
               sV, waits=[(sC, sC.v)])
        E.addc("vector", lambda v: v.tensor_tensor(statc[:, 48:56], statc[:, 16:24], statc[:, 16:24], ALU.mult), sV)
        E.addc("vector", lambda v: v.tensor_scalar_add(statc[:, 56:64], statc[:, 8:16], -EPS), sV)
        E.addc("vector", lambda v: v.tensor_tensor(statc[:, 48:56], statc[:, 48:56], statc[:, 56:64], ALU.mult), sV)
        E.addc("vector", lambda v: v.tensor_scalar_add(statc[:, 48:56], statc[:, 48:56], EPS), sV)
        E.add("scalar", lambda s: s.activation(statc[:, 48:56], statc[:, 48:56], AF.Sqrt),
              waits=[(sV, sV.v)], inc=sC)
        E.addc("vector", lambda v: v.reciprocal(statc[:, 56:64], statc[:, 48:56]),
               sV, waits=[(sC, sC.v)])
        E.addc("vector", lambda v: v.tensor_tensor(statc[:, 24:32], statc[:, 16:24], statc[:, 56:64], ALU.mult), sV)
        E.addc("vector", lambda v: v.tensor_tensor(statc[:, 32:40], statc[:, 16:24], statc[:, 0:8], ALU.mult), sV)
        E.addc("vector", lambda v: v.tensor_scalar_mul(statc[:, 32:40], statc[:, 32:40], -1.0), sV)
        E.addc("vector", lambda v: v.tensor_tensor(statc[:, 40:48], statc[:, 24:32], statc[:, 0:8], ALU.mult), sV)
        E.addc("vector", lambda v: v.tensor_scalar_mul(statc[:, 40:48], statc[:, 40:48], -1.0), sV)
        for c in range(NCH):
            E.add("vector", lambda v, c=c: v.tensor_scalar(
                xn_tok[:, c * D:(c + 1) * D], xhat[:, c * D:(c + 1) * D],
                statc[:, 0 + c:1 + c], statc[:, 16 + c:17 + c], ALU.subtract, ALU.mult),
                inc=(sV if c == NCH - 1 else None))
        E.dma("sync", dr["xn_rows"].ap().rearrange("(p c) d -> p c d", c=NCH),
              xn_tok[:].rearrange("p (c d) -> p c d", c=NCH), sX, waits=[(sV, sV.v)])

        E.add("tensor", lambda t: t.transpose(pX0[0:32, 0:128], statc[:, 16:48], identf[:]),
              waits=[(sV, sV.v)], inc=sT)
        E.add("vector", lambda v: v.tensor_copy(srow_t[:, :], pX0[0:32, 0:128]),
              waits=[(sT, sT.v)], inc=sV)
        for slot, srcrow in [(0, 1), (1, 3), (2, 0), (3, 2)]:
            E.dma("sync", statrow[0:1, :], srow_t[8 * srcrow:8 * srcrow + 8, :], sX,
                  waits=[(sV, sV.v)] + ([(sT, sT.v)] if slot > 0 else []))
            cXr = (sX, sX.v)
            for h in range(2):
                pb = pX0 if h == 0 else pX1
                E.add("tensor", lambda t, pb=pb, h=h: t.matmul(
                    pb[:, 0:512], onesr[:, :],
                    statrow[0:1, h * 512:(h + 1) * 512], start=True, stop=True),
                    waits=([cXr] if h == 0 else []), inc=sT)
            cTb = (sT, sT.v)
            for h in range(2):
                pb = pX0 if h == 0 else pX1
                E.add("vector", lambda v, pb=pb, slot=slot, h=h: v.tensor_copy(
                    rep_sl(slot, h), pb[:, 0:512]),
                    waits=([cTb] if h == 0 else []), inc=(sV if h == 1 else None))

        for dc in range(NDC):
            items = []
            for c in range(NCH):
                items.append((xhat[:, c * D + dc * 128: c * D + (dc + 1) * 128],
                              lambda v, pb, dc=dc, c=c: v.tensor_copy(
                                  xgT[:, dc * CT + c * 128: dc * CT + (c + 1) * 128], pb)))
            fw = [(sV, sV.v)] if dc == 0 else []
            tr_evac_pipe(items, "bf", first_waits=fw)
            cVe = (sV, sV.v)
            for h in range(2):
                E.add("vector", lambda v, dc=dc, h=h: v.tensor_tensor(
                    xnT_t[:, h * 512:(h + 1) * 512],
                    xgT[:, dc * CT + h * 512: dc * CT + (h + 1) * 512], rep_sl(2, h), ALU.mult),
                    waits=([cVe, (sT, sT.v)] if h == 0 else []))
            for h in range(2):
                E.add("vector", lambda v, h=h: v.tensor_tensor(
                    xnT_t[:, h * 512:(h + 1) * 512], xnT_t[:, h * 512:(h + 1) * 512],
                    rep_sl(3, h), ALU.add), inc=(sV if h == 1 else None))
            cVx = (sV, sV.v)
            for h in range(2):
                pb = pM0 if h == 0 else pM1
                E.add("tensor", lambda t, pb=pb, dc=dc, h=h: t.matmul(
                    pb[0:NE, :], rtw[:, dc * NE:(dc + 1) * NE], xnT_t[:, h * 512:(h + 1) * 512],
                    start=(dc == 0), stop=(dc == NDC - 1)),
                    waits=([cVx] if h == 0 else []), inc=(sT if h == 1 else None))
            E.add("vector", lambda v, dc=dc: v.tensor_scalar_mul(
                xgT[:, dc * CT: dc * CT + TOK], xgT[:, dc * CT: dc * CT + TOK],
                vcols[:, VG1 + dc:VG1 + dc + 1]))
            E.add("vector", lambda v, dc=dc: v.tensor_copy(
                xgT[:, dc * CT + TOK: dc * CT + TOK + 1], vcols[:, VG1 + dc:VG1 + dc + 1]))
            E.add("vector", lambda v, dc=dc: v.tensor_copy(
                xgT[:, dc * CT + TOK + 1: dc * CT + TOK + 2], vcols[:, VB1 + dc:VB1 + dc + 1]),
                inc=sV)

        cRT = (sT, sT.v)
        for h in range(2):
            pb = pM0 if h == 0 else pM1
            E.add("vector", lambda v, pb=pb, h=h: v.tensor_copy(
                rsc[:, h * 512:(h + 1) * 512], pb[0:NE, :]),
                waits=([cRT] if h == 0 else []), inc=(sV if h == 1 else None))
        items = []
        for c in range(NCH):
            items.append((rsc[:, c * 128:(c + 1) * 128],
                          lambda v, pb, c=c: v.tensor_copy(small[:, 8:16], pb[:, 0:8])))
        tbase = sT.v
        for c in range(NCH):
            pb = (pX0 if (c % 2 == 0) else pX1)[:, 0:8]
            E.add("tensor", lambda t, c=c, pb=pb: t.transpose(
                pb, rsc[:, c * 128:(c + 1) * 128], identf[0:8, 0:8]),
                waits=([(sV, sV.v)] if c == 0 else [(sV, sV.v)]), inc=sT)
            E.addc("vector", lambda v, pb=pb: v.tensor_copy(small[:, 8:16], pb),
                   sV, waits=[(sT, sT.v)])
            E.addc("vector", lambda v: v.max(out=mx8[:], in_=small[:, 8:16]), sV)
            E.addc("vector", lambda v: v.max_index(idx8[:], mx8[:], small[:, 8:16]), sV)
            E.addc("vector", lambda v, c=c: v.tensor_copy(routc[:, 32 + c:33 + c], mx8[:, 0:1]), sV)
            E.addc("vector", lambda v, c=c: v.tensor_copy(routc[:, 40 + c:41 + c], mx8[:, 1:2]), sV)
            E.addc("vector", lambda v, c=c: v.tensor_copy(routc[:, 0 + c:1 + c], idx8[:, 0:1]), sV)
            E.addc("vector", lambda v, c=c: v.tensor_copy(routc[:, 8 + c:9 + c], idx8[:, 1:2]), sV)
        E.addc("vector", lambda v: v.tensor_tensor(routc[:, 40:48], routc[:, 40:48], routc[:, 32:40], ALU.subtract), sV)
        E.add("scalar", lambda s: s.activation(routc[:, 32:40], routc[:, 40:48], AF.Exp),
              waits=[(sV, sV.v)], inc=sC)
        E.addc("vector", lambda v: v.tensor_scalar_add(routc[:, 40:48], routc[:, 32:40], 1.0),
               sV, waits=[(sC, sC.v)])
        E.addc("vector", lambda v: v.reciprocal(routc[:, 16:24], routc[:, 40:48]), sV)
        E.addc("vector", lambda v: v.tensor_scalar(routc[:, 24:32], routc[:, 16:24], -1.0, 1.0, ALU.mult, ALU.add), sV)
        for s_ in range(2):
            E.addc("vector", lambda v, s_=s_: v.tensor_copy(
                tpv[:, s_ * 3 * NCH + 0: (s_ + 1) * 3 * NCH: 3], pcv[:, 0:2 * NCH:2]), sV)
            E.addc("vector", lambda v, s_=s_: v.tensor_copy(
                tpv[:, s_ * 3 * NCH + 1: (s_ + 1) * 3 * NCH: 3], pcv[:, 1:2 * NCH:2]), sV)
            E.addc("vector", lambda v, s_=s_: v.tensor_copy(
                tpv[:, s_ * 3 * NCH + 2: (s_ + 1) * 3 * NCH: 3], routc[:, 16 + 8 * s_:24 + 8 * s_]), sV)

        for g_ in range(NG):
            s_, e_ = g_ // NE, g_ % NE
            icol = routc[:, 8 * s_: 8 * s_ + 8]
            E.addc("vector", lambda v, icol=icol, e_=e_: v.tensor_single_scalar(
                msk8[:, 0:8], icol, float(e_), ALU.is_equal),
                sV, waits=[(sT, sT.v)])
            E.addc("vector", lambda v: v.tensor_tensor_scan(
                msk8[:, 8:16], msk8[:, 0:8], msk8[:, 0:8], 0.0, ALU.add, ALU.bypass), sV)
            E.addc("vector", lambda v: v.tensor_tensor(
                msk8[:, 16:24], msk8[:, 8:16], msk8[:, 0:8], ALU.subtract), sV)
            E.add("tensor", lambda t: t.matmul(pPh[:, 0:1], Us[:], msk8[:, 15:16],
                                               start=True, stop=True),
                  waits=[(sV, sV.v)], inc=sT)
            E.addc("vector", lambda v: v.tensor_copy(msk8[:, 24:25], pPh[:, 0:1]),
                   sV, waits=[(sT, sT.v)])
            E.addc("vector", lambda v: v.tensor_scalar_add(msk8[:, 16:24], msk8[:, 16:24], msk8[:, 24:25]), sV)
            E.addc("vector", lambda v: v.tensor_single_scalar(msk8[:, 32:40], msk8[:, 0:8], 0.0, ALU.is_equal), sV)
            E.addc("vector", lambda v: v.scalar_tensor_tensor(
                msk8[:, 16:24], msk8[:, 32:40], 4096.0, msk8[:, 16:24], ALU.mult, ALU.add), sV)
            pg = pM2[:, 8 * (g_ % 2): 8 * (g_ % 2) + 6]
            for c in range(NCH):
                E.addc("vector", lambda v, c=c: v.tensor_single_scalar(
                    selt[:], iotaC[:], msk8[:, 16 + c:17 + c], ALU.is_equal),
                    sV, waits=([] if c == 0 else [(sT, sT.v)]))
                for jb in range(2):
                    E.add("tensor", lambda t, c=c, jb=jb, s_=s_, pg=pg: t.matmul(
                        pg[:, 3 * jb: 3 * (jb + 1)], selt[:, jb * 128:(jb + 1) * 128],
                        tpv[:, s_ * 3 * NCH + 3 * c: s_ * 3 * NCH + 3 * (c + 1)],
                        start=(c == 0 and jb == 0), stop=(c == NCH - 1 and jb == 1),
                        skip_group_check=True),
                        waits=([(sV, sV.v)] if jb == 0 else []),
                        inc=(sT if jb == 1 else None))
            E.addc("vector", lambda v, g_=g_, pg=pg: v.tensor_copy(
                lists[:, 6 * g_: 6 * (g_ + 1)], pg),
                sV, waits=[(sT, sT.v)])
        pp1 = lists[:, 0:6 * NG:3]
        cc = lists[:, 1:6 * NG:3]
        gg = lists[:, 2:6 * NG:3]
        E.addc("vector", lambda v: v.scalar_tensor_tensor(small[:, 0:32], pp1, 8.0, cc, ALU.mult, ALU.add), sV)
        E.addc("vector", lambda v: v.tensor_scalar_add(small[:, 0:32], small[:, 0:32], -8.0), sV)
        E.addc("vector", lambda v: v.tensor_scalar_max(small[:, 32:64], small[:, 0:32], 0.0), sV)
        E.addc("vector", lambda v: v.tensor_copy(idxgI[:], small[:, 32:64]), sV)
        E.addc("vector", lambda v: v.tensor_single_scalar(small[:, 32:64], small[:, 0:32], 0.0, ALU.is_lt), sV)
        E.addc("vector", lambda v: v.scalar_tensor_tensor(small[:, 0:32], small[:, 32:64],
                                                          float(TOK + 8), small[:, 0:32], ALU.mult, ALU.add), sV)
        E.addc("vector", lambda v: v.tensor_copy(idxsI[:], small[:, 0:32]), sV)
        E.addc("vector", lambda v: v.tensor_copy(gatec[:], gg), sV)

        if BUILD_PHASES < 2:
            trunc_finalize()
            return nc
        esA.close()
        wqb = [SB("wqb0", [128, NDC * 128], BF16, esB), SB("wqb1", [128, NDC * 128], BF16, esB)]
        qout = SB("qout", [128, TOK], BF16, esB)
        vtr = [SB("vtr0", [128, 128], BF16, esB), SB("vtr1", [128, 128], BF16, esB)]
        NOB = NDC + 2

        def wq_src(o):
            if o < NDC:
                return dr["wq"].ap().rearrange("(dc p) (ob oj) -> ob p dc oj", p=128, oj=128)[o]
            return dr["wkv"].ap().rearrange("(dc p) (ob oj) -> ob p dc oj", p=128, oj=128)[o - NDC]

        qv_in = dr["q_in"].ap().rearrange("(j r t) -> j r t", j=NCORES, r=256)
        kv_inK = dr["kv_in"].ap().rearrange("(a b c) -> a b c", a=2, b=128)[0]
        kv_inV = dr["kv_in"].ap().rearrange("(a b c) -> a b c", a=2, b=1024)[1]

        E.dma("gpsimd", wqb[0][:].rearrange("p (dc oj) -> p dc oj", dc=NDC), wq_src(0), sW,
              waits=[(sV, sV.v)])
        blk_done_T = []
        epi_done_V = []
        store_O = []
        for o in range(NOB):
            wb = wqb[o % 2]
            if o + 1 < NOB:
                w = [(sT, blk_done_T[o - 1])] if o >= 1 else []
                E.dma("gpsimd", wqb[(o + 1) % 2][:].rearrange("p (dc oj) -> p dc oj", dc=NDC),
                      wq_src(o + 1), sW, waits=w)
            pmain = (pM0, pM1) if (o % 2 == 0) else (pM2, pM3)
            phsl = (pPh if (o % 2 == 0) else pPh2)[:, 0:2]
            for dc in range(NDC):
                wts = []
                if dc == 0:
                    wts.append((sW, sW.v - (16 if o + 1 < NOB else 0)))
                    if o >= 2:
                        wts.append((sV, epi_done_V[o - 2]))
                for h in range(2):
                    E.add("tensor", lambda t, wb=wb, dc=dc, h=h, pmain=pmain: t.matmul(
                        pmain[h][:], wb[:, dc * 128:(dc + 1) * 128],
                        xgT[:, dc * CT + h * 512: dc * CT + (h + 1) * 512],
                        start=(dc == 0), stop=(dc == NDC - 1)),
                        waits=(wts if h == 0 else []))
                E.add("tensor", lambda t, wb=wb, dc=dc, phsl=phsl: t.matmul(
                    phsl, wb[:, dc * 128:(dc + 1) * 128],
                    xgT[:, dc * CT + TOK: dc * CT + TOK + 2],
                    start=(dc == 0), stop=(dc == NDC - 1), skip_group_check=True),
                    inc=(sT if dc == NDC - 1 else None))
            blk_done_T.append(sT.v)
            bias = (vcols[:, VBQ + o: VBQ + o + 1] if o < NDC
                    else vcols[:, VBKV + (o - NDC): VBKV + (o - NDC) + 1])
            wq_epi_w = [(sT, sT.v)]
            if o >= 1:
                wq_epi_w.append((sO, store_O[o - 1]))
            E.addc("vector", lambda v, phsl=phsl: v.tensor_copy(ucol[:, 0:2], phsl),
                   sV, waits=wq_epi_w)
            E.addc("vector", lambda v, bias=bias: v.tensor_tensor(ucol[:, 2:3], ucol[:, 1:2], bias, ALU.add), sV)
            for h in range(2):
                sl = slice(h * 512, (h + 1) * 512)
                E.add("vector", lambda v, sl=sl, h=h, pmain=pmain: v.tensor_tensor(
                    qout[:, sl], pmain[h][:], rep_sl(0, h), ALU.mult))
                E.add("vector", lambda v, sl=sl, h=h: v.scalar_tensor_tensor(
                    qout[:, sl], rep_sl(1, h), ucol[:, 0:1], qout[:, sl], ALU.mult, ALU.add))
                E.add("vector", lambda v, sl=sl: v.tensor_scalar_add(qout[:, sl], qout[:, sl], ucol[:, 2:3]),
                      inc=(sV if h == 1 else None))
            epi_done_V.append(sV.v)
            if o < NDC:
                E.dma("sync", qv_in[o // 2, (o % 2) * 128:(o % 2) * 128 + 128, :], qout[:], sO,
                      waits=[(sV, sV.v)])
                store_O.append(sO.v)
            elif o == NDC:
                E.dma("sync", kv_inK, qout[:], sO, waits=[(sV, sV.v)])
                store_O.append(sO.v)
            else:
                tb0, vb0, ob0 = sT.v, sV.v, sO.v
                for c in range(NCH):
                    vb = vtr[c % 2]
                    w = [(sV, sV.v)] if c == 0 else []
                    pb = pXb(c)[:, 0:128]
                    if c >= 2:
                        w.append((sV, vb0 + c - 1))
                    E.add("tensor", lambda t, c=c, pb=pb: t.transpose(
                        pb, qout[:, c * 128:(c + 1) * 128], identb[:]), waits=w, inc=sT)
                    w2_ = [(sT, sT.v)]
                    if c >= 2:
                        w2_.append((sO, ob0 + 16 * (c - 1)))
                    E.add("vector", lambda v, vb=vb, pb=pb: v.tensor_copy(vb[:], pb),
                          waits=w2_, inc=sV)
                    E.dma("sync", kv_inV[c * 128:(c + 1) * 128, :], vb[:], sO, waits=[(sV, sV.v)])
                store_O.append(sO.v)

        E.add("gpsimd", lambda g: g.collective_compute(
            "AllGather", ALU.bypass, replica_groups=rg,
            ins=[dr["kv_in"].ap().opt()], outs=[dr["kv_out"].ap().opt()]),
            waits=[(sO, sO.v)], inc=sCC)

        def a2a_q(g):
            with allow_ici_all_to_all():
                return g.collective_compute("AllToAll", ALU.bypass, replica_groups=rg,
                                            ins=[dr["q_in"].ap().opt()], outs=[dr["q_out"].ap().opt()])
        E.add("gpsimd", a2a_q, inc=sCC)
        cc_after_kvq = sCC.v

        if BUILD_PHASES < 3:
            trunc_finalize()
            return nc
        esB.close()
        esC = ExitStack()
        open_stacks.append(esC)
        gx = [SB("gx0", [128, D], BF16, esC), SB("gx1", [128, D], BF16, esC)]
        xeT = SB("xeT", [128, NDC * 512], BF16, esC)
        w13 = [SB("w13a", [128, NDC * 512], BF16, esC), SB("w13b", [128, NDC * 512], BF16, esC)]
        w2b = [SB("w2ba", [128, 4 * D], BF16, esC), SB("w2bb", [128, 4 * D], BF16, esC)]
        y1t = SB("y1t", [128, 4 * 512], BF16, esC)
        y3t = SB("y3t", [128, 4 * 512], BF16, esC)
        wout = [SB("wout0", [128, D], F32, esC), SB("wout1", [128, D], F32, esC)]
        xnr = dr["xn_rows"].ap()

        gath_T = sT.v
        E.dma("sync", w13[0][:].rearrange("p (dc f) -> p dc f", dc=NDC),
              dr["w1s"].ap()[0].rearrange("(dc p) f -> p dc f", p=128), sW,
              waits=[(sW, precast_W[0]), (sV, sV.v), (sO, sO.v), (sT, sT.v)])
        E.dma("sync", w13[1][:].rearrange("p (dc f) -> p dc f", dc=NDC),
              dr["w3s"].ap()[0].rearrange("(dc p) f -> p dc f", p=128), sW)
        E.dma("sync", w2b[0][:].rearrange("p (fc d) -> p fc d", fc=4),
              dr["w2s"].ap()[0].rearrange("(fc p) d -> p fc d", p=128), sW)
        w13_ld = [sW.v - 32, sW.v]
        w2_ld = [sW.v, 0]
        w13_mm_T = [0, 0]
        w2_mm_T = [0, 0]
        wout_free = [0, 0]
        for e_ in range(NE):
            cols = [2 * e_, 2 * e_ + 1, 2 * (NE + e_), 2 * (NE + e_) + 1]
            for ci, k in enumerate(cols):
                gbuf = gx[ci % 2]
                w = ([(sX, sX.v), (sV, sV.v), (sO, sO.v), (sW, precast_W[-1])]
                     if (e_ == 0 and ci == 0) else [])
                w.append((sW, sW.v))
                if e_ > 0 or ci >= 2:
                    w.append((sT, gath_T))
                E.add("gpsimd", lambda g, gbuf=gbuf, k=k: g.indirect_dma_start(
                    gbuf[:], None, xnr, IndirectOffsetOnAxis(idxgI[:, k:k + 1], 0)),
                    waits=w, inc=sG, incn=16)
                tb0, vb0 = sT.v, sV.v
                for dc in range(NDC):
                    pb = pXb(dc)[:, 0:128]
                    w = [(sG, sG.v)] if dc == 0 else []
                    if vb0 + dc - 1 > 0:
                        w.append((sV, vb0 + dc - 1))
                    E.add("tensor", lambda t, gbuf=gbuf, dc=dc, pb=pb: t.transpose(
                        pb, gbuf[:, dc * 128:(dc + 1) * 128], identb[:]), waits=w, inc=sT)
                    E.add("vector", lambda v, dc=dc, ci=ci, pb=pb: v.tensor_copy(
                        xeT[:, dc * 512 + ci * 128: dc * 512 + (ci + 1) * 128], pb),
                        waits=[(sT, tb0 + dc + 1)], inc=sV)
                gath_T = sT.v
            for wi, ydst in [(0, y1t), (1, y3t)]:
                for ffo in range(4):
                    wts = []
                    if ffo == 0:
                        wts = [(sW, w13_ld[wi]), (sC, sC.v)] + ([(sV, sV.v)] if wi == 0 else [])
                    if ffo >= 2:
                        wts.append((sC, sC.v - 1))
                    pb = pM0 if (ffo % 2 == 0) else pM1
                    for dc in range(NDC):
                        E.add("tensor", lambda t, dc=dc, ffo=ffo, pb=pb, wbuf=w13[wi]: t.matmul(
                            pb[:], wbuf[:, dc * 512 + ffo * 128: dc * 512 + (ffo + 1) * 128],
                            xeT[:, dc * 512:(dc + 1) * 512],
                            start=(dc == 0), stop=(dc == NDC - 1)),
                            waits=(wts if dc == 0 else []),
                            inc=(sT if dc == NDC - 1 else None))
                    fn = AF.Silu if wi == 0 else AF.Copy
                    E.add("scalar", lambda s, ydst=ydst, ffo=ffo, pb=pb, fn=fn: s.activation(
                        ydst[:, ffo * 512:(ffo + 1) * 512], pb[:], fn),
                        waits=[(sT, sT.v)], inc=sC)
                w13_mm_T[wi] = sT.v
            E.add("vector", lambda v: v.tensor_tensor(y1t[:], y1t[:], y3t[:], ALU.mult),
                  waits=[(sC, sC.v)], inc=sV)
            if e_ + 1 < NE:
                E.dma("sync", w13[0][:].rearrange("p (dc f) -> p dc f", dc=NDC),
                      dr["w1s"].ap()[e_ + 1].rearrange("(dc p) f -> p dc f", p=128), sW,
                      waits=[(sW, precast_W[e_ + 1]), (sT, w13_mm_T[0])])
                E.dma("sync", w13[1][:].rearrange("p (dc f) -> p dc f", dc=NDC),
                      dr["w3s"].ap()[e_ + 1].rearrange("(dc p) f -> p dc f", p=128), sW,
                      waits=[(sT, w13_mm_T[1])])
                w13_ld = [sW.v - 16, sW.v]
                E.dma("sync", w2b[(e_ + 1) % 2][:].rearrange("p (fc d) -> p fc d", fc=4),
                      dr["w2s"].ap()[e_ + 1].rearrange("(fc p) d -> p fc d", p=128), sW,
                      waits=[(sT, w2_mm_T[(e_ + 1) % 2])])
                w2_ld[(e_ + 1) % 2] = sW.v
            scat_G = []
            w2buf = w2b[e_ % 2]
            for tc in range(4):
                wo_ = wout[tc % 2]
                k = cols[tc]
                tb0, vb0 = sT.v, sV.v
                for do in range(4):
                    pb = pM2 if (do % 2 == 0) else pM3
                    wts = []
                    if do == 0 and tc == 0:
                        wts = [(sW, w2_ld[e_ % 2]), (sV, sV.v)]
                    if vb0 + do - 1 > 0:
                        wts.append((sV, vb0 + do - 1))
                    for fc in range(4):
                        E.add("tensor", lambda t, fc=fc, do=do, tc=tc, pb=pb, w2buf=w2buf: t.matmul(
                            pb[:], y1t[:, fc * 512 + tc * 128: fc * 512 + (tc + 1) * 128],
                            w2buf[:, fc * D + do * 512: fc * D + (do + 1) * 512],
                            start=(fc == 0), stop=(fc == 3)),
                            waits=(wts if fc == 0 else []),
                            inc=(sT if fc == 3 else None))
                for do in range(4):
                    pb = pM2 if (do % 2 == 0) else pM3
                    w = [(sT, tb0 + do + 1)]
                    if do == 0 and wout_free[tc % 2] > 0:
                        w.append((sG, wout_free[tc % 2]))
                    E.add("vector", lambda v, do=do, pb=pb, wo_=wo_, k=k: v.tensor_scalar_mul(
                        wo_[:, do * 512:(do + 1) * 512], pb[:], gatec[:, k:k + 1]),
                        waits=w, inc=sV)
                dst = dr["moeA"] if tc < 2 else dr["moeB"]
                E.add("gpsimd", lambda g, dst=dst, k=k, wo_=wo_: g.indirect_dma_start(
                    dst.ap(), IndirectOffsetOnAxis(idxsI[:, k:k + 1], 0), wo_[:], None),
                    waits=[(sV, sV.v), (sW, sW.v), (sG, sG.v)], inc=sG, incn=16)
                scat_G.append(sG.v)
                wout_free[tc % 2] = sG.v
            w2_mm_T[e_ % 2] = sT.v
        moe_G = sG.v

        if BUILD_PHASES < 4:
            trunc_finalize()
            return nc
        esC.close()
        es4 = ExitStack()
        open_stacks.append(es4)
        kT = SB("kT", [128, S], BF16, es4)
        vT = SB("vT", [128, 64 * 128], BF16, es4)
        qtl = [SB("qt0", [128, 512], BF16, es4), SB("qt1", [128, 512], BF16, es4)]
        prb = [SB("prb0", [128, 512], BF16, es4), SB("prb1", [128, 512], BF16, es4)]
        atile = SB("atile", [128, 512], BF16, es4)

        E.dma("sync", kT[:].rearrange("p (r t) -> p r t", r=NCORES),
              dr["kv_out"].ap().rearrange("(r a p t) -> p r a t", r=NCORES, a=2, p=128)[:, :, 0, :],
              sX, waits=[(sCC, cc_after_kvq), (sT, sT.v), (sV, sV.v), (sC, sC.v),
                         (sG, sG.v), (sO, sO.v)])
        kvoV = dr["kv_out"].ap().rearrange("(r a c p j) -> r a c p j", r=NCORES, a=2,
                                           c=NCH, p=128)[:, 1].rearrange("r c p j -> p r c j")
        for r_ in range(NCORES):
            E.dma("sync", vT[:].rearrange("p (r c j) -> p r c j", r=NCORES, c=NCH)[:, r_],
                  kvoV[:, r_], sX)
        qov = dr["q_out"].ap().rearrange("(j r t) -> j r t", j=NCORES, r=256)
        at_inv = dr["at_in"].ap().rearrange("(j r t) -> j r t", j=NCORES, r=256)

        kv_ready = (sX, sX.v)
        prev_qload = [0, 0]
        prev_store_O = None
        qtl2 = [qtl[0], qtl[1], SB("qt2", [128, 512], BF16, es4), SB("qt3", [128, 512], BF16, es4)]
        prb2 = [prb[0], prb[1], SB("prb2", [128, 512], BF16, es4), SB("prb3", [128, 512], BF16, es4)]
        atile2 = [atile, SB("atile1", [128, 512], BF16, es4)]
        for qt in range(NQT):
            NK = 4 * qt + 4
            qb = [qtl2[2 * (qt % 2)], qtl2[2 * (qt % 2) + 1]]
            w = [kv_ready] if qt == 0 else []
            if prev_qload[qt % 2] > 0:
                w.append((sT, prev_qload[qt % 2]))
            E.dma("sync", qb[0][:], qov[qt // 2, 0:128, (qt % 2) * 512:((qt % 2) + 1) * 512],
                  sX, waits=w)
            E.dma("sync", qb[1][:], qov[qt // 2, 128:256, (qt % 2) * 512:((qt % 2) + 1) * 512],
                  sX)
            qload = (sX, sX.v)
            tb, cb, vb = sT.v, sC.v, sV.v
            sval = {0: tb + 1}
            for k in range(1, NK):
                sval[k] = tb + 2 * k
            oval = {j: tb + 2 * j + 3 for j in range(NK - 1)}
            oval[NK - 1] = tb + 2 * NK
            ordr = []
            for k in range(NK + 1):
                if k < NK:
                    ordr.append(("S", k))
                if k >= 1:
                    ordr.append(("O", k - 1))
            for tag, k in ordr:
                pS = [(pM0 if (k % 2 == 0) else pM1), (pM2 if (k % 2 == 0) else pM3)]
                pr = [prb2[2 * (k % 2)], prb2[2 * (k % 2) + 1]]
                if tag == "S":
                    w = []
                    if k == 0:
                        w.append(qload)
                        w.append((sT, tb))
                        if prev_store_O is not None:
                            w.append((sO, prev_store_O))
                    if k == 1:
                        w.append((sT, tb))
                    if k >= 2:
                        w.append((sC, cb + 2 * (k - 2) + 2))
                    E.add("tensor", lambda t, k=k, pS=pS, qb=qb: t.matmul(
                        pS[0][:], kT[:, k * 128:(k + 1) * 128], qb[0][:], start=True, stop=True),
                        waits=w)
                    E.add("tensor", lambda t, k=k, pS=pS, qb=qb: t.matmul(
                        pS[1][:], kT[:, k * 128:(k + 1) * 128], qb[1][:], start=True, stop=True),
                        inc=sT)
                else:
                    w = []
                    if k >= 4 * qt:
                        w.append((sV, vb + 2 * (k - 4 * qt) + 2))
                    else:
                        w.append((sC, cb + 2 * k + 2))
                    E.add("tensor", lambda t, k=k, pr=pr, NK=NK: t.matmul(
                        pPh[:], onesb[:], pr[0][:], start=(k == 0), stop=(k == NK - 1)),
                        waits=w)
                    E.add("tensor", lambda t, k=k, pr=pr, NK=NK: t.matmul(
                        pX0[:], vT[:, k * 128:(k + 1) * 128], pr[0][:],
                        start=(k == 0), stop=(k == NK - 1)))
                    E.add("tensor", lambda t, k=k, pr=pr, NK=NK: t.matmul(
                        pPh2[:], onesb[:], pr[1][:], start=(k == 0), stop=(k == NK - 1)))
                    E.add("tensor", lambda t, k=k, pr=pr, NK=NK: t.matmul(
                        pX1[:], vT[:, k * 128:(k + 1) * 128], pr[1][:],
                        start=(k == 0), stop=(k == NK - 1)), inc=sT)
            for k in range(NK):
                pS = [(pM0 if (k % 2 == 0) else pM1), (pM2 if (k % 2 == 0) else pM3)]
                pr = [prb2[2 * (k % 2)], prb2[2 * (k % 2) + 1]]
                w = [(sT, sval[k])]
                if k >= 2:
                    w.append((sT, oval[k - 2]))
                E.add("scalar", lambda s, pS=pS, pr=pr: s.activation(
                    pr[0][:], pS[0][:], AF.Exp, bias=vcols[:, 114:115], scale=SCALE),
                    waits=w, inc=sC)
                E.add("scalar", lambda s, pS=pS, pr=pr: s.activation(
                    pr[1][:], pS[1][:], AF.Exp, bias=vcols[:, 114:115], scale=SCALE),
                    inc=sC)
            for r_ in range(4):
                k = 4 * qt + r_
                pr = [prb2[2 * (k % 2)], prb2[2 * (k % 2) + 1]]
                w0 = [(sC, cb + 2 * k + 2)]
                if k >= 2:
                    w0.append((sT, oval[k - 2]))
                E.add("vector", lambda v, pr=pr, r_=r_: v.tensor_tensor(
                    pr[0][:], pr[0][:], masks[:, r_ * 512:(r_ + 1) * 512], ALU.mult),
                    waits=w0, inc=sV)
                E.add("vector", lambda v, pr=pr, r_=r_: v.tensor_tensor(
                    pr[1][:], pr[1][:], masks[:, r_ * 512:(r_ + 1) * 512], ALU.mult),
                    inc=sV)
            mm_all = (sT, sT.v)
            for hh in range(HPC):
                psum_s = pPh if hh == 0 else pPh2
                psum_a = pX0 if hh == 0 else pX1
                at = atile2[hh]
                E.add("vector", lambda v, psum_s=psum_s: v.reciprocal(rec[:], psum_s[:]),
                      waits=([mm_all] if hh == 0 else []))
                E.add("vector", lambda v, psum_a=psum_a, at=at: v.tensor_tensor(
                    at[:], psum_a[:], rec[:], ALU.mult), inc=sV)
                E.dma("sync", at_inv[qt // 2, hh * 128:(hh + 1) * 128,
                                     (qt % 2) * 512:((qt % 2) + 1) * 512], at[:], sO,
                      waits=[(sV, sV.v)])
            prev_qload[qt % 2] = sT.v
            prev_store_O = sO.v

        def a2a_at(g):
            with allow_ici_all_to_all():
                return g.collective_compute("AllToAll", ALU.bypass, replica_groups=rg,
                                            ins=[dr["at_in"].ap().opt()], outs=[dr["at_out"].ap().opt()])
        E.add("gpsimd", a2a_at, waits=[(sO, sO.v)], inc=sCC)
        cc_after_at = sCC.v

        if BUILD_PHASES < 5:
            trunc_finalize()
            return nc
        es4.close()
        es2 = ExitStack()
        open_stacks.append(es2)
        A = SB("A", [128, NCH * D], F32)
        CT2 = TOK + 2
        x2b = SB("x2b", [128, NDC * CT2], BF16)
        xn_o = [SB("xn_o0", [128, NCH * 128], BF16, es2), SB("xn_o1", [128, NCH * 128], BF16, es2)]
        atn = SB("atn", [128, NH * TOK], BF16, es2)
        xnTt = SB("xnTt", [128, TOK], BF16, es2)
        x2f = SB("x2f", [128, TOK], F32, es2)
        wob = [SB("wob0", [128, NDC * 128], BF16, es2), SB("wob1", [128, NDC * 128], BF16, es2)]

        E.dma("sync", atn[:].rearrange("p (h t) -> p h t", h=NH),
              dr["at_out"].ap().rearrange("(j k p t) -> p (j k) t", j=NCORES, k=HPC, p=128),
              sX, waits=[(sCC, cc_after_at), (sT, sT.v), (sV, sV.v), (sO, sO.v)])
        E.dma("sync", A[:].rearrange("p (c d) -> p c d", c=NCH),
              dr["x"].ap().rearrange("(c p) d -> p c d", p=128), sX)
        atn_ready = (sX, sX.v)
        xnp_T = []

        wo_src = dr["wo"].ap().rearrange("(dc p) (ob oj) -> ob p dc oj", p=128, oj=128)
        E.dma("gpsimd", wob[0][:].rearrange("p (dc oj) -> p dc oj", dc=NDC), wo_src[0], sW)
        blk_T = []
        epi_V = []
        for o in range(NDC):
            wb = wob[o % 2]
            if o + 1 < NDC:
                w = [(sT, blk_T[o - 1])] if o >= 1 else []
                E.dma("gpsimd", wob[(o + 1) % 2][:].rearrange("p (dc oj) -> p dc oj", dc=NDC),
                      wo_src[o + 1], sW, waits=w)
            pmo = (pM0, pM1) if (o % 2 == 0) else (pM2, pM3)
            for hc in range(NH):
                wts = []
                if hc == 0:
                    wts = [(sW, sW.v - (16 if o + 1 < NDC else 0)), atn_ready]
                    if o >= 1:
                        wts.append((sV, epi_V[o - 1]))
                for h in range(2):
                    E.add("tensor", lambda t, wb=wb, hc=hc, h=h, pmo=pmo: t.matmul(
                        pmo[h][:], wb[:, hc * 128:(hc + 1) * 128],
                        atn[:, hc * TOK + h * 512: hc * TOK + (h + 1) * 512],
                        start=(hc == 0), stop=(hc == NH - 1)),
                        waits=(wts if h == 0 else []),
                        inc=(sT if (hc == NH - 1 and h == 1) else None))
            blk_T.append(sT.v)
            xno = xn_o[o % 2]
            wxn = [(sT, xnp_T[o - 2])] if o >= 2 else []
            E.dma("sync", xno[:].rearrange("p (c d) -> p c d", c=NCH),
                  dr["xn_rows"].ap().rearrange("(p c) d -> p c d", c=NCH)[:, :, o * 128:(o + 1) * 128],
                  sX, waits=wxn)
            items = [(xno[:, c * 128:(c + 1) * 128],
                      lambda v, pb, c=c: v.tensor_copy(xnTt[:, c * 128:(c + 1) * 128], pb))
                     for c in range(NCH)]
            tr_evac_pipe(items, "bf", first_waits=[(sV, sV.v), (sX, sX.v)])
            xnp_T.append(sT.v)
            mm_done = (sT, blk_T[-1])
            for h in range(2):
                sl = slice(h * 512, (h + 1) * 512)
                pm = pmo[h]
                E.add("vector", lambda v, sl=sl, pm=pm: v.tensor_tensor(
                    x2f[:, sl], pm[:], xnTt[:, sl], ALU.add),
                    waits=([mm_done, (sV, sV.v)] if h == 0 else []))
                E.add("vector", lambda v, sl=sl, o=o: v.tensor_scalar_add(
                    x2f[:, sl], x2f[:, sl], vcols[:, VBO + o:VBO + o + 1]))
            E.add("vector", lambda v, o=o: v.tensor_copy(x2b[:, o * CT2: o * CT2 + TOK], x2f[:]))
            E.add("vector", lambda v, o=o: v.tensor_copy(
                x2b[:, o * CT2 + TOK: o * CT2 + TOK + 1], vcols[:, VG2 + o:VG2 + o + 1]))
            E.add("vector", lambda v, o=o: v.tensor_copy(
                x2b[:, o * CT2 + TOK + 1: o * CT2 + TOK + 2], vcols[:, VB2 + o:VB2 + o + 1]),
                inc=sV)
            tbA, vbA = sT.v, sV.v
            for c in range(NCH):
                pb = (pX0 if (c % 2 == 0) else pX1)[:, 0:128]
                w = [(sV, sV.v)] if c == 0 else []
                if c >= 2:
                    w.append((sV, vbA + c - 1))
                E.add("tensor", lambda t, c=c, o=o, pb=pb: t.transpose(
                    pb, x2f[:, c * 128:(c + 1) * 128], identf[:]), waits=w, inc=sT)
                asl = A[:, c * D + o * 128: c * D + (o + 1) * 128]
                E.add("vector", lambda v, asl=asl, pb=pb: v.tensor_tensor(asl, asl, pb, ALU.add),
                      waits=[(sT, tbA + c + 1)], inc=sV)
            epi_V.append(sV.v)

        sq2 = SB("sq2", [128, TOK], BF16, es2)
        for o in range(NDC):
            for h in range(2):
                E.add("tensor", lambda t, o=o, h=h: t.matmul(
                    (pM2 if h == 0 else pM3)[:], onesb[:],
                    x2b[:, o * CT2 + h * 512: o * CT2 + (h + 1) * 512],
                    start=(o == 0), stop=(o == NDC - 1)),
                    waits=([(sV, sV.v)] if (o == 0 and h == 0) else []),
                    inc=(sT if (o == NDC - 1 and h == 1) else None))
        for o in range(NDC):
            E.add("vector", lambda v, o=o: v.tensor_tensor(
                sq2[:], x2b[:, o * CT2: o * CT2 + TOK], x2b[:, o * CT2: o * CT2 + TOK], ALU.mult),
                waits=([(sT, sT.v)] if o > 0 else []), inc=sV)
            for h in range(2):
                E.add("tensor", lambda t, o=o, h=h: t.matmul(
                    (pX0 if h == 0 else pX1)[:], onesb[:], sq2[:, h * 512:(h + 1) * 512],
                    start=(o == 0), stop=(o == NDC - 1)),
                    waits=([(sV, sV.v)] if h == 0 else []),
                    inc=(sT if h == 1 else None))
        E.add("vector", lambda v: v.tensor_scalar_mul(rep_sl(2, 0), pM2[:], 1.0 / D),
              waits=[(sT, sT.v)])
        E.add("vector", lambda v: v.tensor_scalar_mul(rep_sl(2, 1), pM3[:], 1.0 / D))
        E.add("vector", lambda v: v.tensor_scalar_mul(rep_sl(3, 0), pX0[:], 1.0 / D))
        E.add("vector", lambda v: v.tensor_scalar_mul(rep_sl(3, 1), pX1[:], 1.0 / D))
        for h in range(2):
            E.add("vector", lambda v, h=h: v.tensor_tensor(rec[:], rep_sl(2, h), rep_sl(2, h), ALU.mult))
            E.add("vector", lambda v, h=h: v.tensor_tensor(rep_sl(3, h), rep_sl(3, h), rec[:], ALU.subtract))
            E.add("vector", lambda v, h=h: v.tensor_scalar_add(rep_sl(3, h), rep_sl(3, h), EPS),
                  inc=(sV if h == 1 else None))
        E.add("scalar", lambda s: s.activation(reps[:, 3 * TOK:4 * TOK], reps[:, 3 * TOK:4 * TOK], AF.Sqrt),
              waits=[(sV, sV.v)], inc=sC)
        E.add("vector", lambda v: v.reciprocal(reps[:, 0:TOK], reps[:, 3 * TOK:4 * TOK]),
              waits=[(sC, sC.v)])
        E.add("vector", lambda v: v.tensor_tensor(reps[:, TOK:2 * TOK], reps[:, 0:TOK],
                                                  reps[:, 2 * TOK:3 * TOK], ALU.mult))
        E.add("vector", lambda v: v.tensor_scalar_mul(reps[:, TOK:2 * TOK], reps[:, TOK:2 * TOK], -1.0),
              inc=sV)

        if BUILD_PHASES < 6:
            trunc_finalize()
            return nc
        es2.close()
        es6 = ExitStack()
        open_stacks.append(es6)
        NSW = 4
        NOF = (FF // NSW) // 128
        gbuf = SB("gbuf", [128, NOF * TOK], BF16, es6)
        wfb = [SB("wfb0", [128, NDC * 128], BF16, es6), SB("wfb1", [128, NDC * 128], BF16, es6)]
        wpb = [SB("wpb0", [128, NOF * 128], BF16, es6), SB("wpb1", [128, NOF * 128], BF16, es6)]
        fcf = SB("fcf", [128, TOK], F32, es6)
        spf = [SB("spf0", [128, 512], F32, es6), SB("spf1", [128, 512], F32, es6)]

        wfc_src = dr["wfc"].ap().rearrange("(dc p) (ob oj) -> ob p dc oj", p=128, oj=128)
        wpj_src = dr["wproj"].ap().rearrange("(sw fc p) (ob oj) -> sw ob p fc oj",
                                             p=128, oj=128, sw=NSW)
        for sw in range(NSW):
            E.dma("gpsimd", wfb[0][:].rearrange("p (dc oj) -> p dc oj", dc=NDC),
                  wfc_src[sw * NOF], sW, waits=[(sT, sT.v)])
            blkf_T = []
            gel_C = []
            for of in range(NOF):
                og = sw * NOF + of
                wb = wfb[of % 2]
                if of + 1 < NOF:
                    w = [(sT, blkf_T[of - 1])] if of >= 1 else []
                    E.dma("gpsimd", wfb[(of + 1) % 2][:].rearrange("p (dc oj) -> p dc oj", dc=NDC),
                          wfc_src[og + 1], sW, waits=w)
                phsl = (pPh if (of % 2 == 0) else pPh2)[:, 0:2]
                for dc in range(NDC):
                    wts = []
                    if dc == 0:
                        wts = [(sW, sW.v - (16 if of + 1 < NOF else 0))]
                        if of <= 1:
                            wts.append((sV, sV.v))
                            wts.append((sC, sC.v))
                        if of >= 2:
                            wts.append((sC, gel_C[of - 2]))
                    for h in range(2):
                        E.add("tensor", lambda t, wb=wb, dc=dc, h=h, of=of: t.matmul(
                            (pM0 if of % 2 == 0 else pM2, pM1 if of % 2 == 0 else pM3)[h][:],
                            wb[:, dc * 128:(dc + 1) * 128],
                            x2b[:, dc * CT2 + h * 512: dc * CT2 + (h + 1) * 512],
                            start=(dc == 0), stop=(dc == NDC - 1)),
                            waits=(wts if h == 0 else []))
                    E.add("tensor", lambda t, wb=wb, dc=dc, phsl=phsl: t.matmul(
                        phsl, wb[:, dc * 128:(dc + 1) * 128],
                        x2b[:, dc * CT2 + TOK: dc * CT2 + TOK + 2],
                        start=(dc == 0), stop=(dc == NDC - 1), skip_group_check=True),
                        inc=(sT if dc == NDC - 1 else None))
                blkf_T.append(sT.v)
                E.addc("vector", lambda v, phsl=phsl: v.tensor_copy(ucol[:, 0:2], phsl),
                       sV, waits=[(sT, sT.v), (sC, sC.v)])
                E.addc("vector", lambda v, og=og: v.tensor_tensor(
                    ucol[:, 2:3], ucol[:, 1:2], vcols[:, VBFC + og:VBFC + og + 1], ALU.add), sV)
                for h in range(2):
                    sl = slice(h * 512, (h + 1) * 512)
                    pm = (pM0 if of % 2 == 0 else pM2, pM1 if of % 2 == 0 else pM3)[h]
                    E.add("vector", lambda v, sl=sl, h=h, pm=pm: v.tensor_tensor(
                        fcf[:, sl], pm[:], rep_sl(0, h), ALU.mult),
                        waits=([(sC, gel_C[of - 1])] if (h == 0 and of >= 1) else []))
                    E.add("vector", lambda v, sl=sl, h=h: v.scalar_tensor_tensor(
                        fcf[:, sl], rep_sl(1, h), ucol[:, 0:1], fcf[:, sl], ALU.mult, ALU.add),
                        inc=(sV if h == 1 else None))
                E.add("scalar", lambda s, of=of: s.activation(
                    gbuf[:, of * TOK:(of + 1) * TOK], fcf[:], AF.Gelu_apprx_tanh, bias=ucol[:, 2:3]),
                    waits=[(sV, sV.v)], inc=sC)
                gel_C.append(sC.v)
            E.dma("gpsimd", wpb[0][:].rearrange("p (fc oj) -> p fc oj", fc=NOF),
                  wpj_src[sw, 0], sW, waits=[(sT, sT.v), (sC, sC.v)])
            blkp_o = []
            pbank_evac = {0: 0, 1: 0, 2: 0, 3: 0}
            for o in range(NDC):
                wb = wpb[o % 2]
                if o + 1 < NDC:
                    w = [(sT, blkp_o[o - 1])] if o >= 1 else []
                    E.dma("gpsimd", wpb[(o + 1) % 2][:].rearrange("p (fc oj) -> p fc oj", fc=NOF),
                          wpj_src[sw, o + 1], sW, waits=w)
                for h in range(2):
                    bi = 2 * (o % 2) + h
                    pm = (pM0, pM1, pM2, pM3)[bi]
                    sp = spf[(2 * o + h) % 2]
                    for fc in range(NOF):
                        wts = []
                        if fc == 0:
                            wts = [(sW, sW.v - (16 if o + 1 < NDC else 0))]
                            if pbank_evac[bi] > 0:
                                wts.append((sV, pbank_evac[bi]))
                            if o <= 1:
                                wts.append((sV, sV.v))
                                wts.append((sC, sC.v))
                        E.add("tensor", lambda t, wb=wb, fc=fc, pm=pm, h=h: t.matmul(
                            pm[:], wb[:, fc * 128:(fc + 1) * 128],
                            gbuf[:, fc * TOK + h * 512: fc * TOK + (h + 1) * 512],
                            start=(fc == 0), stop=(fc == NOF - 1)),
                            waits=(wts if fc == 0 else []),
                            inc=(sT if fc == NOF - 1 else None))
                    if h == 1:
                        blkp_o.append(sT.v)
                    if sw == 0:
                        E.add("vector", lambda v, pm=pm, o=o, sp=sp: v.tensor_scalar_add(
                            sp[:], pm[:], vcols[:, VBPJ + o:VBPJ + o + 1]),
                            waits=[(sT, sT.v)], inc=sV)
                    else:
                        E.add("vector", lambda v, pm=pm, sp=sp: v.tensor_copy(sp[:], pm[:]),
                              waits=[(sT, sT.v)], inc=sV)
                    pbank_evac[bi] = sV.v
                    tbB, vbB = sT.v, sV.v
                    for cc_ in range(4):
                        c = h * 4 + cc_
                        pb = (pX0 if (cc_ % 2 == 0) else pX1)[:, 0:128]
                        w = [(sV, vbB)] if cc_ == 0 else []
                        if cc_ >= 2:
                            w.append((sV, vbB + cc_ - 1))
                        E.add("tensor", lambda t, cc_=cc_, pb=pb, sp=sp: t.transpose(
                            pb, sp[:, cc_ * 128:(cc_ + 1) * 128], identf[:]), waits=w, inc=sT)
                        asl = A[:, c * D + o * 128: c * D + (o + 1) * 128]
                        E.add("vector", lambda v, asl=asl, pb=pb: v.tensor_tensor(
                            asl, asl, pb, ALU.add), waits=[(sT, tbB + cc_ + 1)], inc=sV)

        if BUILD_PHASES < 7:
            trunc_finalize()
            return nc
        es6.close()
        fm = [SB("fm0", [128, D], F32), SB("fm1", [128, D], F32)]
        for c in range(NCH):
            buf = fm[c % 2]
            w = [(sG, moe_G), (sV, sV.v)] if c == 0 else [(sV, sV.v)]
            E.dma("sync", buf[:], dr["moeA"].ap()[c:TOK:NCH, :], sX, waits=w)
            E.add("vector", lambda v, c=c, buf=buf: v.tensor_tensor(
                A[:, c * D:(c + 1) * D], A[:, c * D:(c + 1) * D], buf[:], ALU.add),
                waits=[(sX, sX.v)], inc=sV)
            E.dma("sync", buf[:], dr["moeB"].ap()[c:TOK:NCH, :], sX, waits=[(sV, sV.v)])
            E.add("vector", lambda v, c=c, buf=buf: v.tensor_tensor(
                A[:, c * D:(c + 1) * D], A[:, c * D:(c + 1) * D], buf[:], ALU.add),
                waits=[(sX, sX.v)], inc=sV)
            E.dma("sync", dr["out"].ap()[c * 128:(c + 1) * 128, :], A[:, c * D:(c + 1) * D],
                  sO, waits=[(sV, sV.v)])
        E.add("gpsimd", lambda g: g.wait_ge(sO.h, sO.v))

        E.flush(block)
    return nc


def _host_inputs(inputs):
    x = np.asarray(inputs["L0_attn_hidden_states"])[0]
    vecs = np.zeros((192, 128), np.float32)

    def putv(row0, vec):
        v = np.asarray(vec, np.float32).reshape(-1, 128)
        vecs[row0:row0 + v.shape[0]] = v

    putv(0, inputs["L0_attn_ln1_g"]); putv(16, inputs["L0_attn_ln1_b"])
    putv(32, inputs["L0_attn_ln2_g"]); putv(48, inputs["L0_attn_ln2_b"])
    putv(64, inputs["L0_attn_bq"]); putv(80, inputs["L0_attn_bo"])
    putv(96, inputs["L0_attn_bproj"]); putv(112, inputs["L0_attn_bkv"])
    putv(128, inputs["L0_attn_bfc"])
    vecs[114] = -SHIFT
    masks = np.zeros((4, 128, 512), np.float32)
    for r in range(4):
        masks[r] = (np.arange(512)[None, :] >= (np.arange(128)[:, None] + 128 * r))
    pcvals = np.zeros((128, 2 * NCH), np.float32)
    for c in range(NCH):
        pcvals[:, 2 * c] = np.arange(128) + 1.0
        pcvals[:, 2 * c + 1] = c
    common = {
        "wq": np.asarray(inputs["L0_attn_wq"], np.float32),
        "wkv": np.asarray(inputs["L0_attn_wkv"], np.float32),
        "wo": np.asarray(inputs["L0_attn_wo"], np.float32),
        "wfc": np.asarray(inputs["L0_attn_wfc"], np.float32),
        "wproj": np.asarray(inputs["L0_attn_wproj"], np.float32),
        "routerT": np.ascontiguousarray(np.asarray(inputs["L0_ffn_router_w"], np.float32).T),
        "w1": np.asarray(inputs["L0_ffn_w1"], np.float32),
        "w2": np.asarray(inputs["L0_ffn_w2"], np.float32),
        "w3": np.asarray(inputs["L0_ffn_w3"], np.float32),
        "vecs": vecs,
        "identf": np.eye(128, dtype=np.float32),
        "identb": np.eye(128, dtype=np.float32).astype(BF),
        "onesb": np.ones((128, 128), np.float32).astype(BF),
        "Ustrict": (np.arange(128)[:, None] < np.arange(128)[None, :]).astype(np.float32),
        "masks": masks.astype(BF),
        "iotaC": np.tile(np.arange(C2, dtype=np.float32), (128, 1)),
        "pcvals": pcvals.astype(BF),
        "onesr": np.ones((1, 128), np.float32),
    }
    in_maps = []
    for c in range(NCORES):
        m = dict(common)
        m["x"] = np.ascontiguousarray(x[c * TOK:(c + 1) * TOK])
        in_maps.append(m)
    return in_maps


def kernel(**inputs):
    in_maps = _host_inputs(inputs)
    nc = build_program()
    res = run_bass_kernel_spmd(nc, in_maps, core_ids=list(range(NCORES)))
    out = np.concatenate([np.asarray(res.results[c]["out"]) for c in range(NCORES)], axis=0)
    return out[None].astype(np.float32)
